# version 10
# baseline (speedup 1.0000x reference)
import numpy as np
import ml_dtypes
import concourse.bass as bass
import concourse.mybir as mybir
from concourse.bass_utils import run_bass_kernel_spmd

F32 = mybir.dt.float32
BF16 = mybir.dt.bfloat16
AF = mybir.ActivationFunctionType
ALU = mybir.AluOpType
AX = mybir.AxisListType.X

NCORES = 8
D = 1024
HID = 2048
NE = 8
NEXP = NE + 1
T = 2048
NF = 513
DC = D // 128
HC = HID // 128
TT = T // 128
FC = 5
FCW = [128, 128, 128, 128, 1]
EPS_FFT = 1e-8
EPS_LN = 1e-5
NEG_BIG = -1e9

XNT_O = 36832
R_O = XNT_O + 32768
G_O = R_O + 81920
IFR_O = G_O + 40960
WB_O = IFR_O + 20480
END_O = WB_O + 16384
assert END_O <= 229344, END_O

OUTP_O = [R_O + i * 4096 for i in range(16)]
XSTG_O = [R_O + 16 * 4096, R_O + 17 * 4096]
XSQ_O = R_O + 18 * 4096
XNF_O = R_O + 19 * 4096
XNT32_O = R_O + 4 * 4096
RB1RAW_O = R_O + 5 * 4096
SB1RAW_O = R_O + 7 * 4096
GATH_O = G_O
RW2HI_O = [R_O + (16 + i) * 4096 for i in range(3)]

A1T_O = [G_O + i * 4096 for i in range(8)]
ZERO_O = G_O + 8 * 4096
FRE_O = G_O
FIM_O = G_O + 8224
ATS_O = G_O + 16448


def g_slot(fc):
    return G_O + fc * 8192


def hw_slot(hc):
    return G_O + hc * 2048


FQT_O = [IFR_O + i * 4096 for i in range(4)]
IFRE_O = IFR_O
IFIM_O = IFR_O + 10240

WSB_O = WB_O


def abuf_slot(i):
    return WB_O + i * 2048


A2SCR_O = [WB_O + 8192, WB_O + 12288]


def rw1_slot(i):
    return WB_O + i * 2048


def rw2_slot(hc):
    if hc < 10:
        return IFR_O + hc * 2048
    return RW2HI_O[(hc - 10) // 2] + ((hc - 10) % 2) * 2048


def build_program(phases=7, debug=False):
    import os as _os
    K_SUB = int(_os.environ.get("K_SUB", "0"))
    K_TTLIM = int(_os.environ.get("K_TTLIM", "16"))
    nc = bass.Bass("TRN2", target_bir_lowering=False, num_devices=NCORES)

    x_d = nc.dram_tensor("x", [T, D], F32, kind="ExternalInput").ap()
    wq_d = nc.dram_tensor("Wq", [D, D], F32, kind="ExternalInput").ap()
    wk_d = nc.dram_tensor("Wk", [D, D], F32, kind="ExternalInput").ap()
    wv_d = nc.dram_tensor("Wv", [D, D], F32, kind="ExternalInput").ap()
    rww_d = nc.dram_tensor("router_w", [D, NE], F32, kind="ExternalInput").ap()
    rwb_d = nc.dram_tensor("router_b", [NE], F32, kind="ExternalInput").ap()
    sw1_d = nc.dram_tensor("sw1", [D, HID], F32, kind="ExternalInput").ap()
    sb1_d = nc.dram_tensor("sb1", [HID], F32, kind="ExternalInput").ap()
    sw2_d = nc.dram_tensor("sw2", [HID, D], F32, kind="ExternalInput").ap()
    sb2_d = nc.dram_tensor("sb2", [D], F32, kind="ExternalInput").ap()
    rw1_d = nc.dram_tensor("rw1", [NE, D, HID], F32, kind="ExternalInput").ap()
    rb1_d = nc.dram_tensor("rb1", [NE, HID], F32, kind="ExternalInput").ap()
    rw2_d = nc.dram_tensor("rw2", [NE, HID, D], F32, kind="ExternalInput").ap()
    rb2_d = nc.dram_tensor("rb2", [NE, D], F32, kind="ExternalInput").ap()
    fre_d = nc.dram_tensor("dft_re", [D, NF], BF16, kind="ExternalInput").ap()
    fim_d = nc.dram_tensor("dft_im", [D, NF], BF16, kind="ExternalInput").ap()
    ifre_d = nc.dram_tensor("idft_re", [640, D], BF16, kind="ExternalInput").ap()
    ifim_d = nc.dram_tensor("idft_im", [640, D], BF16, kind="ExternalInput").ap()
    ident_d = nc.dram_tensor("ident", [128, 128], F32, kind="ExternalInput").ap()
    onesf_d = nc.dram_tensor("onesf", [1, 128], F32, kind="ExternalInput").ap()
    onesb_d = nc.dram_tensor("onesb", [1, 128], BF16, kind="ExternalInput").ap()
    maskv_d = nc.dram_tensor("maskv", [NCORES], F32, kind="ExternalInput").ap()
    y_d = nc.dram_tensor("y", [T, D], F32, kind="ExternalOutput").ap()

    a_dram = nc.dram_tensor("a_scr", [6, D, NF], BF16, kind="Internal")
    a_nyq = nc.dram_tensor("a_nyq", [6, D], BF16, kind="Internal")
    cc_in = nc.dram_tensor("cc_in", [1, 1280], F32, kind="Internal")
    cc_out = nc.dram_tensor("cc_out", [NCORES, 1280], F32, kind="Internal")

    def at(name, shape, dtype, off):
        import os as _os
        if _os.environ.get("KAUTO") == "1":
            return nc.alloc_sbuf_tensor(name, shape, dtype)
        return nc.alloc_sbuf_tensor_at(name, shape, dtype, offset=off)

    xnT = at("xnT_s", [128, DC * T], BF16, XNT_O)
    xstg = [at(f"xstg{i}", [128, 1024], F32, XSTG_O[i]) for i in range(2)]
    xsq = at("xsq", [128, 1024], F32, XSQ_O)
    xnf = at("xnf", [128, 1024], F32, XNF_O)
    xnT32 = at("xnT32", [128, DC * 128], F32, XNT32_O)
    rb1raw = at("rb1raw", [NE, HID], F32, RB1RAW_O)
    sb1raw = at("sb1raw", [1, HID], F32, SB1RAW_O)
    gath = at("gath", [NCORES, 1280], F32, GATH_O)
    outp = [at(f"outp{i}", [128, 1024], F32, OUTP_O[i]) for i in range(16)]
    cre = [at(f"cre{fc}", [128, T], F32, R_O + fc * 16384) for fc in range(FC)]
    cim = [at(f"cim{fc}", [128, T], F32, R_O + fc * 16384 + 8192) for fc in range(FC)]
    a1t = [at(f"a1t{i}", [128, 1024], F32, A1T_O[i]) for i in range(8)]
    zeros = at("zeros", [128, 1024], F32, ZERO_O)
    fre_sb = at("fre_sb", [128, 8 * NF], BF16, FRE_O)
    fim_sb = at("fim_sb", [128, 8 * NF], BF16, FIM_O)
    ats = [at("ats", [128, NF], BF16, ATS_O),
           at("ats2", [128, NF], BF16, ATS_O + 1056)]
    gre = [at(f"gre{fc}", [128, T], BF16, g_slot(fc)) for fc in range(FC)]
    gim = [at(f"gim{fc}", [128, T], BF16, g_slot(fc) + 4096) for fc in range(FC)]
    hw = [at(f"hw{hc}", [128, 1024], BF16, hw_slot(hc)) for hc in range(HC)]
    fqt = [at(f"fqt{i}", [128, 1024], F32, FQT_O[i]) for i in range(4)]
    ifre_sb = at("ifre_sb", [128, FC * 1024], BF16, IFRE_O)
    ifim_sb = at("ifim_sb", [128, FC * 1024], BF16, IFIM_O)
    wsb = at("wsb", [128, 8 * 1024], BF16, WSB_O)
    abuf = [at(f"abuf{i}", [128, 8 * 128], BF16, abuf_slot(i)) for i in range(8)]
    a2scr = [at(f"a2scr{i}", [128, 1024], F32, A2SCR_O[i]) for i in range(2)]
    rw1b = [at(f"rw1b{i}", [128, 8 * 128], BF16, rw1_slot(i)) for i in range(2)]
    rw2b = [at(f"rw2b{hc}", [128, 1024], BF16, rw2_slot(hc)) for hc in range(HC)]

    ident = nc.alloc_sbuf_tensor("ident_sb", [128, 128], F32)
    onesf = nc.alloc_sbuf_tensor("onesf_sb", [1, 128], F32)
    onesb = nc.alloc_sbuf_tensor("onesb_sb", [1, 128], BF16)
    statnm = nc.alloc_sbuf_tensor("statnm", [128, TT], F32)
    statrs = nc.alloc_sbuf_tensor("statrs", [128, TT], F32)
    statpr = nc.alloc_sbuf_tensor("statpr", [128, TT], F32)
    wtok = nc.alloc_sbuf_tensor("wtok_sb", [128, TT * NE], F32)
    wT = nc.alloc_sbuf_tensor("wT_sb", [NE, T], BF16)
    rb1T = nc.alloc_sbuf_tensor("rb1T_sb", [128, HC * NE], F32)
    sb1T = nc.alloc_sbuf_tensor("sb1T_sb", [128, HC], F32)
    rb2sb = nc.alloc_sbuf_tensor("rb2_sb", [NE, D], BF16)
    sb2sb = nc.alloc_sbuf_tensor("sb2_sb", [1, D], BF16)
    rwwsb = nc.alloc_sbuf_tensor("rww_sb", [128, DC * NE], F32)
    rwbsb = nc.alloc_sbuf_tensor("rwb_sb", [1, NE], F32)
    Ssb = nc.alloc_sbuf_tensor("S_sb", [128, 10], F32)
    Ttot = nc.alloc_sbuf_tensor("Ttot_sb", [128, 10], F32)
    Trow = nc.alloc_sbuf_tensor("Trow_sb", [10, 128], F32)
    maskv = nc.alloc_sbuf_tensor("maskv_sb", [NCORES, 1], F32)
    sme = nc.alloc_sbuf_tensor("sme", [128, NE], F32)
    smp = nc.alloc_sbuf_tensor("smp", [128, NE], F32)
    smq = nc.alloc_sbuf_tensor("smq", [128, NE], F32)
    sms = nc.alloc_sbuf_tensor("sms", [128, 8], F32)
    cst = nc.alloc_sbuf_tensor("cst", [128, 2], F32)

    ps = [nc.alloc_psum_tensor(f"ps{i}", [128, 1024], F32) for i in range(4)]

    sem_names = (["te", "ve", "ac", "gp", "st", "cc", "s_set", "s_setg", "s_wab", "s_ast", "s_ast1", "s_if", "s_dbg", "sxa", "sxb", "sxc", "sxd", "sxd1", "sxd2", "sw10", "sw11", "s_rw2", "s_exq"] + [f"sab{i}" for i in range(8)])
    cnt = {s: 0 for s in sem_names}
    streams = {k: [] for k in ["te", "ve", "ac", "gp", "sy"]}

    def op(stream, fn, waits=(), inc=None, by=1):
        w = [t for t in waits if t is not None]
        streams[stream].append((fn, w, inc, by))
        if inc is None:
            return None
        cnt[inc] += by
        return (inc, cnt[inc])

    def dma(queue, sem, out, in_, waits=()):
        stream = {"qs": "sy", "qg": "gp", "qa": "ac"}[queue]
        return op(stream, lambda e, o=out, i=in_: e.dma_start(out=o, in_=i),
                  waits, inc=sem, by=16)

    def sdma(out, in_, waits=()):
        return op("sy", lambda e, o=out, i=in_: e.dma_start(out=o, in_=i),
                  waits, inc="st", by=16)

    tok = {}

    dma("qs", "s_set", ident[:], ident_d)
    dma("qs", "s_set", onesf[:], onesf_d)
    dma("qg", "s_setg", onesb[:], onesb_d)
    dma("qs", "s_set", rwwsb.ap().rearrange("p (dc e) -> p dc e", dc=DC),
        rww_d.rearrange("(dc p) e -> p dc e", p=128))
    dma("qs", "s_set", rwbsb[:], rwb_d[None, :])
    dma("qs", "s_set", maskv[:], maskv_d[:, None])
    dma("qs", "s_set", rb1raw[:], rb1_d)
    dma("qs", "s_set", sb1raw[:], sb1_d[None, :])
    dma("qg", "s_setg", rb2sb[:], rb2_d)
    dma("qg", "s_setg", sb2sb[:], sb2_d[None, :])
    dma("qg", "s_setg", fre_sb.ap().rearrange("p (dc f) -> p dc f", dc=8),
        fre_d.rearrange("(dc p) f -> p dc f", p=128))
    dma("qg", "s_setg", fim_sb.ap().rearrange("p (dc f) -> p dc f", dc=8),
        fim_d.rearrange("(dc p) f -> p dc f", p=128))
    t_set = ("s_set", cnt["s_set"])
    t_setg = ("s_setg", cnt["s_setg"])
    t_ident = t_ones = t_rww = t_rwb = t_maskv = t_rb1raw = t_sb1raw = t_set
    t_onesb = t_rb2 = t_sb2 = t_fre = t_fim = t_setg
    t_zero = op("gp", lambda e: e.memset(zeros[:], 0.0), inc="gp")
    op("gp", lambda e: e.memset(cst[:, 0:1], EPS_LN), inc=None)
    t_cst = op("gp", lambda e: e.memset(cst[:, 1:2], EPS_FFT), inc="gp")

    for hc in range(HC):
        tk = op("te", lambda e, hc=hc: e.transpose(
            ps[3][:, (hc % 2) * 512:(hc % 2) * 512 + 8],
            rb1raw[0:NE, hc * 128:(hc + 1) * 128], ident[0:NE, 0:NE]),
            waits=[t_rb1raw, t_ident, tok.get(("rb1ev", hc - 2))], inc="te")
        tok[("rb1ev", hc)] = op("ve", lambda e, hc=hc: e.tensor_copy(
            rb1T[:, hc * 8:(hc + 1) * 8], ps[3][:, (hc % 2) * 512:(hc % 2) * 512 + 8]),
            waits=[tk], inc="ve")
    for hc in range(HC):
        tk = op("te", lambda e, hc=hc: e.transpose(
            ps[2][:, (hc % 2) * 512:(hc % 2) * 512 + 1],
            sb1raw[0:1, hc * 128:(hc + 1) * 128], ident[0:1, 0:1]),
            waits=[t_sb1raw, t_ident, tok.get(("sb1ev", hc - 2))], inc="te")
        tok[("sb1ev", hc)] = op("ve", lambda e, hc=hc: e.tensor_copy(
            sb1T[:, hc:hc + 1], ps[2][:, (hc % 2) * 512:(hc % 2) * 512 + 1]),
            waits=[tk], inc="ve")
    t_bias_tr = tok[("sb1ev", HC - 1)]

    for ttl in range(TT if K_SUB != 1 else 0):
        s = ttl % 2
        t_x = dma("qs", ("sxa", "sxb")[s], xstg[s][:], x_d[ttl * 128:(ttl + 1) * 128, :],
                  waits=[tok.get(("lnxdone", ttl - 2))])
        t_sq = op("ac", lambda e, s=s: e.activation(xsq[:], xstg[s][:], AF.Square),
                  waits=[t_x, tok.get(("lnssq", ttl - 1))], inc="ac")
        t_ssq = op("ve", lambda e: e.reduce_sum(sms[:, 0:1], xsq[:], axis=AX),
                   waits=[t_sq], inc="ve")
        tok[("lnssq", ttl)] = t_ssq
        t_xs = op("ve", lambda e, s=s: e.reduce_sum(sms[:, 1:2], xstg[s][:], axis=AX),
                  waits=[t_x, t_ssq], inc="ve")
        t_nm = op("ve", lambda e, ttl=ttl: e.tensor_scalar_mul(
            statnm[:, ttl:ttl + 1], sms[:, 1:2], -1.0 / D), waits=[t_xs], inc="ve")
        t_m2 = op("ve", lambda e, ttl=ttl: e.tensor_mul(
            sms[:, 2:3], statnm[:, ttl:ttl + 1], statnm[:, ttl:ttl + 1]),
            waits=[t_nm], inc="ve")
        t_va = op("ve", lambda e: e.tensor_scalar_mul(sms[:, 3:4], sms[:, 0:1], 1.0 / D),
                  waits=[t_m2], inc="ve")
        t_vb = op("ve", lambda e: e.tensor_sub(sms[:, 4:5], sms[:, 3:4], sms[:, 2:3]),
                  waits=[t_va], inc="ve")
        t_ln = op("ac", lambda e: e.activation(sms[:, 5:6], sms[:, 4:5], AF.Ln, bias=cst[:, 0:1]),
                  waits=[t_vb, t_cst], inc="ac")
        t_rs = op("ac", lambda e, ttl=ttl: e.activation(
            statrs[:, ttl:ttl + 1], sms[:, 5:6], AF.Exp, scale=-0.5),
            waits=[t_ln], inc="ac")
        t_pr = op("ve", lambda e, ttl=ttl: e.tensor_mul(
            statpr[:, ttl:ttl + 1], statnm[:, ttl:ttl + 1], statrs[:, ttl:ttl + 1]),
            waits=[t_rs], inc="ve")
        tok[("pr", ttl)] = t_pr
        t_xn = op("ve", lambda e, s=s, ttl=ttl: e.tensor_scalar(
            xnf[:], xstg[s][:], statnm[:, ttl:ttl + 1], statrs[:, ttl:ttl + 1],
            ALU.add, ALU.mult),
            waits=[t_pr, tok.get(("lntr", ttl - 1))], inc="ve")
        tok[("lnxdone", ttl)] = t_xn
        if K_SUB == 2:
            tok[("lnevac", ttl)] = t_xn
            tok[("lnxnT", ttl)] = t_xn
            tok[("smdone", ttl)] = t_xn
            tok[("wT", ttl)] = t_xn
            tok[("w", ttl)] = t_xn
            tok[("lnrb", ttl)] = None
            tok[("lntr", ttl)] = None
            continue
        rot = ps[ttl % 2]
        t_tr = None
        for dc in range(DC):
            t_tr = op("te", lambda e, dc=dc, rot=rot: e.transpose(
                rot[:, dc * 128:(dc + 1) * 128], xnf[:, dc * 128:(dc + 1) * 128], ident[:]),
                waits=[t_xn, t_ident, tok.get(("lnevac", ttl - 2)),
                       tok.get(("lnxnT", ttl - 2))] if dc == 0 else [],
                inc="te" if dc == DC - 1 else None)
        tok[("lntr", ttl)] = t_tr
        last_ac = None
        for dc in range(DC):
            last_ac = op("ac", lambda e, dc=dc, rot=rot, ttl=ttl: e.copy(
                xnT[:, dc * T + ttl * 128:dc * T + (ttl + 1) * 128],
                rot[:, dc * 128:(dc + 1) * 128]), waits=[t_tr] if dc == 0 else [],
                inc="ac" if dc == DC - 1 else None)
        last_ev = None
        for dc in range(DC):
            last_ev = op("ve", lambda e, dc=dc, rot=rot: e.tensor_copy(
                xnT32[:, dc * 128:(dc + 1) * 128], rot[:, dc * 128:(dc + 1) * 128]),
                waits=[last_ac, tok.get(("lnrb", ttl - 1))] if dc == 0 else [],
                inc="ve" if dc == DC - 1 else None)
        tok[("lnevac", ttl)] = last_ev
        tok[("lnxnT", ttl)] = last_ac
        if K_SUB == 3:
            tok[("smdone", ttl)] = last_ev
            tok[("wT", ttl)] = last_ev
            tok[("w", ttl)] = last_ev
            tok[("lnrb", ttl)] = t_tr
            continue
        rps = ps[2 + ttl % 2]
        for dc in range(DC):
            op("te", lambda e, dc=dc, rps=rps: e.matmul(
                rps[:, 0:NE], xnT32[:, dc * 128:(dc + 1) * 128],
                rwwsb[:, dc * NE:(dc + 1) * NE], start=(dc == 0), stop=False),
                waits=[last_ev, t_rww, t_bias_tr, tok.get(("smdone", ttl - 2))]
                if dc == 0 else [], inc=None)
        t_rb = op("te", lambda e, rps=rps: e.matmul(
            rps[:, 0:NE], onesf[:], rwbsb[:], start=False, stop=True),
            waits=[t_ones, t_rwb], inc="te")
        tok[("lnrb", ttl)] = t_rb
        if K_SUB == 4:
            t_cp = op("ve", lambda e, rps=rps, ttl=ttl: e.tensor_copy(
                wtok[:, ttl * NE:(ttl + 1) * NE], rps[:, 0:NE]), waits=[t_rb], inc="ve")
            tok[("smdone", ttl)] = t_cp
            tok[("wT", ttl)] = t_cp
            tok[("w", ttl)] = t_cp
            continue
        t_mx = op("ve", lambda e, rps=rps: e.reduce_max(
            sms[:, 6:7], rps[:, 0:NE], axis=AX, negate=True), waits=[t_rb], inc="ve")
        t_ex = op("ac", lambda e, rps=rps: e.activation(
            sme[:], rps[:, 0:NE], AF.Exp, bias=sms[:, 6:7]), waits=[t_mx], inc="ac")
        t_sm = op("ve", lambda e: e.reduce_sum(sms[:, 7:8], sme[:], axis=AX),
                  waits=[t_ex], inc="ve")
        t_rc = op("ve", lambda e: e.reciprocal(sms[:, 0:1], sms[:, 7:8]),
                  waits=[t_sm], inc="ve")
        t_p = op("ve", lambda e: e.tensor_scalar_mul(smp[:], sme[:], sms[:, 0:1]),
                 waits=[t_rc], inc="ve")
        t_m1 = op("ve", lambda e: e.reduce_max(sms[:, 1:2], smp[:], axis=AX),
                  waits=[t_p], inc="ve")
        t_eq = op("ve", lambda e: e.tensor_single_scalar(
            smq[:], smp[:], sms[:, 1:2], ALU.is_equal), waits=[t_m1], inc="ve")
        t_mk = op("ve", lambda e: e.scalar_tensor_tensor(
            smq[:], smq[:], NEG_BIG, smp[:], ALU.mult, ALU.add), waits=[t_eq], inc="ve")
        t_m2b = op("ve", lambda e: e.reduce_max(sms[:, 2:3], smq[:], axis=AX),
                   waits=[t_mk], inc="ve")
        t_ge = op("ve", lambda e: e.tensor_single_scalar(
            smq[:], smp[:], sms[:, 2:3], ALU.is_ge), waits=[t_m2b], inc="ve")
        t_w = op("ve", lambda e, ttl=ttl: e.tensor_mul(
            wtok[:, ttl * NE:(ttl + 1) * NE], smp[:], smq[:]), waits=[t_ge], inc="ve")
        tok[("w", ttl)] = t_w
        t_wt = op("te", lambda e, rps=rps, ttl=ttl: e.transpose(
            rps[0:NE, 512:640], wtok[:, ttl * NE:(ttl + 1) * NE], ident[:]),
            waits=[t_w], inc="te")
        t_wte = op("ve", lambda e, rps=rps, ttl=ttl: e.tensor_copy(
            wT[:, ttl * 128:(ttl + 1) * 128], rps[0:NE, 512:640]), waits=[t_wt], inc="ve")
        tok[("smdone", ttl)] = t_wte
        tok[("wT", ttl)] = t_wte

    t_ln_all = tok.get(("lnevac", TT - 1))
    t_ln_te = tok.get(("lnrb", TT - 1))

    if phases >= 2:
        w_drams = [wq_d, wk_d, wv_d]
        unit = 0
        ab_ev = {}
        ab_st = {}
        for proj in range(3):
            t_w = dma("qg", "s_wab", wsb.ap().rearrange("p (ec d) -> p ec d", ec=8),
                      w_drams[proj].rearrange("(ec p) d -> p ec d", p=128),
                      waits=[ab_ev.get((proj - 1, 1, DC - 1)), t_zero])
            for pl in range(2):
                fsb = fre_sb if pl == 0 else fim_sb
                t_f = t_fre if pl == 0 else t_fim
                for dc in range(DC):
                    rot = ps[unit % 2]
                    t_mm = None
                    for ec in range(8):
                        for n0, n1 in [(0, 512), (512, 513)]:
                            t_mm = op("te", lambda e, rot=rot, ec=ec, dc=dc, fsb=fsb, n0=n0, n1=n1:
                                      e.matmul(rot[:, n0:n1],
                                               wsb[:, ec * 1024 + dc * 128:ec * 1024 + (dc + 1) * 128],
                                               fsb[:, ec * NF + n0:ec * NF + n1],
                                               start=(ec == 0), stop=(ec == 7)),
                                      waits=[t_w, t_f, ab_ev.get(unit - 2),
                                             tok[("lnevac", TT - 1)], tok[("lnevac", TT - 2)],
                                             tok[("lnxnT", TT - 1)], tok[("lnxnT", TT - 2)]]
                                      if ec == 0 and n0 == 0 else [],
                                      inc="te" if (ec == 7 and n0 == 512) else None)
                    slot = unit % 2
                    ssem = ("s_ast", "s_ast1")[slot]
                    t_ev = op("ac", lambda e, rot=rot, slot=slot: e.copy(ats[slot][:], rot[:, 0:NF]),
                              waits=[t_mm, (ssem, 32 * (unit // 2 - 1 + 1)) if unit >= 2 else None],
                              inc="ac")
                    dma("qa", ssem, a_dram.ap()[proj * 2 + pl, dc * 128:(dc + 1) * 128, :],
                        ats[slot][:], waits=[t_ev])
                    t_st = dma("qa", ssem, a_nyq.ap()[proj * 2 + pl, dc * 128:(dc + 1) * 128],
                               ats[slot][:, 512:513])
                    ab_ev[unit] = t_ev
                    ab_ev[(proj, pl, dc)] = t_ev
                    ab_st[(proj, pl)] = t_st
                    unit += 1
        t_ab_last_ev = [ab_ev[unit - 1], ab_ev[unit - 2]]
        t_ast_all = ("s_ast", cnt["s_ast"])
        t_ast_all1 = ("s_ast1", cnt["s_ast1"])
        for k in list(ab_st):
            ab_st[k] = t_ast_all
        ab_st1 = t_ast_all1


    if phases >= 3:
        def emit_a1_load(fc):
            fw = FCW[fc]
            for i, pl in enumerate([2, 3, 4, 5]):
                slot = i + 4 * (fc % 2)
                if fw == 1:
                    t_l = None
                    for dc in range(DC):
                        t_l = dma("qg", f"sab{slot}", abuf[slot].ap()[:, dc:dc + 1],
                                  a_nyq.ap()[pl, dc * 128:(dc + 1) * 128],
                                  waits=[ab_st[(pl // 2, pl % 2)], ab_st1,
                                         tok.get(("a1mmdone", fc - 2, i))] if dc == 0 else [])
                    tok[("a1w", fc, i)] = t_l
                else:
                    tok[("a1w", fc, i)] = dma(
                        "qg", f"sab{slot}", abuf[slot].ap()[:, 0:8 * fw].rearrange("p (dc f) -> p dc f", dc=DC),
                        a_dram.ap()[pl].rearrange("(dc p) f -> p dc f", p=128)[:, :, fc * 128:fc * 128 + fw],
                        waits=[ab_st[(pl // 2, pl % 2)], ab_st1, tok.get(("a1mmdone", fc - 2, i))])

        emit_a1_load(0)
        emit_a1_load(1)

        a1ev = {}
        a1u = 0
        for fc in range(FC):
            fw = FCW[fc]
            for tch in range(2):
                t0 = tch * 1024
                plane_ev = []
                for i in range(4):
                    slot = i + 4 * (fc % 2)
                    rot = ps[a1u % 4]
                    t_mm = None
                    for dc in range(DC):
                        for nq in range(2):
                            t_mm = op("te", lambda e, rot=rot, fw=fw, slot=slot, dc=dc, nq=nq, t0=t0:
                                      e.matmul(rot[0:fw, nq * 512:(nq + 1) * 512],
                                               abuf[slot][:, dc * fw:(dc + 1) * fw],
                                               xnT[:, dc * T + t0 + nq * 512:dc * T + t0 + (nq + 1) * 512],
                                               start=(dc == 0), stop=(dc == DC - 1)),
                                      waits=[tok[("a1w", fc, i)], a1ev.get(a1u - 4)] +
                                            (t_ab_last_ev if a1u < 2 else []) +
                                            ([tok[("smdone", TT - 1)]] if a1u in (2, 3) else [])
                                      if dc == 0 and nq == 0 else [],
                                      inc="te" if (dc == DC - 1 and nq == 1) else None)
                    if tch == 1:
                        tok[("a1mmdone", fc, i)] = t_mm
                        if i == 3 and fc + 2 < FC:
                            emit_a1_load(fc + 2)
                    prev_c = tok.get(("a1c", fc, tch - 1, 1)) if tch == 1 else \
                        tok.get(("a1c", fc - 1, 1, 1))
                    t_ev = op("ac", lambda e, i=i, rot=rot, fw=fw: e.copy(
                        a1t[i][0:fw, :], rot[0:fw, :]),
                        waits=[t_mm, prev_c if i == 0 else None], inc="ac")
                    plane_ev.append(t_ev)
                    a1ev[a1u] = t_ev
                    a1u += 1
                t_s1 = op("ac", lambda e, fw=fw: e.activation(a1t[4][0:fw, :], a1t[0][0:fw, :], AF.Square),
                          waits=[plane_ev[0]], inc="ac")
                t_s2 = op("ac", lambda e, fw=fw: e.activation(a1t[5][0:fw, :], a1t[1][0:fw, :], AF.Square),
                          waits=[plane_ev[1], t_s1], inc="ac")
                t_ss = op("ve", lambda e, fw=fw: e.tensor_add(a1t[4][0:fw, :], a1t[4][0:fw, :], a1t[5][0:fw, :]),
                          waits=[t_s2], inc="ve")
                t_l1 = op("ac", lambda e, fw=fw: e.activation(a1t[4][0:fw, :], a1t[4][0:fw, :], AF.Ln),
                          waits=[t_ss], inc="ac")
                t_e1 = op("ac", lambda e, fw=fw: e.activation(a1t[4][0:fw, :], a1t[4][0:fw, :], AF.Exp, scale=0.5),
                          waits=[t_l1], inc="ac")
                t_l2 = op("ac", lambda e, fw=fw: e.activation(a1t[4][0:fw, :], a1t[4][0:fw, :], AF.Ln, bias=cst[0:fw, 1:2]),
                          waits=[t_l1, t_e1], inc="ac")
                t_e2 = op("ac", lambda e, fw=fw: e.activation(a1t[4][0:fw, :], a1t[4][0:fw, :], AF.Exp, scale=-1.0),
                          waits=[t_l2], inc="ac")
                t_n1 = op("ve", lambda e, fw=fw: e.tensor_mul(a1t[0][0:fw, :], a1t[0][0:fw, :], a1t[4][0:fw, :]),
                          waits=[t_e2], inc="ve")
                t_n2 = op("ve", lambda e, fw=fw: e.tensor_mul(a1t[1][0:fw, :], a1t[1][0:fw, :], a1t[4][0:fw, :]),
                          waits=[t_n1], inc="ve")
                t_p1 = op("ve", lambda e, fw=fw: e.tensor_mul(a1t[5][0:fw, :], a1t[0][0:fw, :], a1t[2][0:fw, :]),
                          waits=[t_n2, plane_ev[2]], inc="ve")
                t_p2 = op("ve", lambda e, fw=fw: e.tensor_mul(a1t[6][0:fw, :], a1t[1][0:fw, :], a1t[3][0:fw, :]),
                          waits=[t_p1, plane_ev[3]], inc="ve")
                t_p3 = op("ve", lambda e, fw=fw: e.tensor_sub(a1t[5][0:fw, :], a1t[5][0:fw, :], a1t[6][0:fw, :]),
                          waits=[t_p2], inc="ve")
                t_c1 = op("ve", lambda e, fw=fw, fc=fc, t0=t0, tch=tch: e.tensor_tensor_scan(
                    cre[fc][0:fw, t0:t0 + 1024], zeros[0:fw, :], a1t[5][0:fw, :],
                    0.0 if tch == 0 else cre[fc][0:fw, t0 - 1:t0],
                    ALU.add, ALU.add),
                    waits=[t_p3, t_zero, tok.get(("a1c", fc, tch - 1, 0))], inc="ve")
                tok[("a1c", fc, tch, 0)] = t_c1
                t_q1 = op("ve", lambda e, fw=fw: e.tensor_mul(a1t[6][0:fw, :], a1t[0][0:fw, :], a1t[3][0:fw, :]),
                          waits=[t_c1], inc="ve")
                t_q2 = op("ve", lambda e, fw=fw: e.tensor_mul(a1t[7][0:fw, :], a1t[1][0:fw, :], a1t[2][0:fw, :]),
                          waits=[t_q1], inc="ve")
                t_q3 = op("ve", lambda e, fw=fw: e.tensor_add(a1t[6][0:fw, :], a1t[6][0:fw, :], a1t[7][0:fw, :]),
                          waits=[t_q2], inc="ve")
                t_c2 = op("ve", lambda e, fw=fw, fc=fc, t0=t0, tch=tch: e.tensor_tensor_scan(
                    cim[fc][0:fw, t0:t0 + 1024], zeros[0:fw, :], a1t[6][0:fw, :],
                    0.0 if tch == 0 else cim[fc][0:fw, t0 - 1:t0],
                    ALU.add, ALU.add),
                    waits=[t_q3, tok.get(("a1c", fc, tch - 1, 1))], inc="ve")
                tok[("a1c", fc, tch, 1)] = t_c2
        t_a1_last_ev = [a1ev[a1u - 1], a1ev[a1u - 2], a1ev[a1u - 3], a1ev[a1u - 4]]


    if phases >= 4:
        t_tz = op("ve", lambda e: e.memset(Ttot[:], 0.0), inc="ve")
        t_tt = t_tz
        for fc in range(FC):
            fw = FCW[fc]
            t_tt = op("ve", lambda e, fc=fc, fw=fw: e.tensor_copy(
                Ttot[0:fw, fc:fc + 1], cre[fc][0:fw, T - 1:T]),
                waits=[tok[("a1c", fc, 1, 0)], t_tt], inc="ve")
            t_tt = op("ve", lambda e, fc=fc, fw=fw: e.tensor_copy(
                Ttot[0:fw, 5 + fc:6 + fc], cim[fc][0:fw, T - 1:T]),
                waits=[tok[("a1c", fc, 1, 1)], t_tt], inc="ve")
        t_trw = op("ve", lambda e: e.tensor_copy(Trow[:], ps[3][0:10, 0:128]),
                   waits=[("sxd1", 1)], inc="ve")
        t_S = op("ve", lambda e: e.tensor_copy(Ssb[:], ps[3][:, 512:522]),
                 waits=[("sxd2", 1)], inc="ve")


    if phases >= 5:
        def emit_a2_load(fc):
            fw = FCW[fc]
            for i in range(2):
                slot = i + 2 * (fc % 2)
                w0 = [ab_st[(0, i)], ab_st1, tok.get(("a2mmdone", fc - 2, i)),
                      tok.get(("a1mmdone", FC - 1, i + 2 * (fc % 2))) if fc < 2 else None]
                if fw == 1:
                    t_l = None
                    for dc in range(DC):
                        t_l = dma("qg", f"sab{slot}", abuf[slot].ap()[:, dc:dc + 1],
                                  a_nyq.ap()[i, dc * 128:(dc + 1) * 128],
                                  waits=w0 if dc == 0 else [])
                    tok[("a2w", fc, i)] = t_l
                else:
                    tok[("a2w", fc, i)] = dma(
                        "qg", f"sab{slot}", abuf[slot].ap()[:, 0:8 * fw].rearrange("p (dc f) -> p dc f", dc=DC),
                        a_dram.ap()[i].rearrange("(dc p) f -> p dc f", p=128)[:, :, fc * 128:fc * 128 + fw],
                        waits=w0)

        emit_a2_load(0)
        emit_a2_load(1)
        a2ev = {}
        a2u = 0
        for fc in range(FC):
            fw = FCW[fc]
            for tch in range(2):
                t0 = tch * 1024
                plane_ev = []
                for i in range(2):
                    slot = i + 2 * (fc % 2)
                    rot = ps[a2u % 3]
                    t_mm = None
                    for dc in range(DC):
                        for nq in range(2):
                            t_mm = op("te", lambda e, rot=rot, fw=fw, slot=slot, dc=dc, nq=nq, t0=t0:
                                      e.matmul(rot[0:fw, nq * 512:(nq + 1) * 512],
                                               abuf[slot][:, dc * fw:(dc + 1) * fw],
                                               xnT[:, dc * T + t0 + nq * 512:dc * T + t0 + (nq + 1) * 512],
                                               start=(dc == 0), stop=(dc == DC - 1)),
                                      waits=[tok[("a2w", fc, i)], a2ev.get(a2u - 3)] +
                                            (t_a1_last_ev if a2u < 4 else [])
                                      if dc == 0 and nq == 0 else [],
                                      inc="te" if (dc == DC - 1 and nq == 1) else None)
                    if tch == 1:
                        tok[("a2mmdone", fc, i)] = t_mm
                        if i == 1 and fc + 2 < FC:
                            emit_a2_load(fc + 2)
                    prev_g = tok.get(("gdone", fc, tch - 1)) if tch == 1 else \
                        tok.get(("gdone", fc - 1, 1))
                    t_ev = op("ac", lambda e, i=i, rot=rot, fw=fw: e.copy(
                        fqt[i][0:fw, :], rot[0:fw, :]),
                        waits=[t_mm, prev_g if i == 0 else None], inc="ac")
                    plane_ev.append(t_ev)
                    a2ev[a2u] = t_ev
                    a2u += 1
                t_s1 = op("ac", lambda e, fw=fw: e.activation(a2scr[0][0:fw, :], fqt[0][0:fw, :], AF.Square),
                          waits=[plane_ev[0], tok.get(("gdone", fc, tch - 1)),
                                 tok.get(("gdone", fc - 1, 1))], inc="ac")
                t_s2 = op("ac", lambda e, fw=fw: e.activation(a2scr[1][0:fw, :], fqt[1][0:fw, :], AF.Square),
                          waits=[plane_ev[1], t_s1], inc="ac")
                t_ss = op("ve", lambda e, fw=fw: e.tensor_add(a2scr[0][0:fw, :], a2scr[0][0:fw, :], a2scr[1][0:fw, :]),
                          waits=[t_s2], inc="ve")
                t_l1 = op("ac", lambda e, fw=fw: e.activation(a2scr[0][0:fw, :], a2scr[0][0:fw, :], AF.Ln),
                          waits=[t_ss], inc="ac")
                t_e1 = op("ac", lambda e, fw=fw: e.activation(a2scr[0][0:fw, :], a2scr[0][0:fw, :], AF.Exp, scale=0.5),
                          waits=[t_l1], inc="ac")
                t_l2 = op("ac", lambda e, fw=fw: e.activation(a2scr[0][0:fw, :], a2scr[0][0:fw, :], AF.Ln, bias=cst[0:fw, 1:2]),
                          waits=[t_e1], inc="ac")
                t_e2 = op("ac", lambda e, fw=fw: e.activation(a2scr[0][0:fw, :], a2scr[0][0:fw, :], AF.Exp, scale=-1.0),
                          waits=[t_l2], inc="ac")
                t_n1 = op("ve", lambda e, fw=fw: e.tensor_mul(fqt[2][0:fw, :], fqt[0][0:fw, :], a2scr[0][0:fw, :]),
                          waits=[t_e2], inc="ve")
                t_n2 = op("ve", lambda e, fw=fw: e.tensor_mul(fqt[3][0:fw, :], fqt[1][0:fw, :], a2scr[0][0:fw, :]),
                          waits=[t_n1], inc="ve")
                jre, jim = fc, 5 + fc
                t_a = op("ve", lambda e, fw=fw, fc=fc, t0=t0, jre=jre: e.tensor_scalar_add(
                    fqt[0][0:fw, :], cre[fc][0:fw, t0:t0 + 1024], Ssb[0:fw, jre:jre + 1]),
                    waits=[t_n2, t_S], inc="ve")
                t_b = op("ve", lambda e, fw=fw, fc=fc, t0=t0, jim=jim: e.tensor_scalar_add(
                    fqt[1][0:fw, :], cim[fc][0:fw, t0:t0 + 1024], Ssb[0:fw, jim:jim + 1]),
                    waits=[t_a], inc="ve")
                t_u1 = op("ve", lambda e, fw=fw: e.tensor_mul(a2scr[0][0:fw, :], fqt[0][0:fw, :], fqt[2][0:fw, :]),
                          waits=[t_b], inc="ve")
                t_u2 = op("ve", lambda e, fw=fw: e.tensor_mul(a2scr[1][0:fw, :], fqt[1][0:fw, :], fqt[3][0:fw, :]),
                          waits=[t_u1], inc="ve")
                t_u3 = op("ve", lambda e, fw=fw: e.tensor_add(a2scr[0][0:fw, :], a2scr[0][0:fw, :], a2scr[1][0:fw, :]),
                          waits=[t_u2], inc="ve")
                t_g1 = op("ve", lambda e, fw=fw, fc=fc, t0=t0: e.tensor_copy(
                    gre[fc][0:fw, t0:t0 + 1024], a2scr[0][0:fw, :]), waits=[t_u3], inc="ve")
                t_v1 = op("ve", lambda e, fw=fw: e.tensor_mul(a2scr[1][0:fw, :], fqt[1][0:fw, :], fqt[2][0:fw, :]),
                          waits=[t_g1], inc="ve")
                t_v2 = op("ve", lambda e, fw=fw: e.tensor_mul(a2scr[0][0:fw, :], fqt[0][0:fw, :], fqt[3][0:fw, :]),
                          waits=[t_v1], inc="ve")
                t_v3 = op("ve", lambda e, fw=fw: e.tensor_sub(a2scr[1][0:fw, :], a2scr[1][0:fw, :], a2scr[0][0:fw, :]),
                          waits=[t_v2], inc="ve")
                t_g2 = op("ve", lambda e, fw=fw, fc=fc, t0=t0: e.tensor_copy(
                    gim[fc][0:fw, t0:t0 + 1024], a2scr[1][0:fw, :]), waits=[t_v3], inc="ve")
                tok[("gdone", fc, tch)] = t_g2
                if fc == 0 and tch == 0:
                    op("te", lambda e: e.transpose(ps[3][0:10, 0:128], Ttot[:, 0:10], ident[:]),
                       waits=[t_tt, tok[("smdone", TT - 1)]], inc="sxd1")
                if fc == 0 and tch == 1:
                    t_cin = dma("qs", "s_exq",
                                cc_in.ap().rearrange("o (j f) -> (o j) f", j=10), Trow[:],
                                waits=[t_trw])
                    if _os.environ.get("K_NOCC") == "1":
                        t_cc = None
                        for _r in range(NCORES):
                            t_cc = op("gp", lambda e, _r=_r: e.dma_start(
                                out=cc_out.ap()[_r:_r + 1, :], in_=cc_in.ap()),
                                waits=[t_cin] if _r == 0 else [], inc="cc", by=16)
                    else:
                        t_cc = op("gp", lambda e: e.collective_compute(
                            "AllGather", ALU.bypass, replica_groups=[list(range(NCORES))],
                            ins=[cc_in.ap().opt()], outs=[cc_out.ap().opt()]),
                            waits=[t_cin], inc="cc", by=1)
                    t_gth = dma("qs", "s_exq", gath[:], cc_out.ap(),
                                waits=[t_cc, tok[("lnxdone", TT - 1)]])
                    for j in range(10):
                        op("te", lambda e, j=j: e.matmul(
                            ps[3][:, 512 + j:513 + j], gath[0:NCORES, j * 128:(j + 1) * 128],
                            maskv[:], start=True, stop=True),
                            waits=[t_gth, t_maskv, t_trw] if j == 0 else [],
                            inc="sxd2" if j == 9 else None)
        t_g_all = tok[("gdone", FC - 1, 1)]

        dma("qg", "s_if", ifre_sb.ap().rearrange("p (fc d) -> p fc d", fc=FC),
            ifre_d.rearrange("(fc p) d -> p fc d", p=128)[:, 0:FC, :],
            waits=[t_g_all])
        t_ifim = dma("qg", "s_if", ifim_sb.ap().rearrange("p (fc d) -> p fc d", fc=FC),
                     ifim_d.rearrange("(fc p) d -> p fc d", p=128)[:, 0:FC, :])
        t_ifre = t_ifim


    if phases >= 6:
        xstg4 = xstg + [xsq, xnf]
        for ttl in range(TT):
            s = ttl % 4
            t_x = dma("qs", ("sxa", "sxb", "sxc", "sxd")[s], xstg4[s][:],
                      x_d[ttl * 128:(ttl + 1) * 128, :],
                      waits=[t_g_all if ttl < 4 else None, tok.get(("irfr1", ttl - 4))])
            rot = ps[ttl % 4]
            t_mm = None
            first = True
            for nd in range(2):
                for fc in range(FC):
                    fw = FCW[fc]
                    for pl in range(2):
                        gsb = gre[fc] if pl == 0 else gim[fc]
                        isb = ifre_sb if pl == 0 else ifim_sb
                        t_mm = op("te", lambda e, rot=rot, nd=nd, fc=fc, fw=fw, gsb=gsb, isb=isb, ttl=ttl, pl=pl:
                                  e.matmul(rot[:, nd * 512:(nd + 1) * 512],
                                           gsb[0:fw, ttl * 128:(ttl + 1) * 128],
                                           isb[0:fw, fc * 1024 + nd * 512:fc * 1024 + (nd + 1) * 512],
                                           start=(fc == 0 and pl == 0), stop=(fc == FC - 1 and pl == 1)),
                                  waits=[t_g_all, t_ifim, tok.get(("irfev", ttl - 2)),
                                         a2ev.get(a2u - 1), a2ev.get(a2u - 2)]
                                  if first else [],
                                  inc="te" if (nd == 1 and fc == FC - 1 and pl == 1) else None)
                        first = False
            t_xn2 = op("ve", lambda e, s=s, ttl=ttl: e.tensor_scalar(
                a2scr[0][:], xstg4[s][:], statrs[:, ttl:ttl + 1], statpr[:, ttl:ttl + 1],
                ALU.mult, ALU.add),
                waits=[t_x, tok[("pr", ttl)], tok.get(("irfr1", ttl - 1))], inc="ve")
            t_r1 = op("ve", lambda e, s=s, ttl=ttl: e.tensor_add(
                outp[ttl][:], xstg4[s][:], a2scr[0][:]), waits=[t_xn2], inc="ve")
            tok[("irfr1", ttl)] = t_r1
            t_r2 = op("ve", lambda e, rot=rot, ttl=ttl: e.tensor_add(
                outp[ttl][:], outp[ttl][:], rot[:]), waits=[t_r1, t_mm], inc="ve")
            tok[("irfev", ttl)] = t_r2
        t_irf_all = tok[("irfev", TT - 1)]


    if phases >= 7:
        def w1_dram(e):
            return rw1_d[e] if e < NE else sw1_d

        def w2_dram(e):
            return rw2_d[e] if e < NE else sw2_d

        uidx = 0
        hidx = 0
        ffng = {}
        ffnev = {}
        ffnh_mm = {}
        for th in range(2):
            t0 = th * 1024
            for e in range(NEXP):
                prev_u = tok.get(("ffnu_mm", th, e - 1)) if e > 0 else \
                    (tok.get(("ffnu_mm", th - 1, NEXP - 1)) if th > 0 else None)
                for hc in range(HC):
                    slot = hidx % 2
                    war = ffnh_mm.get(hidx - 2)
                    tok[("w1l", th, e, hc)] = dma(
                        "qg", ("sw10", "sw11")[slot], rw1b[slot].ap().rearrange("p (dc h) -> p dc h", dc=DC),
                        w1_dram(e).rearrange("(dc p) h -> p dc h", p=128)[:, :, hc * 128:(hc + 1) * 128],
                        waits=[war])
                    rot = ps[hidx % 2]
                    gelu_war = ffng.get(hidx - 2)
                    t_mm = None
                    for dc in range(DC):
                        for nq in range(2):
                            t_mm = op("te", lambda e_, rot=rot, slot=slot, dc=dc, nq=nq, t0=t0:
                                      e_.matmul(rot[:, nq * 512:(nq + 1) * 512],
                                                rw1b[slot][:, dc * 128:(dc + 1) * 128],
                                                xnT[:, dc * T + t0 + nq * 512:dc * T + t0 + (nq + 1) * 512],
                                                start=(dc == 0), stop=(dc == DC - 1)),
                                      waits=[tok[("w1l", th, e, hc)], gelu_war, t_irf_all]
                                      if dc == 0 and nq == 0 else [],
                                      inc="te" if (dc == DC - 1 and nq == 1) else None)
                    ffnh_mm[hidx] = t_mm
                    if e < NE:
                        bias_ap = (lambda hc=hc, e=e: rb1T[:, hc * NE + e:hc * NE + e + 1])
                    else:
                        bias_ap = (lambda hc=hc: sb1T[:, hc:hc + 1])
                    t_g = op("ac", lambda e_, rot=rot, hc=hc, b=bias_ap: e_.activation(
                        hw[hc][:], rot[:], AF.Gelu, bias=b()),
                        waits=[t_mm, prev_u, t_bias_tr, tok[("rb1ev", HC - 1)]], inc="ac")
                    ffng[hidx] = t_g
                    hidx += 1
                for hc in range(HC):
                    tok[("w2l", th, e, hc)] = dma(
                        "qg", "s_rw2", rw2b[hc][:], w2_dram(e)[hc * 128:(hc + 1) * 128, :],
                        waits=[prev_u, t_irf_all] if hc == 0 else [])
                t_gelu_all = ffng[hidx - 1]
                t_umm = None
                for tl in range(8):
                    ttl = th * 8 + tl
                    rot = ps[2 + uidx % 2]
                    war = ffnev.get(uidx - 2)
                    last = e == NEXP - 1
                    t_mm = None
                    for hc in range(HC):
                        for nd in range(2):
                            t_mm = op("te", lambda e_, rot=rot, hc=hc, nd=nd, tl=tl:
                                      e_.matmul(rot[:, nd * 512:(nd + 1) * 512],
                                                hw[hc][:, tl * 128:(tl + 1) * 128],
                                                rw2b[hc][:, nd * 512:(nd + 1) * 512],
                                                start=(hc == 0),
                                                stop=(hc == HC - 1 and not last)),
                                      waits=[t_gelu_all, tok[("w2l", th, e, HC - 1)], war,
                                             t_S if uidx < 2 else None]
                                      if hc == 0 and nd == 0 else [],
                                      inc="te" if (hc == HC - 1 and nd == 1 and not last) else None)
                    if last:
                        op("te", lambda e_, rot=rot, ttl=ttl: e_.matmul(
                            rot[:, 0:512], wT[:, ttl * 128:(ttl + 1) * 128],
                            rb2sb[:, 0:512], start=False, stop=False),
                            waits=[tok[("wT", ttl)], t_rb2], inc=None)
                        op("te", lambda e_, rot=rot, ttl=ttl: e_.matmul(
                            rot[:, 512:1024], wT[:, ttl * 128:(ttl + 1) * 128],
                            rb2sb[:, 512:1024], start=False, stop=False), inc=None)
                        op("te", lambda e_, rot=rot: e_.matmul(
                            rot[:, 0:512], onesb[:], sb2sb[:, 0:512],
                            start=False, stop=False), waits=[t_onesb, t_sb2], inc=None)
                        t_mm = op("te", lambda e_, rot=rot: e_.matmul(
                            rot[:, 512:1024], onesb[:], sb2sb[:, 512:1024],
                            start=False, stop=True), inc="te")
                    t_umm = t_mm
                    if e < NE:
                        t_ev = op("ve", lambda e_, rot=rot, ttl=ttl, e=e: e_.scalar_tensor_tensor(
                            outp[ttl][:], rot[:], wtok[:, ttl * NE + e:ttl * NE + e + 1],
                            outp[ttl][:], ALU.mult, ALU.add),
                            waits=[t_mm, tok[("w", ttl)], tok[("irfev", ttl)]], inc="ve")
                    else:
                        t_ev = op("ve", lambda e_, rot=rot, ttl=ttl: e_.tensor_add(
                            outp[ttl][:], outp[ttl][:], rot[:]),
                            waits=[t_mm, tok[("irfev", ttl)]], inc="ve")
                    ffnev[uidx] = t_ev
                    uidx += 1
                    if last:
                        sdma(y_d[ttl * 128:(ttl + 1) * 128, :], outp[ttl][:], waits=[t_ev])
                tok[("ffnu_mm", th, e)] = t_umm



    if debug:
        dbg_c = nc.dram_tensor("dbg_c", [2 * FC, 128, T], F32, kind="ExternalOutput").ap()
        dbg_w = nc.dram_tensor("dbg_w", [128, TT * NE], F32, kind="ExternalOutput").ap()
        dbg_stat = nc.dram_tensor("dbg_stat", [128, 3 * TT], F32, kind="ExternalOutput").ap()
        dbg_S = nc.dram_tensor("dbg_S", [128, 10], F32, kind="ExternalOutput").ap()
        if K_SUB == 1:
            sdma(dbg_w, rb1T[:], waits=[tok[("rb1ev", HC - 1)]])
        if phases >= 1 and K_SUB != 1:
            sdma(dbg_w, wtok[:], waits=[tok[("w", TT - 1)]])
            sdma(dbg_stat[:, 0:TT], statnm[:], waits=[tok[("pr", TT - 1)]])
            sdma(dbg_stat[:, TT:2 * TT], statrs[:], waits=[tok[("pr", TT - 1)]])
            sdma(dbg_stat[:, 2 * TT:3 * TT], statpr[:], waits=[tok[("pr", TT - 1)]])
        if phases >= 3 and phases < 6:
            for fc in range(FC):
                fw = FCW[fc]
                sdma(dbg_c[fc, 0:fw, :], cre[fc][0:fw, :], waits=[tok[("a1c", fc, 1, 0)]])
                sdma(dbg_c[FC + fc, 0:fw, :], cim[fc][0:fw, :], waits=[tok[("a1c", fc, 1, 1)]])
        if phases >= 4:
            sdma(dbg_S, Ssb[:], waits=[t_S])
        if phases >= 5:
            dbg_g = nc.dram_tensor("dbg_g", [2 * FC, 128, T], F32, kind="ExternalOutput").ap()
            for fc in range(FC):
                fw = FCW[fc]
                dma("qg", "s_dbg", dbg_g[fc, 0:fw, :], gre[fc][0:fw, :],
                    waits=[tok[("gdone", FC - 1, 1)]])
                dma("qg", "s_dbg", dbg_g[FC + fc, 0:fw, :], gim[fc][0:fw, :],
                    waits=[tok[("gdone", FC - 1, 1)]])
        if phases >= 6 and phases < 7:
            for ttl in range(TT):
                sdma(y_d[ttl * 128:(ttl + 1) * 128, :], outp[ttl][:],
                     waits=[tok[("irfev", ttl)]])
    op("sy", lambda e: e.nop(), waits=[("st", cnt["st"])])

    import contextlib
    with contextlib.ExitStack() as ctx:
        sems = {s: ctx.enter_context(nc.semaphore(s)) for s in sem_names}
        block = ctx.enter_context(nc.Block())

        def emit(stream_key):
            def body(eng):
                seen = {s: 0 for s in sem_names}
                for fn, waits, inc, by in streams[stream_key]:
                    for (s, v) in waits:
                        if v > seen[s]:
                            eng.wait_ge(sems[s], v)
                            seen[s] = v
                    ins = fn(eng)
                    if inc is not None:
                        ins.then_inc(sems[inc], by)
            return body

        block.tensor(emit("te"))
        block.vector(emit("ve"))
        block.scalar(emit("ac"))
        block.gpsimd(emit("gp"))
        block.sync(emit("sy"))

    return nc


_PROG_CACHE = {}


def _consts():
    e = np.arange(D, dtype=np.float64)
    f = np.arange(NF, dtype=np.float64)
    ang = 2.0 * np.pi * np.outer(e, f) / D
    fre = np.cos(ang)
    fim = -np.sin(ang)
    fim[:, 0] = 0.0
    fim[:, NF - 1] = 0.0
    dd = np.arange(D, dtype=np.float64)
    angi = 2.0 * np.pi * np.outer(f, dd) / D
    cf = np.full(NF, 2.0)
    cf[0] = 1.0
    cf[NF - 1] = 1.0
    ifre = (cf[:, None] / D) * np.cos(angi)
    ifim = -(cf[:, None] / D) * np.sin(angi)
    ifre_p = np.zeros((640, D))
    ifim_p = np.zeros((640, D))
    ifre_p[:NF] = ifre
    ifim_p[:NF] = ifim
    bf = ml_dtypes.bfloat16
    return (fre.astype(bf), fim.astype(bf), ifre_p.astype(bf), ifim_p.astype(bf))


def kernel(L0_attn_x, L0_attn_Wq, L0_attn_Wk, L0_attn_Wv,
           L0_ffn_router_w, L0_ffn_router_b, L0_ffn_sw1, L0_ffn_sb1,
           L0_ffn_sw2, L0_ffn_sb2, L0_ffn_rw1, L0_ffn_rb1,
           L0_ffn_rw2, L0_ffn_rb2):
    if "nc" not in _PROG_CACHE:
        _PROG_CACHE["nc"] = build_program()
    nc = _PROG_CACHE["nc"]

    fre, fim, ifre, ifim = _consts()
    ident = np.eye(128, dtype=np.float32)
    onesf = np.ones((1, 128), dtype=np.float32)
    onesb = np.ones((1, 128), dtype=ml_dtypes.bfloat16)

    x = np.ascontiguousarray(np.asarray(L0_attn_x, dtype=np.float32).reshape(NCORES * T, D))
    common = dict(
        Wq=np.asarray(L0_attn_Wq, np.float32), Wk=np.asarray(L0_attn_Wk, np.float32),
        Wv=np.asarray(L0_attn_Wv, np.float32),
        router_w=np.asarray(L0_ffn_router_w, np.float32),
        router_b=np.asarray(L0_ffn_router_b, np.float32),
        sw1=np.asarray(L0_ffn_sw1, np.float32), sb1=np.asarray(L0_ffn_sb1, np.float32),
        sw2=np.asarray(L0_ffn_sw2, np.float32), sb2=np.asarray(L0_ffn_sb2, np.float32),
        rw1=np.asarray(L0_ffn_rw1, np.float32), rb1=np.asarray(L0_ffn_rb1, np.float32),
        rw2=np.asarray(L0_ffn_rw2, np.float32), rb2=np.asarray(L0_ffn_rb2, np.float32),
        dft_re=fre, dft_im=fim, idft_re=ifre, idft_im=ifim,
        ident=ident, onesf=onesf, onesb=onesb,
    )
    in_maps = []
    for c in range(NCORES):
        mask = ((np.arange(NCORES) < c) & (np.arange(NCORES) // 4 == c // 4)).astype(np.float32)
        m = dict(common)
        m["x"] = x[c * T:(c + 1) * T]
        m["maskv"] = mask
        in_maps.append(m)
    res = run_bass_kernel_spmd(nc, in_maps, core_ids=list(range(NCORES)))
    out = np.concatenate([np.asarray(res.results[c]["y"], np.float32) for c in range(NCORES)], axis=0)
    return out.reshape(2, 8192, D)


# version 11
# speedup vs baseline: 1.1570x; 1.1570x over previous
import numpy as np
import ml_dtypes
import concourse.bass as bass
import concourse.mybir as mybir
from concourse.bass_utils import run_bass_kernel_spmd

F32 = mybir.dt.float32
BF16 = mybir.dt.bfloat16
AF = mybir.ActivationFunctionType
ALU = mybir.AluOpType
AX = mybir.AxisListType.X

NCORES = 8
D = 1024
HID = 2048
NE = 8
NEXP = NE + 1
T = 2048
NF = 513
DC = D // 128
HC = HID // 128
TT = T // 128
FC = 5
FCW = [128, 128, 128, 128, 1]
EPS_FFT = 1e-8
EPS_LN = 1e-5
NEG_BIG = -1e9

XNT_O = 36832
R_O = XNT_O + 32768
G_O = R_O + 81920
IFR_O = G_O + 40960
WB_O = IFR_O + 20480
END_O = WB_O + 16384
assert END_O <= 229344, END_O

OUTP_O = [R_O + i * 4096 for i in range(16)]
XSTG_O = [R_O + 16 * 4096, R_O + 17 * 4096]
XSQ_O = R_O + 18 * 4096
XNF_O = R_O + 19 * 4096
XNT32_O = R_O + 4 * 4096
RB1RAW_O = R_O + 5 * 4096
SB1RAW_O = R_O + 7 * 4096
GATH_O = G_O
RW2HI_O = [R_O + (16 + i) * 4096 for i in range(3)]

A1T_O = [G_O + i * 4096 for i in range(8)]
ZERO_O = G_O + 8 * 4096
FRE_O = G_O
FIM_O = G_O + 8224
ATS_O = G_O + 16448


def g_slot(fc):
    return G_O + fc * 8192


def hw_slot(hc):
    return G_O + hc * 2048


FQT_O = [IFR_O + i * 4096 for i in range(4)]
IFRE_O = IFR_O
IFIM_O = IFR_O + 10240

WSB_O = WB_O


def abuf_slot(i):
    return WB_O + i * 2048


A2SCR_O = [WB_O + 8192, WB_O + 12288]


def rw1_slot(i):
    return WB_O + i * 2048


def rw2_slot(hc):
    if hc < 10:
        return IFR_O + hc * 2048
    return RW2HI_O[(hc - 10) // 2] + ((hc - 10) % 2) * 2048


def build_program(phases=7, debug=False):
    import os as _os
    K_SUB = int(_os.environ.get("K_SUB", "0"))
    K_TTLIM = int(_os.environ.get("K_TTLIM", "16"))
    nc = bass.Bass("TRN2", target_bir_lowering=False, num_devices=NCORES)

    x_d = nc.dram_tensor("x", [T, D], F32, kind="ExternalInput").ap()
    wq_d = nc.dram_tensor("Wq", [D, D], F32, kind="ExternalInput").ap()
    wk_d = nc.dram_tensor("Wk", [D, D], F32, kind="ExternalInput").ap()
    wv_d = nc.dram_tensor("Wv", [D, D], F32, kind="ExternalInput").ap()
    rww_d = nc.dram_tensor("router_w", [D, NE], F32, kind="ExternalInput").ap()
    rwb_d = nc.dram_tensor("router_b", [NE], F32, kind="ExternalInput").ap()
    sw1_d = nc.dram_tensor("sw1", [D, HID], F32, kind="ExternalInput").ap()
    sb1_d = nc.dram_tensor("sb1", [HID], F32, kind="ExternalInput").ap()
    sw2_d = nc.dram_tensor("sw2", [HID, D], F32, kind="ExternalInput").ap()
    sb2_d = nc.dram_tensor("sb2", [D], F32, kind="ExternalInput").ap()
    rw1_d = nc.dram_tensor("rw1", [NE, D, HID], F32, kind="ExternalInput").ap()
    rb1_d = nc.dram_tensor("rb1", [NE, HID], F32, kind="ExternalInput").ap()
    rw2_d = nc.dram_tensor("rw2", [NE, HID, D], F32, kind="ExternalInput").ap()
    rb2_d = nc.dram_tensor("rb2", [NE, D], F32, kind="ExternalInput").ap()
    fre_d = nc.dram_tensor("dft_re", [D, NF], BF16, kind="ExternalInput").ap()
    fim_d = nc.dram_tensor("dft_im", [D, NF], BF16, kind="ExternalInput").ap()
    ifre_d = nc.dram_tensor("idft_re", [640, D], BF16, kind="ExternalInput").ap()
    ifim_d = nc.dram_tensor("idft_im", [640, D], BF16, kind="ExternalInput").ap()
    ident_d = nc.dram_tensor("ident", [128, 128], F32, kind="ExternalInput").ap()
    onesf_d = nc.dram_tensor("onesf", [1, 128], F32, kind="ExternalInput").ap()
    onesb_d = nc.dram_tensor("onesb", [1, 128], BF16, kind="ExternalInput").ap()
    maskv_d = nc.dram_tensor("maskv", [NCORES], F32, kind="ExternalInput").ap()
    y_d = nc.dram_tensor("y", [T, D], F32, kind="ExternalOutput").ap()

    a_dram = nc.dram_tensor("a_scr", [6, D, NF], BF16, kind="Internal")
    a_nyq = nc.dram_tensor("a_nyq", [6, D], BF16, kind="Internal")
    cc_in = nc.dram_tensor("cc_in", [1, 1280], F32, kind="Internal")
    cc_out = nc.dram_tensor("cc_out", [NCORES, 1280], F32, kind="Internal")

    def at(name, shape, dtype, off):
        import os as _os
        if _os.environ.get("KAUTO") == "1":
            return nc.alloc_sbuf_tensor(name, shape, dtype)
        return nc.alloc_sbuf_tensor_at(name, shape, dtype, offset=off)

    xnT = at("xnT_s", [128, DC * T], BF16, XNT_O)
    xstg = [at(f"xstg{i}", [128, 1024], F32, XSTG_O[i]) for i in range(2)]
    xsq = at("xsq", [128, 1024], F32, XSQ_O)
    xnf = at("xnf", [128, 1024], F32, XNF_O)
    xnT32 = at("xnT32", [128, DC * 128], F32, XNT32_O)
    rb1raw = at("rb1raw", [NE, HID], F32, RB1RAW_O)
    sb1raw = at("sb1raw", [1, HID], F32, SB1RAW_O)
    gath = at("gath", [NCORES, 1280], F32, GATH_O)
    outp = [at(f"outp{i}", [128, 1024], F32, OUTP_O[i]) for i in range(16)]
    cre = [at(f"cre{fc}", [128, T], F32, R_O + fc * 16384) for fc in range(FC)]
    cim = [at(f"cim{fc}", [128, T], F32, R_O + fc * 16384 + 8192) for fc in range(FC)]
    a1t = [at(f"a1t{i}", [128, 1024], F32, A1T_O[i]) for i in range(8)]
    zeros = at("zeros", [128, 1024], F32, ZERO_O)
    fre_sb = at("fre_sb", [128, 8 * NF], BF16, FRE_O)
    fim_sb = at("fim_sb", [128, 8 * NF], BF16, FIM_O)
    ats = [at("ats", [128, NF], BF16, ATS_O),
           at("ats2", [128, NF], BF16, ATS_O + 1056)]
    gre = [at(f"gre{fc}", [128, T], BF16, g_slot(fc)) for fc in range(FC)]
    gim = [at(f"gim{fc}", [128, T], BF16, g_slot(fc) + 4096) for fc in range(FC)]
    hw = [at(f"hw{hc}", [128, 1024], BF16, hw_slot(hc)) for hc in range(HC)]
    fqt = [at(f"fqt{i}", [128, 1024], F32, FQT_O[i]) for i in range(4)]
    ifre_sb = at("ifre_sb", [128, FC * 1024], BF16, IFRE_O)
    ifim_sb = at("ifim_sb", [128, FC * 1024], BF16, IFIM_O)
    wsb = at("wsb", [128, 8 * 1024], BF16, WSB_O)
    abuf = [at(f"abuf{i}", [128, 8 * 128], BF16, abuf_slot(i)) for i in range(8)]
    a2scr = [at(f"a2scr{i}", [128, 1024], F32, A2SCR_O[i]) for i in range(2)]
    rw1hb = [at("rw1h0", [128, 8 * 1024], BF16, G_O),
             at("rw1h1", [128, 8 * 1024], BF16, IFR_O)]
    rw2hb = [at("rw2h0", [128, 8 * 1024], BF16, G_O + 16384),
             at("rw2h1", [128, 8 * 1024], BF16, IFR_O + 16384)]
    hwb = [at("hwb0", [128, 8 * 512], BF16, R_O + 16 * 4096),
           at("hwb1", [128, 8 * 512], BF16, R_O + 18 * 4096)]

    ident = nc.alloc_sbuf_tensor("ident_sb", [128, 128], F32)
    onesf = nc.alloc_sbuf_tensor("onesf_sb", [1, 128], F32)
    onesb = nc.alloc_sbuf_tensor("onesb_sb", [1, 128], BF16)
    statnm = nc.alloc_sbuf_tensor("statnm", [128, TT], F32)
    statrs = nc.alloc_sbuf_tensor("statrs", [128, TT], F32)
    statpr = nc.alloc_sbuf_tensor("statpr", [128, TT], F32)
    wtok = nc.alloc_sbuf_tensor("wtok_sb", [128, TT * NE], F32)
    wT = nc.alloc_sbuf_tensor("wT_sb", [NE, T], BF16)
    rb1T = nc.alloc_sbuf_tensor("rb1T_sb", [128, HC * NE], F32)
    sb1T = nc.alloc_sbuf_tensor("sb1T_sb", [128, HC], F32)
    rb2sb = nc.alloc_sbuf_tensor("rb2_sb", [NE, D], BF16)
    sb2sb = nc.alloc_sbuf_tensor("sb2_sb", [1, D], BF16)
    rwwsb = nc.alloc_sbuf_tensor("rww_sb", [128, DC * NE], F32)
    rwbsb = nc.alloc_sbuf_tensor("rwb_sb", [1, NE], F32)
    Ssb = nc.alloc_sbuf_tensor("S_sb", [128, 10], F32)
    Ttot = nc.alloc_sbuf_tensor("Ttot_sb", [128, 10], F32)
    Trow = nc.alloc_sbuf_tensor("Trow_sb", [10, 128], F32)
    maskv = nc.alloc_sbuf_tensor("maskv_sb", [NCORES, 1], F32)
    sme = nc.alloc_sbuf_tensor("sme", [128, NE], F32)
    smp = nc.alloc_sbuf_tensor("smp", [128, NE], F32)
    smq = nc.alloc_sbuf_tensor("smq", [128, NE], F32)
    sms = nc.alloc_sbuf_tensor("sms", [128, 8], F32)
    cst = nc.alloc_sbuf_tensor("cst", [128, 2], F32)

    ps = [nc.alloc_psum_tensor(f"ps{i}", [128, 1024], F32) for i in range(4)]

    sem_names = (["te", "ve", "ac", "gp", "st", "cc", "s_set", "s_setg", "s_wab", "s_ast", "s_ast1", "s_if", "s_dbg", "sxa", "sxb", "sxc", "sxd", "sxd1", "sxd2", "sw10", "sw11", "sw20", "sw21", "s_rw2", "s_exq"] + [f"sab{i}" for i in range(8)])
    cnt = {s: 0 for s in sem_names}
    streams = {k: [] for k in ["te", "ve", "ac", "gp", "sy"]}

    def op(stream, fn, waits=(), inc=None, by=1):
        w = [t for t in waits if t is not None]
        streams[stream].append((fn, w, inc, by))
        if inc is None:
            return None
        cnt[inc] += by
        return (inc, cnt[inc])

    def dma(queue, sem, out, in_, waits=()):
        stream = {"qs": "sy", "qg": "gp", "qa": "ac"}[queue]
        return op(stream, lambda e, o=out, i=in_: e.dma_start(out=o, in_=i),
                  waits, inc=sem, by=16)

    def sdma(out, in_, waits=()):
        return op("sy", lambda e, o=out, i=in_: e.dma_start(out=o, in_=i),
                  waits, inc="st", by=16)

    tok = {}

    dma("qs", "s_set", ident[:], ident_d)
    dma("qs", "s_set", onesf[:], onesf_d)
    dma("qg", "s_setg", onesb[:], onesb_d)
    dma("qs", "s_set", rwwsb.ap().rearrange("p (dc e) -> p dc e", dc=DC),
        rww_d.rearrange("(dc p) e -> p dc e", p=128))
    dma("qs", "s_set", rwbsb[:], rwb_d[None, :])
    dma("qs", "s_set", maskv[:], maskv_d[:, None])
    dma("qs", "s_set", rb1raw[:], rb1_d)
    dma("qs", "s_set", sb1raw[:], sb1_d[None, :])
    dma("qg", "s_setg", rb2sb[:], rb2_d)
    dma("qg", "s_setg", sb2sb[:], sb2_d[None, :])
    dma("qg", "s_setg", fre_sb.ap().rearrange("p (dc f) -> p dc f", dc=8),
        fre_d.rearrange("(dc p) f -> p dc f", p=128))
    dma("qg", "s_setg", fim_sb.ap().rearrange("p (dc f) -> p dc f", dc=8),
        fim_d.rearrange("(dc p) f -> p dc f", p=128))
    t_set = ("s_set", cnt["s_set"])
    t_setg = ("s_setg", cnt["s_setg"])
    t_ident = t_ones = t_rww = t_rwb = t_maskv = t_rb1raw = t_sb1raw = t_set
    t_onesb = t_rb2 = t_sb2 = t_fre = t_fim = t_setg
    t_zero = op("gp", lambda e: e.memset(zeros[:], 0.0), inc="gp")
    op("gp", lambda e: e.memset(cst[:, 0:1], EPS_LN), inc=None)
    t_cst = op("gp", lambda e: e.memset(cst[:, 1:2], EPS_FFT), inc="gp")

    for hc in range(HC):
        tk = op("te", lambda e, hc=hc: e.transpose(
            ps[3][:, (hc % 2) * 512:(hc % 2) * 512 + 8],
            rb1raw[0:NE, hc * 128:(hc + 1) * 128], ident[0:NE, 0:NE]),
            waits=[t_rb1raw, t_ident, tok.get(("rb1ev", hc - 2))], inc="te")
        tok[("rb1ev", hc)] = op("ve", lambda e, hc=hc: e.tensor_copy(
            rb1T[:, hc * 8:(hc + 1) * 8], ps[3][:, (hc % 2) * 512:(hc % 2) * 512 + 8]),
            waits=[tk], inc="ve")
    for hc in range(HC):
        tk = op("te", lambda e, hc=hc: e.transpose(
            ps[2][:, (hc % 2) * 512:(hc % 2) * 512 + 1],
            sb1raw[0:1, hc * 128:(hc + 1) * 128], ident[0:1, 0:1]),
            waits=[t_sb1raw, t_ident, tok.get(("sb1ev", hc - 2))], inc="te")
        tok[("sb1ev", hc)] = op("ve", lambda e, hc=hc: e.tensor_copy(
            sb1T[:, hc:hc + 1], ps[2][:, (hc % 2) * 512:(hc % 2) * 512 + 1]),
            waits=[tk], inc="ve")
    t_bias_tr = tok[("sb1ev", HC - 1)]

    for ttl in range(TT if K_SUB != 1 else 0):
        s = ttl % 2
        t_x = dma("qs", ("sxa", "sxb")[s], xstg[s][:], x_d[ttl * 128:(ttl + 1) * 128, :],
                  waits=[tok.get(("lnxdone", ttl - 2))])
        t_sq = op("ac", lambda e, s=s: e.activation(xsq[:], xstg[s][:], AF.Square),
                  waits=[t_x, tok.get(("lnssq", ttl - 1))], inc="ac")
        t_ssq = op("ve", lambda e: e.reduce_sum(sms[:, 0:1], xsq[:], axis=AX),
                   waits=[t_sq], inc="ve")
        tok[("lnssq", ttl)] = t_ssq
        t_xs = op("ve", lambda e, s=s: e.reduce_sum(sms[:, 1:2], xstg[s][:], axis=AX),
                  waits=[t_x, t_ssq], inc="ve")
        t_nm = op("ve", lambda e, ttl=ttl: e.tensor_scalar_mul(
            statnm[:, ttl:ttl + 1], sms[:, 1:2], -1.0 / D), waits=[t_xs], inc="ve")
        t_m2 = op("ve", lambda e, ttl=ttl: e.tensor_mul(
            sms[:, 2:3], statnm[:, ttl:ttl + 1], statnm[:, ttl:ttl + 1]),
            waits=[t_nm], inc="ve")
        t_va = op("ve", lambda e: e.tensor_scalar_mul(sms[:, 3:4], sms[:, 0:1], 1.0 / D),
                  waits=[t_m2], inc="ve")
        t_vb = op("ve", lambda e: e.tensor_sub(sms[:, 4:5], sms[:, 3:4], sms[:, 2:3]),
                  waits=[t_va], inc="ve")
        t_ln = op("ac", lambda e: e.activation(sms[:, 5:6], sms[:, 4:5], AF.Ln, bias=cst[:, 0:1]),
                  waits=[t_vb, t_cst], inc="ac")
        t_rs = op("ac", lambda e, ttl=ttl: e.activation(
            statrs[:, ttl:ttl + 1], sms[:, 5:6], AF.Exp, scale=-0.5),
            waits=[t_ln], inc="ac")
        t_pr = op("ve", lambda e, ttl=ttl: e.tensor_mul(
            statpr[:, ttl:ttl + 1], statnm[:, ttl:ttl + 1], statrs[:, ttl:ttl + 1]),
            waits=[t_rs], inc="ve")
        tok[("pr", ttl)] = t_pr
        t_xn = op("ve", lambda e, s=s, ttl=ttl: e.tensor_scalar(
            xnf[:], xstg[s][:], statnm[:, ttl:ttl + 1], statrs[:, ttl:ttl + 1],
            ALU.add, ALU.mult),
            waits=[t_pr, tok.get(("lntr", ttl - 1))], inc="ve")
        tok[("lnxdone", ttl)] = t_xn
        if K_SUB == 2:
            tok[("lnevac", ttl)] = t_xn
            tok[("lnxnT", ttl)] = t_xn
            tok[("smdone", ttl)] = t_xn
            tok[("wT", ttl)] = t_xn
            tok[("w", ttl)] = t_xn
            tok[("lnrb", ttl)] = None
            tok[("lntr", ttl)] = None
            continue
        rot = ps[ttl % 2]
        t_tr = None
        for dc in range(DC):
            t_tr = op("te", lambda e, dc=dc, rot=rot: e.transpose(
                rot[:, dc * 128:(dc + 1) * 128], xnf[:, dc * 128:(dc + 1) * 128], ident[:]),
                waits=[t_xn, t_ident, tok.get(("lnevac", ttl - 2)),
                       tok.get(("lnxnT", ttl - 2))] if dc == 0 else [],
                inc="te" if dc == DC - 1 else None)
        tok[("lntr", ttl)] = t_tr
        last_ac = None
        for dc in range(DC):
            last_ac = op("ac", lambda e, dc=dc, rot=rot, ttl=ttl: e.copy(
                xnT[:, dc * T + ttl * 128:dc * T + (ttl + 1) * 128],
                rot[:, dc * 128:(dc + 1) * 128]), waits=[t_tr] if dc == 0 else [],
                inc="ac" if dc == DC - 1 else None)
        last_ev = None
        for dc in range(DC):
            last_ev = op("ve", lambda e, dc=dc, rot=rot: e.tensor_copy(
                xnT32[:, dc * 128:(dc + 1) * 128], rot[:, dc * 128:(dc + 1) * 128]),
                waits=[last_ac, tok.get(("lnrb", ttl - 1))] if dc == 0 else [],
                inc="ve" if dc == DC - 1 else None)
        tok[("lnevac", ttl)] = last_ev
        tok[("lnxnT", ttl)] = last_ac
        if K_SUB == 3:
            tok[("smdone", ttl)] = last_ev
            tok[("wT", ttl)] = last_ev
            tok[("w", ttl)] = last_ev
            tok[("lnrb", ttl)] = t_tr
            continue
        rps = ps[2 + ttl % 2]
        for dc in range(DC):
            op("te", lambda e, dc=dc, rps=rps: e.matmul(
                rps[:, 0:NE], xnT32[:, dc * 128:(dc + 1) * 128],
                rwwsb[:, dc * NE:(dc + 1) * NE], start=(dc == 0), stop=False),
                waits=[last_ev, t_rww, t_bias_tr, tok.get(("smdone", ttl - 2))]
                if dc == 0 else [], inc=None)
        t_rb = op("te", lambda e, rps=rps: e.matmul(
            rps[:, 0:NE], onesf[:], rwbsb[:], start=False, stop=True),
            waits=[t_ones, t_rwb], inc="te")
        tok[("lnrb", ttl)] = t_rb
        if K_SUB == 4:
            t_cp = op("ve", lambda e, rps=rps, ttl=ttl: e.tensor_copy(
                wtok[:, ttl * NE:(ttl + 1) * NE], rps[:, 0:NE]), waits=[t_rb], inc="ve")
            tok[("smdone", ttl)] = t_cp
            tok[("wT", ttl)] = t_cp
            tok[("w", ttl)] = t_cp
            continue
        t_mx = op("ve", lambda e, rps=rps: e.reduce_max(
            sms[:, 6:7], rps[:, 0:NE], axis=AX, negate=True), waits=[t_rb], inc="ve")
        t_ex = op("ac", lambda e, rps=rps: e.activation(
            sme[:], rps[:, 0:NE], AF.Exp, bias=sms[:, 6:7]), waits=[t_mx], inc="ac")
        t_sm = op("ve", lambda e: e.reduce_sum(sms[:, 7:8], sme[:], axis=AX),
                  waits=[t_ex], inc="ve")
        t_rc = op("ve", lambda e: e.reciprocal(sms[:, 0:1], sms[:, 7:8]),
                  waits=[t_sm], inc="ve")
        t_p = op("ve", lambda e: e.tensor_scalar_mul(smp[:], sme[:], sms[:, 0:1]),
                 waits=[t_rc], inc="ve")
        t_m1 = op("ve", lambda e: e.reduce_max(sms[:, 1:2], smp[:], axis=AX),
                  waits=[t_p], inc="ve")
        t_eq = op("ve", lambda e: e.tensor_single_scalar(
            smq[:], smp[:], sms[:, 1:2], ALU.is_equal), waits=[t_m1], inc="ve")
        t_mk = op("ve", lambda e: e.scalar_tensor_tensor(
            smq[:], smq[:], NEG_BIG, smp[:], ALU.mult, ALU.add), waits=[t_eq], inc="ve")
        t_m2b = op("ve", lambda e: e.reduce_max(sms[:, 2:3], smq[:], axis=AX),
                   waits=[t_mk], inc="ve")
        t_ge = op("ve", lambda e: e.tensor_single_scalar(
            smq[:], smp[:], sms[:, 2:3], ALU.is_ge), waits=[t_m2b], inc="ve")
        t_w = op("ve", lambda e, ttl=ttl: e.tensor_mul(
            wtok[:, ttl * NE:(ttl + 1) * NE], smp[:], smq[:]), waits=[t_ge], inc="ve")
        tok[("w", ttl)] = t_w
        t_wt = op("te", lambda e, rps=rps, ttl=ttl: e.transpose(
            rps[0:NE, 512:640], wtok[:, ttl * NE:(ttl + 1) * NE], ident[:]),
            waits=[t_w], inc="te")
        t_wte = op("ve", lambda e, rps=rps, ttl=ttl: e.tensor_copy(
            wT[:, ttl * 128:(ttl + 1) * 128], rps[0:NE, 512:640]), waits=[t_wt], inc="ve")
        tok[("smdone", ttl)] = t_wte
        tok[("wT", ttl)] = t_wte

    t_ln_all = tok.get(("lnevac", TT - 1))
    t_ln_te = tok.get(("lnrb", TT - 1))

    if phases >= 2:
        w_drams = [wq_d, wk_d, wv_d]
        unit = 0
        ab_ev = {}
        ab_st = {}
        for proj in range(3):
            t_w = dma("qg", "s_wab", wsb.ap().rearrange("p (ec d) -> p ec d", ec=8),
                      w_drams[proj].rearrange("(ec p) d -> p ec d", p=128),
                      waits=[ab_ev.get((proj - 1, 1, DC - 1)), t_zero])
            for pl in range(2):
                fsb = fre_sb if pl == 0 else fim_sb
                t_f = t_fre if pl == 0 else t_fim
                for dc in range(DC):
                    rot = ps[unit % 2]
                    t_mm = None
                    for ec in range(8):
                        for n0, n1 in [(0, 512), (512, 513)]:
                            t_mm = op("te", lambda e, rot=rot, ec=ec, dc=dc, fsb=fsb, n0=n0, n1=n1:
                                      e.matmul(rot[:, n0:n1],
                                               wsb[:, ec * 1024 + dc * 128:ec * 1024 + (dc + 1) * 128],
                                               fsb[:, ec * NF + n0:ec * NF + n1],
                                               start=(ec == 0), stop=(ec == 7)),
                                      waits=[t_w, t_f, ab_ev.get(unit - 2),
                                             tok[("lnevac", TT - 1)], tok[("lnevac", TT - 2)],
                                             tok[("lnxnT", TT - 1)], tok[("lnxnT", TT - 2)]]
                                      if ec == 0 and n0 == 0 else [],
                                      inc="te" if (ec == 7 and n0 == 512) else None)
                    slot = unit % 2
                    ssem = ("s_ast", "s_ast1")[slot]
                    t_ev = op("ac", lambda e, rot=rot, slot=slot: e.copy(ats[slot][:], rot[:, 0:NF]),
                              waits=[t_mm, (ssem, 32 * (unit // 2 - 1 + 1)) if unit >= 2 else None],
                              inc="ac")
                    dma("qa", ssem, a_dram.ap()[proj * 2 + pl, dc * 128:(dc + 1) * 128, :],
                        ats[slot][:], waits=[t_ev])
                    t_st = dma("qa", ssem, a_nyq.ap()[proj * 2 + pl, dc * 128:(dc + 1) * 128],
                               ats[slot][:, 512:513])
                    ab_ev[unit] = t_ev
                    ab_ev[(proj, pl, dc)] = t_ev
                    ab_st[(proj, pl)] = t_st
                    unit += 1
        t_ab_last_ev = [ab_ev[unit - 1], ab_ev[unit - 2]]
        t_ast_all = ("s_ast", cnt["s_ast"])
        t_ast_all1 = ("s_ast1", cnt["s_ast1"])
        for k in list(ab_st):
            ab_st[k] = t_ast_all
        ab_st1 = t_ast_all1


    if phases >= 3:
        def emit_a1_load(fc):
            fw = FCW[fc]
            for i, pl in enumerate([2, 3, 4, 5]):
                slot = i + 4 * (fc % 2)
                if fw == 1:
                    t_l = None
                    for dc in range(DC):
                        t_l = dma("qg", f"sab{slot}", abuf[slot].ap()[:, dc:dc + 1],
                                  a_nyq.ap()[pl, dc * 128:(dc + 1) * 128],
                                  waits=[ab_st[(pl // 2, pl % 2)], ab_st1,
                                         tok.get(("a1mmdone", fc - 2, i))] if dc == 0 else [])
                    tok[("a1w", fc, i)] = t_l
                else:
                    tok[("a1w", fc, i)] = dma(
                        "qg", f"sab{slot}", abuf[slot].ap()[:, 0:8 * fw].rearrange("p (dc f) -> p dc f", dc=DC),
                        a_dram.ap()[pl].rearrange("(dc p) f -> p dc f", p=128)[:, :, fc * 128:fc * 128 + fw],
                        waits=[ab_st[(pl // 2, pl % 2)], ab_st1, tok.get(("a1mmdone", fc - 2, i))])

        emit_a1_load(0)
        emit_a1_load(1)

        a1ev = {}
        a1u = 0
        for fc in range(FC):
            fw = FCW[fc]
            for tch in range(2):
                t0 = tch * 1024
                plane_ev = []
                for i in range(4):
                    slot = i + 4 * (fc % 2)
                    rot = ps[a1u % 4]
                    t_mm = None
                    for dc in range(DC):
                        for nq in range(2):
                            t_mm = op("te", lambda e, rot=rot, fw=fw, slot=slot, dc=dc, nq=nq, t0=t0:
                                      e.matmul(rot[0:fw, nq * 512:(nq + 1) * 512],
                                               abuf[slot][:, dc * fw:(dc + 1) * fw],
                                               xnT[:, dc * T + t0 + nq * 512:dc * T + t0 + (nq + 1) * 512],
                                               start=(dc == 0), stop=(dc == DC - 1)),
                                      waits=[tok[("a1w", fc, i)], a1ev.get(a1u - 4)] +
                                            (t_ab_last_ev if a1u < 2 else []) +
                                            ([tok[("smdone", TT - 1)]] if a1u in (2, 3) else [])
                                      if dc == 0 and nq == 0 else [],
                                      inc="te" if (dc == DC - 1 and nq == 1) else None)
                    if tch == 1:
                        tok[("a1mmdone", fc, i)] = t_mm
                        if i == 3 and fc + 2 < FC:
                            emit_a1_load(fc + 2)
                    prev_c = tok.get(("a1c", fc, tch - 1, 1)) if tch == 1 else \
                        tok.get(("a1c", fc - 1, 1, 1))
                    t_ev = op("ac", lambda e, i=i, rot=rot, fw=fw: e.copy(
                        a1t[i][0:fw, :], rot[0:fw, :]),
                        waits=[t_mm, prev_c if i == 0 else None], inc="ac")
                    plane_ev.append(t_ev)
                    a1ev[a1u] = t_ev
                    a1u += 1
                t_s1 = op("ac", lambda e, fw=fw: e.activation(a1t[4][0:fw, :], a1t[0][0:fw, :], AF.Square),
                          waits=[plane_ev[0]], inc="ac")
                t_s2 = op("ac", lambda e, fw=fw: e.activation(a1t[5][0:fw, :], a1t[1][0:fw, :], AF.Square),
                          waits=[plane_ev[1], t_s1], inc="ac")
                t_ss = op("ve", lambda e, fw=fw: e.tensor_add(a1t[4][0:fw, :], a1t[4][0:fw, :], a1t[5][0:fw, :]),
                          waits=[t_s2], inc="ve")
                t_l1 = op("ac", lambda e, fw=fw: e.activation(a1t[4][0:fw, :], a1t[4][0:fw, :], AF.Ln),
                          waits=[t_ss], inc="ac")
                t_e1 = op("ac", lambda e, fw=fw: e.activation(a1t[4][0:fw, :], a1t[4][0:fw, :], AF.Exp, scale=0.5),
                          waits=[t_l1], inc="ac")
                t_l2 = op("ac", lambda e, fw=fw: e.activation(a1t[4][0:fw, :], a1t[4][0:fw, :], AF.Ln, bias=cst[0:fw, 1:2]),
                          waits=[t_l1, t_e1], inc="ac")
                t_e2 = op("ac", lambda e, fw=fw: e.activation(a1t[4][0:fw, :], a1t[4][0:fw, :], AF.Exp, scale=-1.0),
                          waits=[t_l2], inc="ac")
                t_n1 = op("ve", lambda e, fw=fw: e.tensor_mul(a1t[0][0:fw, :], a1t[0][0:fw, :], a1t[4][0:fw, :]),
                          waits=[t_e2], inc="ve")
                t_n2 = op("ve", lambda e, fw=fw: e.tensor_mul(a1t[1][0:fw, :], a1t[1][0:fw, :], a1t[4][0:fw, :]),
                          waits=[t_n1], inc="ve")
                t_p1 = op("ve", lambda e, fw=fw: e.tensor_mul(a1t[5][0:fw, :], a1t[0][0:fw, :], a1t[2][0:fw, :]),
                          waits=[t_n2, plane_ev[2]], inc="ve")
                t_p2 = op("ve", lambda e, fw=fw: e.tensor_mul(a1t[6][0:fw, :], a1t[1][0:fw, :], a1t[3][0:fw, :]),
                          waits=[t_p1, plane_ev[3]], inc="ve")
                t_p3 = op("ve", lambda e, fw=fw: e.tensor_sub(a1t[5][0:fw, :], a1t[5][0:fw, :], a1t[6][0:fw, :]),
                          waits=[t_p2], inc="ve")
                t_c1 = op("ve", lambda e, fw=fw, fc=fc, t0=t0, tch=tch: e.tensor_tensor_scan(
                    cre[fc][0:fw, t0:t0 + 1024], zeros[0:fw, :], a1t[5][0:fw, :],
                    0.0 if tch == 0 else cre[fc][0:fw, t0 - 1:t0],
                    ALU.add, ALU.add),
                    waits=[t_p3, t_zero, tok.get(("a1c", fc, tch - 1, 0))], inc="ve")
                tok[("a1c", fc, tch, 0)] = t_c1
                t_q1 = op("ve", lambda e, fw=fw: e.tensor_mul(a1t[6][0:fw, :], a1t[0][0:fw, :], a1t[3][0:fw, :]),
                          waits=[t_c1], inc="ve")
                t_q2 = op("ve", lambda e, fw=fw: e.tensor_mul(a1t[7][0:fw, :], a1t[1][0:fw, :], a1t[2][0:fw, :]),
                          waits=[t_q1], inc="ve")
                t_q3 = op("ve", lambda e, fw=fw: e.tensor_add(a1t[6][0:fw, :], a1t[6][0:fw, :], a1t[7][0:fw, :]),
                          waits=[t_q2], inc="ve")
                t_c2 = op("ve", lambda e, fw=fw, fc=fc, t0=t0, tch=tch: e.tensor_tensor_scan(
                    cim[fc][0:fw, t0:t0 + 1024], zeros[0:fw, :], a1t[6][0:fw, :],
                    0.0 if tch == 0 else cim[fc][0:fw, t0 - 1:t0],
                    ALU.add, ALU.add),
                    waits=[t_q3, tok.get(("a1c", fc, tch - 1, 1))], inc="ve")
                tok[("a1c", fc, tch, 1)] = t_c2
        t_a1_last_ev = [a1ev[a1u - 1], a1ev[a1u - 2], a1ev[a1u - 3], a1ev[a1u - 4]]


    if phases >= 4:
        t_tz = op("ve", lambda e: e.memset(Ttot[:], 0.0), inc="ve")
        t_tt = t_tz
        for fc in range(FC):
            fw = FCW[fc]
            t_tt = op("ve", lambda e, fc=fc, fw=fw: e.tensor_copy(
                Ttot[0:fw, fc:fc + 1], cre[fc][0:fw, T - 1:T]),
                waits=[tok[("a1c", fc, 1, 0)], t_tt], inc="ve")
            t_tt = op("ve", lambda e, fc=fc, fw=fw: e.tensor_copy(
                Ttot[0:fw, 5 + fc:6 + fc], cim[fc][0:fw, T - 1:T]),
                waits=[tok[("a1c", fc, 1, 1)], t_tt], inc="ve")
        t_trw = op("ve", lambda e: e.tensor_copy(Trow[:], ps[3][0:10, 0:128]),
                   waits=[("sxd1", 1)], inc="ve")
        t_S = op("ve", lambda e: e.tensor_copy(Ssb[:], ps[3][:, 512:522]),
                 waits=[("sxd2", 1)], inc="ve")


    if phases >= 5:
        def emit_a2_load(fc):
            fw = FCW[fc]
            for i in range(2):
                slot = i + 2 * (fc % 2)
                w0 = [ab_st[(0, i)], ab_st1, tok.get(("a2mmdone", fc - 2, i)),
                      tok.get(("a1mmdone", FC - 1, i + 2 * (fc % 2))) if fc < 2 else None]
                if fw == 1:
                    t_l = None
                    for dc in range(DC):
                        t_l = dma("qg", f"sab{slot}", abuf[slot].ap()[:, dc:dc + 1],
                                  a_nyq.ap()[i, dc * 128:(dc + 1) * 128],
                                  waits=w0 if dc == 0 else [])
                    tok[("a2w", fc, i)] = t_l
                else:
                    tok[("a2w", fc, i)] = dma(
                        "qg", f"sab{slot}", abuf[slot].ap()[:, 0:8 * fw].rearrange("p (dc f) -> p dc f", dc=DC),
                        a_dram.ap()[i].rearrange("(dc p) f -> p dc f", p=128)[:, :, fc * 128:fc * 128 + fw],
                        waits=w0)

        emit_a2_load(0)
        emit_a2_load(1)
        a2ev = {}
        a2u = 0
        for fc in range(FC):
            fw = FCW[fc]
            for tch in range(2):
                t0 = tch * 1024
                plane_ev = []
                for i in range(2):
                    slot = i + 2 * (fc % 2)
                    rot = ps[a2u % 3]
                    t_mm = None
                    for dc in range(DC):
                        for nq in range(2):
                            t_mm = op("te", lambda e, rot=rot, fw=fw, slot=slot, dc=dc, nq=nq, t0=t0:
                                      e.matmul(rot[0:fw, nq * 512:(nq + 1) * 512],
                                               abuf[slot][:, dc * fw:(dc + 1) * fw],
                                               xnT[:, dc * T + t0 + nq * 512:dc * T + t0 + (nq + 1) * 512],
                                               start=(dc == 0), stop=(dc == DC - 1)),
                                      waits=[tok[("a2w", fc, i)], a2ev.get(a2u - 3)] +
                                            (t_a1_last_ev if a2u < 4 else [])
                                      if dc == 0 and nq == 0 else [],
                                      inc="te" if (dc == DC - 1 and nq == 1) else None)
                    if tch == 1:
                        tok[("a2mmdone", fc, i)] = t_mm
                        if i == 1 and fc + 2 < FC:
                            emit_a2_load(fc + 2)
                    prev_g = tok.get(("gdone", fc, tch - 1)) if tch == 1 else \
                        tok.get(("gdone", fc - 1, 1))
                    t_ev = op("ac", lambda e, i=i, rot=rot, fw=fw: e.copy(
                        fqt[i][0:fw, :], rot[0:fw, :]),
                        waits=[t_mm, prev_g if i == 0 else None], inc="ac")
                    plane_ev.append(t_ev)
                    a2ev[a2u] = t_ev
                    a2u += 1
                t_s1 = op("ac", lambda e, fw=fw: e.activation(a2scr[0][0:fw, :], fqt[0][0:fw, :], AF.Square),
                          waits=[plane_ev[0], tok.get(("gdone", fc, tch - 1)),
                                 tok.get(("gdone", fc - 1, 1))], inc="ac")
                t_s2 = op("ac", lambda e, fw=fw: e.activation(a2scr[1][0:fw, :], fqt[1][0:fw, :], AF.Square),
                          waits=[plane_ev[1], t_s1], inc="ac")
                t_ss = op("ve", lambda e, fw=fw: e.tensor_add(a2scr[0][0:fw, :], a2scr[0][0:fw, :], a2scr[1][0:fw, :]),
                          waits=[t_s2], inc="ve")
                t_l1 = op("ac", lambda e, fw=fw: e.activation(a2scr[0][0:fw, :], a2scr[0][0:fw, :], AF.Ln),
                          waits=[t_ss], inc="ac")
                t_e1 = op("ac", lambda e, fw=fw: e.activation(a2scr[0][0:fw, :], a2scr[0][0:fw, :], AF.Exp, scale=0.5),
                          waits=[t_l1], inc="ac")
                t_l2 = op("ac", lambda e, fw=fw: e.activation(a2scr[0][0:fw, :], a2scr[0][0:fw, :], AF.Ln, bias=cst[0:fw, 1:2]),
                          waits=[t_e1], inc="ac")
                t_e2 = op("ac", lambda e, fw=fw: e.activation(a2scr[0][0:fw, :], a2scr[0][0:fw, :], AF.Exp, scale=-1.0),
                          waits=[t_l2], inc="ac")
                t_n1 = op("ve", lambda e, fw=fw: e.tensor_mul(fqt[2][0:fw, :], fqt[0][0:fw, :], a2scr[0][0:fw, :]),
                          waits=[t_e2], inc="ve")
                t_n2 = op("ve", lambda e, fw=fw: e.tensor_mul(fqt[3][0:fw, :], fqt[1][0:fw, :], a2scr[0][0:fw, :]),
                          waits=[t_n1], inc="ve")
                jre, jim = fc, 5 + fc
                t_a = op("ve", lambda e, fw=fw, fc=fc, t0=t0, jre=jre: e.tensor_scalar_add(
                    fqt[0][0:fw, :], cre[fc][0:fw, t0:t0 + 1024], Ssb[0:fw, jre:jre + 1]),
                    waits=[t_n2, t_S], inc="ve")
                t_b = op("ve", lambda e, fw=fw, fc=fc, t0=t0, jim=jim: e.tensor_scalar_add(
                    fqt[1][0:fw, :], cim[fc][0:fw, t0:t0 + 1024], Ssb[0:fw, jim:jim + 1]),
                    waits=[t_a], inc="ve")
                t_u1 = op("ve", lambda e, fw=fw: e.tensor_mul(a2scr[0][0:fw, :], fqt[0][0:fw, :], fqt[2][0:fw, :]),
                          waits=[t_b], inc="ve")
                t_u2 = op("ve", lambda e, fw=fw: e.tensor_mul(a2scr[1][0:fw, :], fqt[1][0:fw, :], fqt[3][0:fw, :]),
                          waits=[t_u1], inc="ve")
                t_u3 = op("ve", lambda e, fw=fw: e.tensor_add(a2scr[0][0:fw, :], a2scr[0][0:fw, :], a2scr[1][0:fw, :]),
                          waits=[t_u2], inc="ve")
                t_g1 = op("ve", lambda e, fw=fw, fc=fc, t0=t0: e.tensor_copy(
                    gre[fc][0:fw, t0:t0 + 1024], a2scr[0][0:fw, :]), waits=[t_u3], inc="ve")
                t_v1 = op("ve", lambda e, fw=fw: e.tensor_mul(a2scr[1][0:fw, :], fqt[1][0:fw, :], fqt[2][0:fw, :]),
                          waits=[t_g1], inc="ve")
                t_v2 = op("ve", lambda e, fw=fw: e.tensor_mul(a2scr[0][0:fw, :], fqt[0][0:fw, :], fqt[3][0:fw, :]),
                          waits=[t_v1], inc="ve")
                t_v3 = op("ve", lambda e, fw=fw: e.tensor_sub(a2scr[1][0:fw, :], a2scr[1][0:fw, :], a2scr[0][0:fw, :]),
                          waits=[t_v2], inc="ve")
                t_g2 = op("ve", lambda e, fw=fw, fc=fc, t0=t0: e.tensor_copy(
                    gim[fc][0:fw, t0:t0 + 1024], a2scr[1][0:fw, :]), waits=[t_v3], inc="ve")
                tok[("gdone", fc, tch)] = t_g2
                if fc == 0 and tch == 0:
                    op("te", lambda e: e.transpose(ps[3][0:10, 0:128], Ttot[:, 0:10], ident[:]),
                       waits=[t_tt, tok[("smdone", TT - 1)]], inc="sxd1")
                if fc == 0 and tch == 1:
                    t_cin = dma("qs", "s_exq",
                                cc_in.ap().rearrange("o (j f) -> (o j) f", j=10), Trow[:],
                                waits=[t_trw])
                    if _os.environ.get("K_NOCC") == "1":
                        t_cc = None
                        for _r in range(NCORES):
                            t_cc = op("gp", lambda e, _r=_r: e.dma_start(
                                out=cc_out.ap()[_r:_r + 1, :], in_=cc_in.ap()),
                                waits=[t_cin] if _r == 0 else [], inc="cc", by=16)
                    else:
                        t_cc = op("gp", lambda e: e.collective_compute(
                            "AllGather", ALU.bypass, replica_groups=[list(range(NCORES))],
                            ins=[cc_in.ap().opt()], outs=[cc_out.ap().opt()]),
                            waits=[t_cin], inc="cc", by=1)
                    t_gth = dma("qs", "s_exq", gath[:], cc_out.ap(),
                                waits=[t_cc, tok[("lnxdone", TT - 1)]])
                    for j in range(10):
                        op("te", lambda e, j=j: e.matmul(
                            ps[3][:, 512 + j:513 + j], gath[0:NCORES, j * 128:(j + 1) * 128],
                            maskv[:], start=True, stop=True),
                            waits=[t_gth, t_maskv, t_trw] if j == 0 else [],
                            inc="sxd2" if j == 9 else None)
        t_g_all = tok[("gdone", FC - 1, 1)]

        dma("qg", "s_if", ifre_sb.ap().rearrange("p (fc d) -> p fc d", fc=FC),
            ifre_d.rearrange("(fc p) d -> p fc d", p=128)[:, 0:FC, :],
            waits=[t_g_all])
        t_ifim = dma("qg", "s_if", ifim_sb.ap().rearrange("p (fc d) -> p fc d", fc=FC),
                     ifim_d.rearrange("(fc p) d -> p fc d", p=128)[:, 0:FC, :])
        t_ifre = t_ifim


    if phases >= 6:
        xstg4 = xstg + [xsq, xnf]
        for ttl in range(TT):
            s = ttl % 4
            t_x = dma("qs", ("sxa", "sxb", "sxc", "sxd")[s], xstg4[s][:],
                      x_d[ttl * 128:(ttl + 1) * 128, :],
                      waits=[t_g_all if ttl < 4 else None, tok.get(("irfr1", ttl - 4))])
            rot = ps[ttl % 4]
            t_mm = None
            first = True
            for nd in range(2):
                for fc in range(FC):
                    fw = FCW[fc]
                    for pl in range(2):
                        gsb = gre[fc] if pl == 0 else gim[fc]
                        isb = ifre_sb if pl == 0 else ifim_sb
                        t_mm = op("te", lambda e, rot=rot, nd=nd, fc=fc, fw=fw, gsb=gsb, isb=isb, ttl=ttl, pl=pl:
                                  e.matmul(rot[:, nd * 512:(nd + 1) * 512],
                                           gsb[0:fw, ttl * 128:(ttl + 1) * 128],
                                           isb[0:fw, fc * 1024 + nd * 512:fc * 1024 + (nd + 1) * 512],
                                           start=(fc == 0 and pl == 0), stop=(fc == FC - 1 and pl == 1)),
                                  waits=[t_g_all, t_ifim, tok.get(("irfev", ttl - 2)),
                                         a2ev.get(a2u - 1), a2ev.get(a2u - 2)]
                                  if first else [],
                                  inc="te" if (nd == 1 and fc == FC - 1 and pl == 1) else None)
                        first = False
            t_xn2 = op("ve", lambda e, s=s, ttl=ttl: e.tensor_scalar(
                a2scr[0][:], xstg4[s][:], statrs[:, ttl:ttl + 1], statpr[:, ttl:ttl + 1],
                ALU.mult, ALU.add),
                waits=[t_x, tok[("pr", ttl)], tok.get(("irfr1", ttl - 1))], inc="ve")
            t_r1 = op("ve", lambda e, s=s, ttl=ttl: e.tensor_add(
                outp[ttl][:], xstg4[s][:], a2scr[0][:]), waits=[t_xn2], inc="ve")
            tok[("irfr1", ttl)] = t_r1
            t_r2 = op("ve", lambda e, rot=rot, ttl=ttl: e.tensor_add(
                outp[ttl][:], outp[ttl][:], rot[:]), waits=[t_r1, t_mm], inc="ve")
            tok[("irfev", ttl)] = t_r2
        t_irf_all = tok[("irfev", TT - 1)]


    if phases >= 7:
        def w1_dram(e):
            return rw1_d[e] if e < NE else sw1_d

        def w2_dram(e):
            return rw2_d[e] if e < NE else sw2_d

        NU = 2 * NEXP
        h1cnt = 0
        ucnt = 0
        ffn_mm = {}
        gel_by_cnt = {}
        uev_by_cnt = {}
        umm_by = {}
        w1tok = {}
        w2tok = {}

        def emit_ffn_load(u):
            e, half = divmod(u, 2)
            s = u % 2
            war = [ffn_mm.get(u - 2), t_irf_all if u < 2 else None]
            w1tok[(u, 0)] = dma(
                "qg", ("sw10", "sw11")[s],
                rw1hb[s].ap().rearrange("p (dc h) -> p dc h", dc=DC)[:, :, 0:512],
                w1_dram(e).rearrange("(dc p) h -> p dc h", p=128)[:, :, half * 1024:half * 1024 + 512],
                waits=war)
            w1tok[(u, 1)] = dma(
                "qg", ("sw10", "sw11")[s],
                rw1hb[s].ap().rearrange("p (dc h) -> p dc h", dc=DC)[:, :, 512:1024],
                w1_dram(e).rearrange("(dc p) h -> p dc h", p=128)[:, :, half * 1024 + 512:half * 1024 + 1024])
            w2tok[u] = dma(
                "qg", ("sw20", "sw21")[s],
                rw2hb[s].ap().rearrange("p (hc d) -> p hc d", hc=DC),
                w2_dram(e).rearrange("(hc p) d -> p hc d", p=128)[:, half * 8:(half + 1) * 8, :],
                waits=war)

        emit_ffn_load(0)
        emit_ffn_load(1)
        for u in range(NU):
            e, half = divmod(u, 2)
            s = u % 2
            last_e8 = (e == NEXP - 1 and half == 1)
            t_mm = None
            for step in range(5):
                if step < 4:
                    tch4 = step
                    g = u * 4 + tch4
                    b = tch4 % 2
                    t0 = tch4 * 512
                    for hc8 in range(8):
                        hcg = half * 8 + hc8
                        pa = (ps[0], ps[1])[(h1cnt // 2) % 2]
                        c0 = (h1cnt % 2) * 512
                        t_hmm = None
                        for dc in range(DC):
                            w0 = []
                            if dc == 0:
                                w0 = [w1tok[(u, hc8 // 4)],
                                      gel_by_cnt.get(h1cnt - 4),
                                      t_irf_all if u == 0 and tch4 == 0 else None]
                            t_hmm = op("te", lambda e_, pa=pa, c0=c0, s=s, dc=dc, hc8=hc8, t0=t0:
                                       e_.matmul(pa[:, c0:c0 + 512],
                                                 rw1hb[s][:, dc * 1024 + hc8 * 128:dc * 1024 + (hc8 + 1) * 128],
                                                 xnT[:, dc * T + t0:dc * T + t0 + 512],
                                                 start=(dc == 0), stop=(dc == DC - 1)),
                                       waits=w0, inc="te" if dc == DC - 1 else None)
                        if e < NE:
                            bias_ap = (lambda hcg=hcg, e=e: rb1T[:, hcg * NE + e:hcg * NE + e + 1])
                        else:
                            bias_ap = (lambda hcg=hcg: sb1T[:, hcg:hcg + 1])
                        t_g = op("ac", lambda e_, pa=pa, c0=c0, b=b, hc8=hc8, bias_ap=bias_ap:
                                 e_.activation(hwb[b][:, hc8 * 512:(hc8 + 1) * 512],
                                               pa[:, c0:c0 + 512], AF.Gelu, bias=bias_ap()),
                                 waits=[t_hmm, t_bias_tr, tok[("rb1ev", HC - 1)],
                                        umm_by.get(g - 2) if hc8 == 0 else None], inc="ac")
                        gel_by_cnt[h1cnt] = t_g
                        h1cnt += 1
                    tok[("ffn_gel", g)] = t_g
                if step > 0:
                    tc = step - 1
                    g2 = u * 4 + tc
                    b2 = tc % 2
                    for tl in range(4):
                        ttl = tc * 4 + tl
                        pb = ps[2 + ucnt % 2]
                        t_mm = None
                        for hc8 in range(8):
                            for nd in range(2):
                                w0 = []
                                if hc8 == 0 and nd == 0:
                                    w0 = [w2tok[u], tok[("ffn_gel", g2)],
                                          uev_by_cnt.get(ucnt - 2)]
                                t_mm = op("te", lambda e_, pb=pb, b2=b2, s=s, hc8=hc8, nd=nd, tl=tl:
                                          e_.matmul(pb[:, nd * 512:(nd + 1) * 512],
                                                    hwb[b2][:, hc8 * 512 + tl * 128:hc8 * 512 + (tl + 1) * 128],
                                                    rw2hb[s][:, hc8 * 1024 + nd * 512:hc8 * 1024 + (nd + 1) * 512],
                                                    start=(hc8 == 0),
                                                    stop=(hc8 == 7 and not last_e8)),
                                          waits=w0,
                                          inc="te" if (hc8 == 7 and nd == 1 and not last_e8) else None)
                        if last_e8:
                            op("te", lambda e_, pb=pb, ttl=ttl: e_.matmul(
                                pb[:, 0:512], wT[:, ttl * 128:(ttl + 1) * 128],
                                rb2sb[:, 0:512], start=False, stop=False),
                                waits=[tok[("wT", ttl)], t_rb2], inc=None)
                            op("te", lambda e_, pb=pb, ttl=ttl: e_.matmul(
                                pb[:, 512:1024], wT[:, ttl * 128:(ttl + 1) * 128],
                                rb2sb[:, 512:1024], start=False, stop=False), inc=None)
                            op("te", lambda e_, pb=pb: e_.matmul(
                                pb[:, 0:512], onesb[:], sb2sb[:, 0:512],
                                start=False, stop=False), waits=[t_onesb, t_sb2], inc=None)
                            t_mm = op("te", lambda e_, pb=pb: e_.matmul(
                                pb[:, 512:1024], onesb[:], sb2sb[:, 512:1024],
                                start=False, stop=True), inc="te")
                        if e < NE:
                            t_ev = op("ve", lambda e_, pb=pb, ttl=ttl, e=e: e_.scalar_tensor_tensor(
                                outp[ttl][:], pb[:], wtok[:, ttl * NE + e:ttl * NE + e + 1],
                                outp[ttl][:], ALU.mult, ALU.add),
                                waits=[t_mm, tok[("w", ttl)] if u == 0 else None,
                                       tok[("irfev", ttl)] if u == 0 else None], inc="ve")
                        else:
                            t_ev = op("ve", lambda e_, pb=pb, ttl=ttl: e_.tensor_add(
                                outp[ttl][:], outp[ttl][:], pb[:]),
                                waits=[t_mm], inc="ve")
                        uev_by_cnt[ucnt] = t_ev
                        ucnt += 1
                        if last_e8:
                            sdma(y_d[ttl * 128:(ttl + 1) * 128, :], outp[ttl][:], waits=[t_ev])
                    umm_by[g2] = t_mm
            ffn_mm[u] = t_mm
            if u + 2 < NU:
                emit_ffn_load(u + 2)



    if debug:
        dbg_c = nc.dram_tensor("dbg_c", [2 * FC, 128, T], F32, kind="ExternalOutput").ap()
        dbg_w = nc.dram_tensor("dbg_w", [128, TT * NE], F32, kind="ExternalOutput").ap()
        dbg_stat = nc.dram_tensor("dbg_stat", [128, 3 * TT], F32, kind="ExternalOutput").ap()
        dbg_S = nc.dram_tensor("dbg_S", [128, 10], F32, kind="ExternalOutput").ap()
        if K_SUB == 1:
            sdma(dbg_w, rb1T[:], waits=[tok[("rb1ev", HC - 1)]])
        if phases >= 1 and K_SUB != 1:
            sdma(dbg_w, wtok[:], waits=[tok[("w", TT - 1)]])
            sdma(dbg_stat[:, 0:TT], statnm[:], waits=[tok[("pr", TT - 1)]])
            sdma(dbg_stat[:, TT:2 * TT], statrs[:], waits=[tok[("pr", TT - 1)]])
            sdma(dbg_stat[:, 2 * TT:3 * TT], statpr[:], waits=[tok[("pr", TT - 1)]])
        if phases >= 3 and phases < 6:
            for fc in range(FC):
                fw = FCW[fc]
                sdma(dbg_c[fc, 0:fw, :], cre[fc][0:fw, :], waits=[tok[("a1c", fc, 1, 0)]])
                sdma(dbg_c[FC + fc, 0:fw, :], cim[fc][0:fw, :], waits=[tok[("a1c", fc, 1, 1)]])
        if phases >= 4:
            sdma(dbg_S, Ssb[:], waits=[t_S])
        if phases >= 5:
            dbg_g = nc.dram_tensor("dbg_g", [2 * FC, 128, T], F32, kind="ExternalOutput").ap()
            for fc in range(FC):
                fw = FCW[fc]
                dma("qg", "s_dbg", dbg_g[fc, 0:fw, :], gre[fc][0:fw, :],
                    waits=[tok[("gdone", FC - 1, 1)]])
                dma("qg", "s_dbg", dbg_g[FC + fc, 0:fw, :], gim[fc][0:fw, :],
                    waits=[tok[("gdone", FC - 1, 1)]])
        if phases >= 6 and phases < 7:
            for ttl in range(TT):
                sdma(y_d[ttl * 128:(ttl + 1) * 128, :], outp[ttl][:],
                     waits=[tok[("irfev", ttl)]])
    op("sy", lambda e: e.nop(), waits=[("st", cnt["st"])])

    import contextlib
    with contextlib.ExitStack() as ctx:
        sems = {s: ctx.enter_context(nc.semaphore(s)) for s in sem_names}
        block = ctx.enter_context(nc.Block())

        def emit(stream_key):
            def body(eng):
                seen = {s: 0 for s in sem_names}
                for fn, waits, inc, by in streams[stream_key]:
                    for (s, v) in waits:
                        if v > seen[s]:
                            eng.wait_ge(sems[s], v)
                            seen[s] = v
                    ins = fn(eng)
                    if inc is not None:
                        ins.then_inc(sems[inc], by)
            return body

        block.tensor(emit("te"))
        block.vector(emit("ve"))
        block.scalar(emit("ac"))
        block.gpsimd(emit("gp"))
        block.sync(emit("sy"))

    return nc


_PROG_CACHE = {}


def _consts():
    e = np.arange(D, dtype=np.float64)
    f = np.arange(NF, dtype=np.float64)
    ang = 2.0 * np.pi * np.outer(e, f) / D
    fre = np.cos(ang)
    fim = -np.sin(ang)
    fim[:, 0] = 0.0
    fim[:, NF - 1] = 0.0
    dd = np.arange(D, dtype=np.float64)
    angi = 2.0 * np.pi * np.outer(f, dd) / D
    cf = np.full(NF, 2.0)
    cf[0] = 1.0
    cf[NF - 1] = 1.0
    ifre = (cf[:, None] / D) * np.cos(angi)
    ifim = -(cf[:, None] / D) * np.sin(angi)
    ifre_p = np.zeros((640, D))
    ifim_p = np.zeros((640, D))
    ifre_p[:NF] = ifre
    ifim_p[:NF] = ifim
    bf = ml_dtypes.bfloat16
    return (fre.astype(bf), fim.astype(bf), ifre_p.astype(bf), ifim_p.astype(bf))


def kernel(L0_attn_x, L0_attn_Wq, L0_attn_Wk, L0_attn_Wv,
           L0_ffn_router_w, L0_ffn_router_b, L0_ffn_sw1, L0_ffn_sb1,
           L0_ffn_sw2, L0_ffn_sb2, L0_ffn_rw1, L0_ffn_rb1,
           L0_ffn_rw2, L0_ffn_rb2):
    if "nc" not in _PROG_CACHE:
        _PROG_CACHE["nc"] = build_program()
    nc = _PROG_CACHE["nc"]

    fre, fim, ifre, ifim = _consts()
    ident = np.eye(128, dtype=np.float32)
    onesf = np.ones((1, 128), dtype=np.float32)
    onesb = np.ones((1, 128), dtype=ml_dtypes.bfloat16)

    x = np.ascontiguousarray(np.asarray(L0_attn_x, dtype=np.float32).reshape(NCORES * T, D))
    common = dict(
        Wq=np.asarray(L0_attn_Wq, np.float32), Wk=np.asarray(L0_attn_Wk, np.float32),
        Wv=np.asarray(L0_attn_Wv, np.float32),
        router_w=np.asarray(L0_ffn_router_w, np.float32),
        router_b=np.asarray(L0_ffn_router_b, np.float32),
        sw1=np.asarray(L0_ffn_sw1, np.float32), sb1=np.asarray(L0_ffn_sb1, np.float32),
        sw2=np.asarray(L0_ffn_sw2, np.float32), sb2=np.asarray(L0_ffn_sb2, np.float32),
        rw1=np.asarray(L0_ffn_rw1, np.float32), rb1=np.asarray(L0_ffn_rb1, np.float32),
        rw2=np.asarray(L0_ffn_rw2, np.float32), rb2=np.asarray(L0_ffn_rb2, np.float32),
        dft_re=fre, dft_im=fim, idft_re=ifre, idft_im=ifim,
        ident=ident, onesf=onesf, onesb=onesb,
    )
    in_maps = []
    for c in range(NCORES):
        mask = ((np.arange(NCORES) < c) & (np.arange(NCORES) // 4 == c // 4)).astype(np.float32)
        m = dict(common)
        m["x"] = x[c * T:(c + 1) * T]
        m["maskv"] = mask
        in_maps.append(m)
    res = run_bass_kernel_spmd(nc, in_maps, core_ids=list(range(NCORES)))
    out = np.concatenate([np.asarray(res.results[c]["y"], np.float32) for c in range(NCORES)], axis=0)
    return out.reshape(2, 8192, D)


# version 13
# speedup vs baseline: 1.2149x; 1.0500x over previous
import numpy as np
import ml_dtypes
import concourse.bass as bass
import concourse.mybir as mybir
from concourse.bass_utils import run_bass_kernel_spmd

F32 = mybir.dt.float32
BF16 = mybir.dt.bfloat16
AF = mybir.ActivationFunctionType
ALU = mybir.AluOpType
AX = mybir.AxisListType.X

NCORES = 8
D = 1024
HID = 2048
NE = 8
NEXP = NE + 1
T = 2048
NF = 513
DC = D // 128
HC = HID // 128
TT = T // 128
FC = 5
FCW = [128, 128, 128, 128, 1]
EPS_FFT = 1e-8
EPS_LN = 1e-5
NEG_BIG = -1e9

XNT_O = 36832
R_O = XNT_O + 32768
G_O = R_O + 81920
IFR_O = G_O + 40960
WB_O = IFR_O + 20480
END_O = WB_O + 16384
assert END_O <= 229344, END_O

OUTP_O = [R_O + i * 4096 for i in range(16)]
XSTG_O = [R_O + 16 * 4096, R_O + 17 * 4096]
XSQ_O = R_O + 18 * 4096
XNF_O = R_O + 19 * 4096
XNT32_O = R_O + 4 * 4096
RB1RAW_O = R_O + 5 * 4096
SB1RAW_O = R_O + 7 * 4096
GATH_O = G_O
RW2HI_O = [R_O + (16 + i) * 4096 for i in range(3)]

A1T_O = [G_O + i * 4096 for i in range(8)]
ZERO_O = G_O + 8 * 4096
FRE_O = G_O
FIM_O = G_O + 8224
ATS_O = G_O + 16448


def g_slot(fc):
    return G_O + fc * 8192


def hw_slot(hc):
    return G_O + hc * 2048


FQT_O = [IFR_O + i * 4096 for i in range(4)]
IFRE_O = IFR_O
IFIM_O = IFR_O + 10240

WSB_O = WB_O


def abuf_slot(i):
    return WB_O + i * 2048


A2SCR_O = [WB_O + 8192, WB_O + 12288]


def rw1_slot(i):
    return WB_O + i * 2048


def rw2_slot(hc):
    if hc < 10:
        return IFR_O + hc * 2048
    return RW2HI_O[(hc - 10) // 2] + ((hc - 10) % 2) * 2048


def build_program(phases=7, debug=False):
    import os as _os
    K_SUB = int(_os.environ.get("K_SUB", "0"))
    K_TTLIM = int(_os.environ.get("K_TTLIM", "16"))
    nc = bass.Bass("TRN2", target_bir_lowering=False, num_devices=NCORES)

    x_d = nc.dram_tensor("x", [T, D], F32, kind="ExternalInput").ap()
    wq_d = nc.dram_tensor("Wq", [D, D], F32, kind="ExternalInput").ap()
    wk_d = nc.dram_tensor("Wk", [D, D], F32, kind="ExternalInput").ap()
    wv_d = nc.dram_tensor("Wv", [D, D], F32, kind="ExternalInput").ap()
    rww_d = nc.dram_tensor("router_w", [D, NE], F32, kind="ExternalInput").ap()
    rwb_d = nc.dram_tensor("router_b", [NE], F32, kind="ExternalInput").ap()
    sw1_d = nc.dram_tensor("sw1", [D, HID], F32, kind="ExternalInput").ap()
    sb1_d = nc.dram_tensor("sb1", [HID], F32, kind="ExternalInput").ap()
    sw2_d = nc.dram_tensor("sw2", [HID, D], F32, kind="ExternalInput").ap()
    sb2_d = nc.dram_tensor("sb2", [D], F32, kind="ExternalInput").ap()
    rw1_d = nc.dram_tensor("rw1", [NE, D, HID], F32, kind="ExternalInput").ap()
    rb1_d = nc.dram_tensor("rb1", [NE, HID], F32, kind="ExternalInput").ap()
    rw2_d = nc.dram_tensor("rw2", [NE, HID, D], F32, kind="ExternalInput").ap()
    rb2_d = nc.dram_tensor("rb2", [NE, D], F32, kind="ExternalInput").ap()
    fre_d = nc.dram_tensor("dft_re", [D, NF], BF16, kind="ExternalInput").ap()
    fim_d = nc.dram_tensor("dft_im", [D, NF], BF16, kind="ExternalInput").ap()
    ifre_d = nc.dram_tensor("idft_re", [640, D], BF16, kind="ExternalInput").ap()
    ifim_d = nc.dram_tensor("idft_im", [640, D], BF16, kind="ExternalInput").ap()
    ident_d = nc.dram_tensor("ident", [128, 128], F32, kind="ExternalInput").ap()
    onesf_d = nc.dram_tensor("onesf", [1, 128], F32, kind="ExternalInput").ap()
    onesb_d = nc.dram_tensor("onesb", [1, 128], BF16, kind="ExternalInput").ap()
    maskv_d = nc.dram_tensor("maskv", [NCORES], F32, kind="ExternalInput").ap()
    y_d = nc.dram_tensor("y", [T, D], F32, kind="ExternalOutput").ap()

    a_dram = nc.dram_tensor("a_scr", [6, D, NF], BF16, kind="Internal")
    a_nyq = nc.dram_tensor("a_nyq", [6, D], BF16, kind="Internal")
    cc_in = nc.dram_tensor("cc_in", [1, 1280], F32, kind="Internal")
    cc_out = nc.dram_tensor("cc_out", [NCORES, 1280], F32, kind="Internal")

    def at(name, shape, dtype, off):
        import os as _os
        if _os.environ.get("KAUTO") == "1":
            return nc.alloc_sbuf_tensor(name, shape, dtype)
        return nc.alloc_sbuf_tensor_at(name, shape, dtype, offset=off)

    xnT = at("xnT_s", [128, DC * T], BF16, XNT_O)
    xstg = [at(f"xstg{i}", [128, 1024], F32, XSTG_O[i]) for i in range(2)]
    xsq = at("xsq", [128, 1024], F32, XSQ_O)
    xnf = at("xnf", [128, 1024], F32, XNF_O)
    xnT32 = at("xnT32", [128, DC * 128], F32, XNT32_O)
    rb1raw = at("rb1raw", [NE, HID], F32, RB1RAW_O)
    sb1raw = at("sb1raw", [1, HID], F32, SB1RAW_O)
    gath = at("gath", [NCORES, 1280], F32, GATH_O)
    outp = [at(f"outp{i}", [128, 1024], F32, OUTP_O[i]) for i in range(16)]
    cre = [at(f"cre{fc}", [128, T], F32, R_O + fc * 16384) for fc in range(FC)]
    cim = [at(f"cim{fc}", [128, T], F32, R_O + fc * 16384 + 8192) for fc in range(FC)]
    a1t = [at(f"a1t{i}", [128, 1024], F32, A1T_O[i]) for i in range(8)]
    zeros = at("zeros", [128, 1024], F32, ZERO_O)
    fre_sb = at("fre_sb", [128, 8 * NF], BF16, FRE_O)
    fim_sb = at("fim_sb", [128, 8 * NF], BF16, FIM_O)
    ats = [at("ats", [128, NF], BF16, ATS_O),
           at("ats2", [128, NF], BF16, ATS_O + 1056)]
    gre = [at(f"gre{fc}", [128, T], BF16, g_slot(fc)) for fc in range(FC)]
    gim = [at(f"gim{fc}", [128, T], BF16, g_slot(fc) + 4096) for fc in range(FC)]
    hw = [at(f"hw{hc}", [128, 1024], BF16, hw_slot(hc)) for hc in range(HC)]
    fqt = [at(f"fqt{i}", [128, 1024], F32, FQT_O[i]) for i in range(4)]
    ifre_sb = at("ifre_sb", [128, FC * 1024], BF16, IFRE_O)
    ifim_sb = at("ifim_sb", [128, FC * 1024], BF16, IFIM_O)
    wsb_g = [at("wsbg0", [128, 8 * 1024], BF16, IFR_O),
             at("wsbg1", [128, 8 * 1024], BF16, WSB_O)]
    abuf = [at(f"abuf{i}", [128, 8 * 128], BF16, abuf_slot(i)) for i in range(8)]
    a2scr = [at(f"a2scr{i}", [128, 1024], F32, A2SCR_O[i]) for i in range(2)]
    rw1hb = [at("rw1h0", [128, 8 * 1024], BF16, G_O),
             at("rw1h1", [128, 8 * 1024], BF16, IFR_O)]
    rw2hb = [at("rw2h0", [128, 8 * 1024], BF16, G_O + 16384),
             at("rw2h1", [128, 8 * 1024], BF16, IFR_O + 16384)]
    hwb = [at("hwb0", [128, 8 * 512], BF16, R_O + 16 * 4096),
           at("hwb1", [128, 8 * 512], BF16, R_O + 18 * 4096)]

    ident = nc.alloc_sbuf_tensor("ident_sb", [128, 128], F32)
    onesf = nc.alloc_sbuf_tensor("onesf_sb", [1, 128], F32)
    onesb = nc.alloc_sbuf_tensor("onesb_sb", [1, 128], BF16)
    statnm = nc.alloc_sbuf_tensor("statnm", [128, TT], F32)
    statrs = nc.alloc_sbuf_tensor("statrs", [128, TT], F32)
    statpr = nc.alloc_sbuf_tensor("statpr", [128, TT], F32)
    wtok = nc.alloc_sbuf_tensor("wtok_sb", [128, TT * NE], F32)
    wT = nc.alloc_sbuf_tensor("wT_sb", [NE, T], BF16)
    rb1T = nc.alloc_sbuf_tensor("rb1T_sb", [128, HC * NE], F32)
    sb1T = nc.alloc_sbuf_tensor("sb1T_sb", [128, HC], F32)
    rb2sb = nc.alloc_sbuf_tensor("rb2_sb", [NE, D], BF16)
    sb2sb = nc.alloc_sbuf_tensor("sb2_sb", [1, D], BF16)
    rwwsb = nc.alloc_sbuf_tensor("rww_sb", [128, DC * NE], F32)
    rwbsb = nc.alloc_sbuf_tensor("rwb_sb", [1, NE], F32)
    Ssb = nc.alloc_sbuf_tensor("S_sb", [128, 10], F32)
    Ttot = nc.alloc_sbuf_tensor("Ttot_sb", [128, 10], F32)
    Trow = nc.alloc_sbuf_tensor("Trow_sb", [10, 128], F32)
    maskv = nc.alloc_sbuf_tensor("maskv_sb", [NCORES, 1], F32)
    sme = nc.alloc_sbuf_tensor("sme", [128, NE], F32)
    smr = nc.alloc_sbuf_tensor("smr", [128, NE], F32)
    smp = nc.alloc_sbuf_tensor("smp", [128, NE], F32)
    smq = nc.alloc_sbuf_tensor("smq", [128, NE], F32)
    sms = nc.alloc_sbuf_tensor("sms", [128, 8], F32)
    cst = nc.alloc_sbuf_tensor("cst", [128, 2], F32)

    ps = [nc.alloc_psum_tensor(f"ps{i}", [128, 1024], F32) for i in range(4)]

    sem_names = (["te", "ve", "ac", "gp", "st", "cc", "s_set", "s_setg", "s_wab", "s_ast", "s_ast1", "s_if", "s_dbg", "sxa", "sxb", "sxc", "sxd", "sxd1", "sxd2", "sw10", "sw11", "sw20", "sw21", "s_rw2", "s_exq"] + [f"sab{i}" for i in range(8)])
    cnt = {s: 0 for s in sem_names}
    streams = {k: [] for k in ["te", "ve", "ac", "gp", "sy"]}

    def op(stream, fn, waits=(), inc=None, by=1):
        w = [t for t in waits if t is not None]
        streams[stream].append((fn, w, inc, by))
        if inc is None:
            return None
        cnt[inc] += by
        return (inc, cnt[inc])

    def dma(queue, sem, out, in_, waits=()):
        stream = {"qs": "sy", "qg": "gp", "qa": "ac"}[queue]
        return op(stream, lambda e, o=out, i=in_: e.dma_start(out=o, in_=i),
                  waits, inc=sem, by=16)

    def sdma(out, in_, waits=()):
        return op("sy", lambda e, o=out, i=in_: e.dma_start(out=o, in_=i),
                  waits, inc="st", by=16)

    tok = {}

    dma("qs", "s_set", ident[:], ident_d)
    dma("qs", "s_set", onesf[:], onesf_d)
    dma("qg", "s_setg", onesb[:], onesb_d)
    dma("qs", "s_set", rwwsb.ap().rearrange("p (dc e) -> p dc e", dc=DC),
        rww_d.rearrange("(dc p) e -> p dc e", p=128))
    dma("qs", "s_set", rwbsb[:], rwb_d[None, :])
    dma("qs", "s_set", maskv[:], maskv_d[:, None])
    dma("qs", "s_set", rb1raw[:], rb1_d)
    dma("qs", "s_set", sb1raw[:], sb1_d[None, :])
    dma("qg", "s_setg", rb2sb[:], rb2_d)
    dma("qg", "s_setg", sb2sb[:], sb2_d[None, :])
    dma("qg", "s_setg", fre_sb.ap().rearrange("p (dc f) -> p dc f", dc=8),
        fre_d.rearrange("(dc p) f -> p dc f", p=128))
    dma("qg", "s_setg", fim_sb.ap().rearrange("p (dc f) -> p dc f", dc=8),
        fim_d.rearrange("(dc p) f -> p dc f", p=128))
    t_set = ("s_set", cnt["s_set"])
    t_setg = ("s_setg", cnt["s_setg"])
    t_ident = t_ones = t_rww = t_rwb = t_maskv = t_rb1raw = t_sb1raw = t_set
    t_onesb = t_rb2 = t_sb2 = t_fre = t_fim = t_setg
    t_zero = op("gp", lambda e: e.memset(zeros[:], 0.0), inc="gp")
    op("gp", lambda e: e.memset(cst[:, 0:1], EPS_LN), inc=None)
    t_cst = op("gp", lambda e: e.memset(cst[:, 1:2], EPS_FFT), inc="gp")

    for hc in range(HC):
        tk = op("te", lambda e, hc=hc: e.transpose(
            ps[3][:, (hc % 2) * 512:(hc % 2) * 512 + 8],
            rb1raw[0:NE, hc * 128:(hc + 1) * 128], ident[0:NE, 0:NE]),
            waits=[t_rb1raw, t_ident, tok.get(("rb1ev", hc - 2))], inc="te")
        tok[("rb1ev", hc)] = op("ve", lambda e, hc=hc: e.tensor_copy(
            rb1T[:, hc * 8:(hc + 1) * 8], ps[3][:, (hc % 2) * 512:(hc % 2) * 512 + 8]),
            waits=[tk], inc="ve")
    for hc in range(HC):
        tk = op("te", lambda e, hc=hc: e.transpose(
            ps[2][:, (hc % 2) * 512:(hc % 2) * 512 + 1],
            sb1raw[0:1, hc * 128:(hc + 1) * 128], ident[0:1, 0:1]),
            waits=[t_sb1raw, t_ident, tok.get(("sb1ev", hc - 2))], inc="te")
        tok[("sb1ev", hc)] = op("ve", lambda e, hc=hc: e.tensor_copy(
            sb1T[:, hc:hc + 1], ps[2][:, (hc % 2) * 512:(hc % 2) * 512 + 1]),
            waits=[tk], inc="ve")
    t_bias_tr = tok[("sb1ev", HC - 1)]

    ab_ev = {}
    ab_store_mark = {}
    w_drams = [wk_d, wv_d, wq_d]
    dram_plane = [2, 3, 4, 5, 0, 1]
    t_wsb = {}
    if phases >= 2:
        t_wsb[0] = dma("qg", "s_wab", wsb_g[0].ap().rearrange("p (ec d) -> p ec d", ec=8),
                       w_drams[0].rearrange("(ec p) d -> p ec d", p=128))
        t_wsb[1] = dma("qg", "s_wab", wsb_g[1].ap().rearrange("p (ec d) -> p ec d", ec=8),
                       w_drams[1].rearrange("(ec p) d -> p ec d", p=128))

    def emit_ab_unit(unit):
        proj, rem = divmod(unit, 16)
        pl, dc = divmod(rem, 8)
        fsb = fre_sb if pl == 0 else fim_sb
        t_f = t_fre if pl == 0 else t_fim
        dpl = dram_plane[proj * 2 + pl]
        even = (unit % 2 == 0)
        wbuf = wsb_g[proj % 2]
        t_mm = None
        for ec in range(8):
            for k, (n0, n1) in enumerate([(0, 512), (512, 513)]):
                if even:
                    tgt = (lambda e, n0=n0, n1=n1: ps[3][:, n0:n1])
                else:
                    tgt = (lambda e, k=k: ps[2][:, 512:1024] if k == 0 else ps[3][:, 513:514])
                t_mm = op("te", lambda e, tgt=tgt, ec=ec, dc=dc, fsb=fsb, n0=n0, n1=n1, wbuf=wbuf:
                          e.matmul(tgt(e),
                                   wbuf[:, ec * 1024 + dc * 128:ec * 1024 + (dc + 1) * 128],
                                   fsb[:, ec * NF + n0:ec * NF + n1],
                                   start=(ec == 0), stop=(ec == 7)),
                          waits=[t_wsb[proj], t_f, ab_ev.get(unit - 1),
                                 t_bias_tr if unit < 2 else None]
                          if ec == 0 and k == 0 else [],
                          inc="te" if (ec == 7 and k == 1) else None)
        slot = unit % 2
        ssem = ("s_ast", "s_ast1")[slot]
        war = (ssem, 32 * (unit // 2)) if unit >= 2 else None
        if even:
            t_ev = op("ac", lambda e, slot=slot: e.copy(ats[slot][:, 0:513], ps[3][:, 0:513]),
                      waits=[t_mm, war], inc="ac")
        else:
            op("ac", lambda e, slot=slot: e.copy(ats[slot][:, 0:512], ps[2][:, 512:1024]),
               waits=[t_mm, war], inc=None)
            t_ev = op("ac", lambda e, slot=slot: e.copy(ats[slot][:, 512:513], ps[3][:, 513:514]),
                      inc="ac")
        dma("qa", ssem, a_dram.ap()[dpl, dc * 128:(dc + 1) * 128, :], ats[slot][:],
            waits=[t_ev])
        dma("qa", ssem, a_nyq.ap()[dpl, dc * 128:(dc + 1) * 128], ats[slot][:, 512:513])
        ab_ev[unit] = t_ev
        if unit == 15:
            t_wsb[2] = dma("qg", "s_wab", wsb_g[0].ap().rearrange("p (ec d) -> p ec d", ec=8),
                           w_drams[2].rearrange("(ec p) d -> p ec d", p=128),
                           waits=[t_mm])
        if unit == 31:
            ab_store_mark["kv"] = [("s_ast", cnt["s_ast"]), ("s_ast1", cnt["s_ast1"])]
        if unit == 47:
            ab_store_mark["q"] = [("s_ast", cnt["s_ast"]), ("s_ast1", cnt["s_ast1"])]

    AB_IL = int(_os.environ.get("AB_IL", "1"))

    def _ab_tick(ttl):
        if phases < 2 or K_SUB == 1 or not AB_IL:
            return
        if 3 <= ttl <= 14:
            for u_ in range((ttl - 3) * 4, (ttl - 3) * 4 + 4):
                emit_ab_unit(u_)

    for ttl in range(TT if K_SUB != 1 else 0):
        s = ttl % 2
        t_x = dma("qs", ("sxa", "sxb")[s], xstg[s][:], x_d[ttl * 128:(ttl + 1) * 128, :],
                  waits=[tok.get(("lnxdone", ttl - 2))])
        t_sq = op("ac", lambda e, s=s: e.activation(xsq[:], xstg[s][:], AF.Square),
                  waits=[t_x, tok.get(("lnssq", ttl - 1))], inc="ac")
        t_ssq = op("ve", lambda e: e.reduce_sum(sms[:, 0:1], xsq[:], axis=AX),
                   waits=[t_sq], inc="ve")
        tok[("lnssq", ttl)] = t_ssq
        t_xs = op("ve", lambda e, s=s: e.reduce_sum(sms[:, 1:2], xstg[s][:], axis=AX),
                  waits=[t_x, t_ssq], inc="ve")
        t_nm = op("ve", lambda e, ttl=ttl: e.tensor_scalar_mul(
            statnm[:, ttl:ttl + 1], sms[:, 1:2], -1.0 / D), waits=[t_xs], inc="ve")
        t_m2 = op("ve", lambda e, ttl=ttl: e.tensor_mul(
            sms[:, 2:3], statnm[:, ttl:ttl + 1], statnm[:, ttl:ttl + 1]),
            waits=[t_nm], inc="ve")
        t_va = op("ve", lambda e: e.tensor_scalar_mul(sms[:, 3:4], sms[:, 0:1], 1.0 / D),
                  waits=[t_m2], inc="ve")
        t_vb = op("ve", lambda e: e.tensor_sub(sms[:, 4:5], sms[:, 3:4], sms[:, 2:3]),
                  waits=[t_va], inc="ve")
        t_ln = op("ac", lambda e: e.activation(sms[:, 5:6], sms[:, 4:5], AF.Ln, bias=cst[:, 0:1]),
                  waits=[t_vb, t_cst], inc="ac")
        t_rs = op("ac", lambda e, ttl=ttl: e.activation(
            statrs[:, ttl:ttl + 1], sms[:, 5:6], AF.Exp, scale=-0.5),
            waits=[t_ln], inc="ac")
        t_pr = op("ve", lambda e, ttl=ttl: e.tensor_mul(
            statpr[:, ttl:ttl + 1], statnm[:, ttl:ttl + 1], statrs[:, ttl:ttl + 1]),
            waits=[t_rs], inc="ve")
        tok[("pr", ttl)] = t_pr
        t_xn = op("ve", lambda e, s=s, ttl=ttl: e.tensor_scalar(
            xnf[:], xstg[s][:], statnm[:, ttl:ttl + 1], statrs[:, ttl:ttl + 1],
            ALU.add, ALU.mult),
            waits=[t_pr, tok.get(("lntr", ttl - 1))], inc="ve")
        tok[("lnxdone", ttl)] = t_xn
        if K_SUB == 2:
            tok[("lnevac", ttl)] = t_xn
            tok[("lnxnT", ttl)] = t_xn
            tok[("smdone", ttl)] = t_xn
            tok[("wT", ttl)] = t_xn
            tok[("w", ttl)] = t_xn
            tok[("lnrb", ttl)] = None
            tok[("lntr", ttl)] = None
            _ab_tick(ttl)
            continue
        rot = ps[ttl % 2]
        t_tr = None
        for dc in range(DC):
            t_tr = op("te", lambda e, dc=dc, rot=rot: e.transpose(
                rot[:, dc * 128:(dc + 1) * 128], xnf[:, dc * 128:(dc + 1) * 128], ident[:]),
                waits=[t_xn, t_ident, tok.get(("lnevac", ttl - 2)),
                       tok.get(("lnxnT", ttl - 2))] if dc == 0 else [],
                inc="te" if dc == DC - 1 else None)
        tok[("lntr", ttl)] = t_tr
        last_ac = None
        for dc in range(DC):
            last_ac = op("ac", lambda e, dc=dc, rot=rot, ttl=ttl: e.copy(
                xnT[:, dc * T + ttl * 128:dc * T + (ttl + 1) * 128],
                rot[:, dc * 128:(dc + 1) * 128]), waits=[t_tr] if dc == 0 else [],
                inc="ac" if dc == DC - 1 else None)
        last_ev = None
        for dc in range(DC):
            last_ev = op("ve", lambda e, dc=dc, rot=rot: e.tensor_copy(
                xnT32[:, dc * 128:(dc + 1) * 128], rot[:, dc * 128:(dc + 1) * 128]),
                waits=[last_ac, tok.get(("lnrb", ttl - 1))] if dc == 0 else [],
                inc="ve" if dc == DC - 1 else None)
        tok[("lnevac", ttl)] = last_ev
        tok[("lnxnT", ttl)] = last_ac
        if K_SUB == 3:
            tok[("smdone", ttl)] = last_ev
            tok[("wT", ttl)] = last_ev
            tok[("w", ttl)] = last_ev
            tok[("lnrb", ttl)] = t_tr
            _ab_tick(ttl)
            continue
        rps = ps[2]
        rc = (ttl % 2) * 256
        for dc in range(DC):
            op("te", lambda e, dc=dc, rps=rps, rc=rc: e.matmul(
                rps[:, rc:rc + NE], xnT32[:, dc * 128:(dc + 1) * 128],
                rwwsb[:, dc * NE:(dc + 1) * NE], start=(dc == 0), stop=False),
                waits=[last_ev, t_rww, t_bias_tr, tok.get(("smdone", ttl - 1))]
                if dc == 0 else [], inc=None)
        t_rb = op("te", lambda e, rps=rps, rc=rc: e.matmul(
            rps[:, rc:rc + NE], onesf[:], rwbsb[:], start=False, stop=True),
            waits=[t_ones, t_rwb], inc="te")
        tok[("lnrb", ttl)] = t_rb
        if K_SUB == 4:
            t_cp = op("ve", lambda e, rps=rps, rc=rc, ttl=ttl: e.tensor_copy(
                wtok[:, ttl * NE:(ttl + 1) * NE], rps[:, rc:rc + NE]), waits=[t_rb], inc="ve")
            tok[("smdone", ttl)] = t_cp
            tok[("wT", ttl)] = t_cp
            tok[("w", ttl)] = t_cp
            _ab_tick(ttl)
            continue
        t_cpr = op("ve", lambda e, rps=rps, rc=rc: e.tensor_copy(
            smr[:], rps[:, rc:rc + NE]), waits=[t_rb], inc="ve")
        t_mx = op("ve", lambda e: e.reduce_max(
            sms[:, 6:7], smr[:], axis=AX, negate=True), waits=[t_cpr], inc="ve")
        t_ex = op("ac", lambda e: e.activation(
            sme[:], smr[:], AF.Exp, bias=sms[:, 6:7]), waits=[t_mx], inc="ac")
        t_sm = op("ve", lambda e: e.reduce_sum(sms[:, 7:8], sme[:], axis=AX),
                  waits=[t_ex], inc="ve")
        t_rc = op("ve", lambda e: e.reciprocal(sms[:, 0:1], sms[:, 7:8]),
                  waits=[t_sm], inc="ve")
        t_p = op("ve", lambda e: e.tensor_scalar_mul(smp[:], sme[:], sms[:, 0:1]),
                 waits=[t_rc], inc="ve")
        t_m1 = op("ve", lambda e: e.reduce_max(sms[:, 1:2], smp[:], axis=AX),
                  waits=[t_p], inc="ve")
        t_eq = op("ve", lambda e: e.tensor_single_scalar(
            smq[:], smp[:], sms[:, 1:2], ALU.is_equal), waits=[t_m1], inc="ve")
        t_mk = op("ve", lambda e: e.scalar_tensor_tensor(
            smq[:], smq[:], NEG_BIG, smp[:], ALU.mult, ALU.add), waits=[t_eq], inc="ve")
        t_m2b = op("ve", lambda e: e.reduce_max(sms[:, 2:3], smq[:], axis=AX),
                   waits=[t_mk], inc="ve")
        t_ge = op("ve", lambda e: e.tensor_single_scalar(
            smq[:], smp[:], sms[:, 2:3], ALU.is_ge), waits=[t_m2b], inc="ve")
        t_w = op("ve", lambda e, ttl=ttl: e.tensor_mul(
            wtok[:, ttl * NE:(ttl + 1) * NE], smp[:], smq[:]), waits=[t_ge], inc="ve")
        tok[("w", ttl)] = t_w
        t_wt = op("te", lambda e, rps=rps, rc=rc, ttl=ttl: e.transpose(
            rps[0:NE, rc + 8:rc + 136], wtok[:, ttl * NE:(ttl + 1) * NE], ident[:]),
            waits=[t_w], inc="te")
        t_wte = op("ve", lambda e, rps=rps, rc=rc, ttl=ttl: e.tensor_copy(
            wT[:, ttl * 128:(ttl + 1) * 128], rps[0:NE, rc + 8:rc + 136]), waits=[t_wt], inc="ve")
        tok[("smdone", ttl)] = t_wte
        tok[("wT", ttl)] = t_wte
        _ab_tick(ttl)

    t_ln_all = tok.get(("lnevac", TT - 1))
    if phases >= 2 and not AB_IL and K_SUB != 1:
        for u_ in range(48):
            emit_ab_unit(u_)
    t_ln_te = tok.get(("lnrb", TT - 1))

    if phases >= 2:
        t_ab_last_ev = [ab_ev[46], ab_ev[47]]

    if phases >= 3:
        def emit_a1_load(fc):
            fw = FCW[fc]
            for i, pl in enumerate([2, 3, 4, 5]):
                slot = i + 4 * (fc % 2)
                if fw == 1:
                    t_l = None
                    for dc in range(DC):
                        t_l = dma("qg", f"sab{slot}", abuf[slot].ap()[:, dc:dc + 1],
                                  a_nyq.ap()[pl, dc * 128:(dc + 1) * 128],
                                  waits=ab_store_mark["kv"] +
                                        [tok.get(("a1mmdone", fc - 2, i))] if dc == 0 else [])
                    tok[("a1w", fc, i)] = t_l
                else:
                    tok[("a1w", fc, i)] = dma(
                        "qg", f"sab{slot}", abuf[slot].ap()[:, 0:8 * fw].rearrange("p (dc f) -> p dc f", dc=DC),
                        a_dram.ap()[pl].rearrange("(dc p) f -> p dc f", p=128)[:, :, fc * 128:fc * 128 + fw],
                        waits=ab_store_mark["kv"] + [tok.get(("a1mmdone", fc - 2, i))])

        emit_a1_load(0)
        emit_a1_load(1)

        a1ev = {}
        a1u = 0
        for fc in range(FC):
            fw = FCW[fc]
            for tch in range(2):
                t0 = tch * 1024
                plane_ev = []
                for i in range(4):
                    slot = i + 4 * (fc % 2)
                    rot = ps[a1u % 4]
                    t_mm = None
                    for dc in range(DC):
                        for nq in range(2):
                            t_mm = op("te", lambda e, rot=rot, fw=fw, slot=slot, dc=dc, nq=nq, t0=t0:
                                      e.matmul(rot[0:fw, nq * 512:(nq + 1) * 512],
                                               abuf[slot][:, dc * fw:(dc + 1) * fw],
                                               xnT[:, dc * T + t0 + nq * 512:dc * T + t0 + (nq + 1) * 512],
                                               start=(dc == 0), stop=(dc == DC - 1)),
                                      waits=[tok[("a1w", fc, i)], a1ev.get(a1u - 4)] +
                                            ([tok[("lnxnT", TT - 1)], tok[("lnevac", TT - 1)]]
                                             if a1u < 2 else []) +
                                            ([tok[("smdone", TT - 1)]] + t_ab_last_ev
                                             if a1u in (2, 3) else [])
                                      if dc == 0 and nq == 0 else [],
                                      inc="te" if (dc == DC - 1 and nq == 1) else None)
                    if tch == 1:
                        tok[("a1mmdone", fc, i)] = t_mm
                        if i == 3 and fc + 2 < FC:
                            emit_a1_load(fc + 2)
                    prev_c = tok.get(("a1c", fc, tch - 1, 1)) if tch == 1 else \
                        tok.get(("a1c", fc - 1, 1, 1))
                    t_ev = op("ac", lambda e, i=i, rot=rot, fw=fw: e.copy(
                        a1t[i][0:fw, :], rot[0:fw, :]),
                        waits=[t_mm, prev_c if i == 0 else None], inc="ac")
                    plane_ev.append(t_ev)
                    a1ev[a1u] = t_ev
                    a1u += 1
                t_s1 = op("ac", lambda e, fw=fw: e.activation(a1t[4][0:fw, :], a1t[0][0:fw, :], AF.Square),
                          waits=[plane_ev[0]], inc="ac")
                t_s2 = op("ac", lambda e, fw=fw: e.activation(a1t[5][0:fw, :], a1t[1][0:fw, :], AF.Square),
                          waits=[plane_ev[1], t_s1], inc="ac")
                t_ss = op("ve", lambda e, fw=fw: e.tensor_add(a1t[4][0:fw, :], a1t[4][0:fw, :], a1t[5][0:fw, :]),
                          waits=[t_s2], inc="ve")
                t_l1 = op("ac", lambda e, fw=fw: e.activation(a1t[4][0:fw, :], a1t[4][0:fw, :], AF.Ln),
                          waits=[t_ss], inc="ac")
                t_e1 = op("ac", lambda e, fw=fw: e.activation(a1t[4][0:fw, :], a1t[4][0:fw, :], AF.Exp, scale=0.5),
                          waits=[t_l1], inc="ac")
                t_l2 = op("ac", lambda e, fw=fw: e.activation(a1t[4][0:fw, :], a1t[4][0:fw, :], AF.Ln, bias=cst[0:fw, 1:2]),
                          waits=[t_l1, t_e1], inc="ac")
                t_e2 = op("ac", lambda e, fw=fw: e.activation(a1t[4][0:fw, :], a1t[4][0:fw, :], AF.Exp, scale=-1.0),
                          waits=[t_l2], inc="ac")
                t_n1 = op("ve", lambda e, fw=fw: e.tensor_mul(a1t[0][0:fw, :], a1t[0][0:fw, :], a1t[4][0:fw, :]),
                          waits=[t_e2], inc="ve")
                t_n2 = op("ve", lambda e, fw=fw: e.tensor_mul(a1t[1][0:fw, :], a1t[1][0:fw, :], a1t[4][0:fw, :]),
                          waits=[t_n1], inc="ve")
                t_p1 = op("ve", lambda e, fw=fw: e.tensor_mul(a1t[5][0:fw, :], a1t[0][0:fw, :], a1t[2][0:fw, :]),
                          waits=[t_n2, plane_ev[2]], inc="ve")
                t_p2 = op("ve", lambda e, fw=fw: e.tensor_mul(a1t[6][0:fw, :], a1t[1][0:fw, :], a1t[3][0:fw, :]),
                          waits=[t_p1, plane_ev[3]], inc="ve")
                t_p3 = op("ve", lambda e, fw=fw: e.tensor_sub(a1t[5][0:fw, :], a1t[5][0:fw, :], a1t[6][0:fw, :]),
                          waits=[t_p2], inc="ve")
                t_c1 = op("ve", lambda e, fw=fw, fc=fc, t0=t0, tch=tch: e.tensor_tensor_scan(
                    cre[fc][0:fw, t0:t0 + 1024], zeros[0:fw, :], a1t[5][0:fw, :],
                    0.0 if tch == 0 else cre[fc][0:fw, t0 - 1:t0],
                    ALU.add, ALU.add),
                    waits=[t_p3, t_zero, tok.get(("a1c", fc, tch - 1, 0))], inc="ve")
                tok[("a1c", fc, tch, 0)] = t_c1
                t_q1 = op("ve", lambda e, fw=fw: e.tensor_mul(a1t[6][0:fw, :], a1t[0][0:fw, :], a1t[3][0:fw, :]),
                          waits=[t_c1], inc="ve")
                t_q2 = op("ve", lambda e, fw=fw: e.tensor_mul(a1t[7][0:fw, :], a1t[1][0:fw, :], a1t[2][0:fw, :]),
                          waits=[t_q1], inc="ve")
                t_q3 = op("ve", lambda e, fw=fw: e.tensor_add(a1t[6][0:fw, :], a1t[6][0:fw, :], a1t[7][0:fw, :]),
                          waits=[t_q2], inc="ve")
                t_c2 = op("ve", lambda e, fw=fw, fc=fc, t0=t0, tch=tch: e.tensor_tensor_scan(
                    cim[fc][0:fw, t0:t0 + 1024], zeros[0:fw, :], a1t[6][0:fw, :],
                    0.0 if tch == 0 else cim[fc][0:fw, t0 - 1:t0],
                    ALU.add, ALU.add),
                    waits=[t_q3, tok.get(("a1c", fc, tch - 1, 1))], inc="ve")
                tok[("a1c", fc, tch, 1)] = t_c2
        t_a1_last_ev = [a1ev[a1u - 1], a1ev[a1u - 2], a1ev[a1u - 3], a1ev[a1u - 4]]


    if phases >= 4:
        t_tz = op("ve", lambda e: e.memset(Ttot[:], 0.0), inc="ve")
        t_tt = t_tz
        for fc in range(FC):
            fw = FCW[fc]
            t_tt = op("ve", lambda e, fc=fc, fw=fw: e.tensor_copy(
                Ttot[0:fw, fc:fc + 1], cre[fc][0:fw, T - 1:T]),
                waits=[tok[("a1c", fc, 1, 0)], t_tt], inc="ve")
            t_tt = op("ve", lambda e, fc=fc, fw=fw: e.tensor_copy(
                Ttot[0:fw, 5 + fc:6 + fc], cim[fc][0:fw, T - 1:T]),
                waits=[tok[("a1c", fc, 1, 1)], t_tt], inc="ve")
        t_trw = op("ve", lambda e: e.tensor_copy(Trow[:], ps[3][0:10, 0:128]),
                   waits=[("sxd1", 1)], inc="ve")
        t_S = op("ve", lambda e: e.tensor_copy(Ssb[:], ps[3][:, 512:522]),
                 waits=[("sxd2", 1)], inc="ve")


    if phases >= 5:
        def emit_a2_load(fc):
            fw = FCW[fc]
            for i in range(2):
                slot = i + 2 * (fc % 2)
                w0 = ab_store_mark["q"] + [tok.get(("a2mmdone", fc - 2, i)),
                      tok.get(("a1mmdone", FC - 1, i + 2 * (fc % 2))) if fc < 2 else None]
                if fw == 1:
                    t_l = None
                    for dc in range(DC):
                        t_l = dma("qg", f"sab{slot}", abuf[slot].ap()[:, dc:dc + 1],
                                  a_nyq.ap()[i, dc * 128:(dc + 1) * 128],
                                  waits=w0 if dc == 0 else [])
                    tok[("a2w", fc, i)] = t_l
                else:
                    tok[("a2w", fc, i)] = dma(
                        "qg", f"sab{slot}", abuf[slot].ap()[:, 0:8 * fw].rearrange("p (dc f) -> p dc f", dc=DC),
                        a_dram.ap()[i].rearrange("(dc p) f -> p dc f", p=128)[:, :, fc * 128:fc * 128 + fw],
                        waits=w0)

        emit_a2_load(0)
        emit_a2_load(1)
        a2ev = {}
        a2u = 0
        for fc in range(FC):
            fw = FCW[fc]
            for tch in range(2):
                t0 = tch * 1024
                plane_ev = []
                for i in range(2):
                    slot = i + 2 * (fc % 2)
                    rot = ps[a2u % 3]
                    t_mm = None
                    for dc in range(DC):
                        for nq in range(2):
                            t_mm = op("te", lambda e, rot=rot, fw=fw, slot=slot, dc=dc, nq=nq, t0=t0:
                                      e.matmul(rot[0:fw, nq * 512:(nq + 1) * 512],
                                               abuf[slot][:, dc * fw:(dc + 1) * fw],
                                               xnT[:, dc * T + t0 + nq * 512:dc * T + t0 + (nq + 1) * 512],
                                               start=(dc == 0), stop=(dc == DC - 1)),
                                      waits=[tok[("a2w", fc, i)], a2ev.get(a2u - 3)] +
                                            (t_a1_last_ev if a2u < 4 else [])
                                      if dc == 0 and nq == 0 else [],
                                      inc="te" if (dc == DC - 1 and nq == 1) else None)
                    if tch == 1:
                        tok[("a2mmdone", fc, i)] = t_mm
                        if i == 1 and fc + 2 < FC:
                            emit_a2_load(fc + 2)
                    prev_g = tok.get(("gdone", fc, tch - 1)) if tch == 1 else \
                        tok.get(("gdone", fc - 1, 1))
                    t_ev = op("ac", lambda e, i=i, rot=rot, fw=fw: e.copy(
                        fqt[i][0:fw, :], rot[0:fw, :]),
                        waits=[t_mm, prev_g if i == 0 else None], inc="ac")
                    plane_ev.append(t_ev)
                    a2ev[a2u] = t_ev
                    a2u += 1
                t_s1 = op("ac", lambda e, fw=fw: e.activation(a2scr[0][0:fw, :], fqt[0][0:fw, :], AF.Square),
                          waits=[plane_ev[0], tok.get(("gdone", fc, tch - 1)),
                                 tok.get(("gdone", fc - 1, 1))], inc="ac")
                t_s2 = op("ac", lambda e, fw=fw: e.activation(a2scr[1][0:fw, :], fqt[1][0:fw, :], AF.Square),
                          waits=[plane_ev[1], t_s1], inc="ac")
                t_ss = op("ve", lambda e, fw=fw: e.tensor_add(a2scr[0][0:fw, :], a2scr[0][0:fw, :], a2scr[1][0:fw, :]),
                          waits=[t_s2], inc="ve")
                t_l1 = op("ac", lambda e, fw=fw: e.activation(a2scr[0][0:fw, :], a2scr[0][0:fw, :], AF.Ln),
                          waits=[t_ss], inc="ac")
                t_e1 = op("ac", lambda e, fw=fw: e.activation(a2scr[0][0:fw, :], a2scr[0][0:fw, :], AF.Exp, scale=0.5),
                          waits=[t_l1], inc="ac")
                t_l2 = op("ac", lambda e, fw=fw: e.activation(a2scr[0][0:fw, :], a2scr[0][0:fw, :], AF.Ln, bias=cst[0:fw, 1:2]),
                          waits=[t_e1], inc="ac")
                t_e2 = op("ac", lambda e, fw=fw: e.activation(a2scr[0][0:fw, :], a2scr[0][0:fw, :], AF.Exp, scale=-1.0),
                          waits=[t_l2], inc="ac")
                t_n1 = op("ve", lambda e, fw=fw: e.tensor_mul(fqt[2][0:fw, :], fqt[0][0:fw, :], a2scr[0][0:fw, :]),
                          waits=[t_e2], inc="ve")
                t_n2 = op("ve", lambda e, fw=fw: e.tensor_mul(fqt[3][0:fw, :], fqt[1][0:fw, :], a2scr[0][0:fw, :]),
                          waits=[t_n1], inc="ve")
                jre, jim = fc, 5 + fc
                t_a = op("ve", lambda e, fw=fw, fc=fc, t0=t0, jre=jre: e.tensor_scalar_add(
                    fqt[0][0:fw, :], cre[fc][0:fw, t0:t0 + 1024], Ssb[0:fw, jre:jre + 1]),
                    waits=[t_n2, t_S], inc="ve")
                t_b = op("ve", lambda e, fw=fw, fc=fc, t0=t0, jim=jim: e.tensor_scalar_add(
                    fqt[1][0:fw, :], cim[fc][0:fw, t0:t0 + 1024], Ssb[0:fw, jim:jim + 1]),
                    waits=[t_a], inc="ve")
                t_u1 = op("ve", lambda e, fw=fw: e.tensor_mul(a2scr[0][0:fw, :], fqt[0][0:fw, :], fqt[2][0:fw, :]),
                          waits=[t_b], inc="ve")
                t_u2 = op("ve", lambda e, fw=fw: e.tensor_mul(a2scr[1][0:fw, :], fqt[1][0:fw, :], fqt[3][0:fw, :]),
                          waits=[t_u1], inc="ve")
                t_u3 = op("ve", lambda e, fw=fw: e.tensor_add(a2scr[0][0:fw, :], a2scr[0][0:fw, :], a2scr[1][0:fw, :]),
                          waits=[t_u2], inc="ve")
                t_g1 = op("ve", lambda e, fw=fw, fc=fc, t0=t0: e.tensor_copy(
                    gre[fc][0:fw, t0:t0 + 1024], a2scr[0][0:fw, :]), waits=[t_u3], inc="ve")
                t_v1 = op("ve", lambda e, fw=fw: e.tensor_mul(a2scr[1][0:fw, :], fqt[1][0:fw, :], fqt[2][0:fw, :]),
                          waits=[t_g1], inc="ve")
                t_v2 = op("ve", lambda e, fw=fw: e.tensor_mul(a2scr[0][0:fw, :], fqt[0][0:fw, :], fqt[3][0:fw, :]),
                          waits=[t_v1], inc="ve")
                t_v3 = op("ve", lambda e, fw=fw: e.tensor_sub(a2scr[1][0:fw, :], a2scr[1][0:fw, :], a2scr[0][0:fw, :]),
                          waits=[t_v2], inc="ve")
                t_g2 = op("ve", lambda e, fw=fw, fc=fc, t0=t0: e.tensor_copy(
                    gim[fc][0:fw, t0:t0 + 1024], a2scr[1][0:fw, :]), waits=[t_v3], inc="ve")
                tok[("gdone", fc, tch)] = t_g2
                if fc == 0 and tch == 0:
                    op("te", lambda e: e.transpose(ps[3][0:10, 0:128], Ttot[:, 0:10], ident[:]),
                       waits=[t_tt, tok[("smdone", TT - 1)]], inc="sxd1")
                if fc == 0 and tch == 1:
                    t_cin = dma("qs", "s_exq",
                                cc_in.ap().rearrange("o (j f) -> (o j) f", j=10), Trow[:],
                                waits=[t_trw])
                    if _os.environ.get("K_NOCC") == "1":
                        t_cc = None
                        for _r in range(NCORES):
                            t_cc = op("gp", lambda e, _r=_r: e.dma_start(
                                out=cc_out.ap()[_r:_r + 1, :], in_=cc_in.ap()),
                                waits=[t_cin] if _r == 0 else [], inc="cc", by=16)
                    else:
                        t_cc = op("gp", lambda e: e.collective_compute(
                            "AllGather", ALU.bypass, replica_groups=[list(range(NCORES))],
                            ins=[cc_in.ap().opt()], outs=[cc_out.ap().opt()]),
                            waits=[t_cin], inc="cc", by=1)
                    t_gth = dma("qs", "s_exq", gath[:], cc_out.ap(),
                                waits=[t_cc, tok[("lnxdone", TT - 1)]])
                    for j in range(10):
                        op("te", lambda e, j=j: e.matmul(
                            ps[3][:, 512 + j:513 + j], gath[0:NCORES, j * 128:(j + 1) * 128],
                            maskv[:], start=True, stop=True),
                            waits=[t_gth, t_maskv, t_trw] if j == 0 else [],
                            inc="sxd2" if j == 9 else None)
        t_g_all = tok[("gdone", FC - 1, 1)]

        dma("qg", "s_if", ifre_sb.ap().rearrange("p (fc d) -> p fc d", fc=FC),
            ifre_d.rearrange("(fc p) d -> p fc d", p=128)[:, 0:FC, :],
            waits=[t_g_all])
        t_ifim = dma("qg", "s_if", ifim_sb.ap().rearrange("p (fc d) -> p fc d", fc=FC),
                     ifim_d.rearrange("(fc p) d -> p fc d", p=128)[:, 0:FC, :])
        t_ifre = t_ifim


    if phases >= 6:
        xstg4 = xstg + [xsq, xnf]
        for ttl in range(TT):
            s = ttl % 4
            t_x = dma("qs", ("sxa", "sxb", "sxc", "sxd")[s], xstg4[s][:],
                      x_d[ttl * 128:(ttl + 1) * 128, :],
                      waits=[t_g_all if ttl < 4 else None, tok.get(("irfr1", ttl - 4))])
            rot = ps[ttl % 4]
            t_mm = None
            first = True
            for nd in range(2):
                for fc in range(FC):
                    fw = FCW[fc]
                    for pl in range(2):
                        gsb = gre[fc] if pl == 0 else gim[fc]
                        isb = ifre_sb if pl == 0 else ifim_sb
                        t_mm = op("te", lambda e, rot=rot, nd=nd, fc=fc, fw=fw, gsb=gsb, isb=isb, ttl=ttl, pl=pl:
                                  e.matmul(rot[:, nd * 512:(nd + 1) * 512],
                                           gsb[0:fw, ttl * 128:(ttl + 1) * 128],
                                           isb[0:fw, fc * 1024 + nd * 512:fc * 1024 + (nd + 1) * 512],
                                           start=(fc == 0 and pl == 0), stop=(fc == FC - 1 and pl == 1)),
                                  waits=[t_g_all, t_ifim, tok.get(("irfev", ttl - 2)),
                                         a2ev.get(a2u - 1), a2ev.get(a2u - 2)]
                                  if first else [],
                                  inc="te" if (nd == 1 and fc == FC - 1 and pl == 1) else None)
                        first = False
            t_xn2 = op("ve", lambda e, s=s, ttl=ttl: e.tensor_scalar(
                a2scr[0][:], xstg4[s][:], statrs[:, ttl:ttl + 1], statpr[:, ttl:ttl + 1],
                ALU.mult, ALU.add),
                waits=[t_x, tok[("pr", ttl)], tok.get(("irfr1", ttl - 1))], inc="ve")
            t_r1 = op("ve", lambda e, s=s, ttl=ttl: e.tensor_add(
                outp[ttl][:], xstg4[s][:], a2scr[0][:]), waits=[t_xn2], inc="ve")
            tok[("irfr1", ttl)] = t_r1
            t_r2 = op("ve", lambda e, rot=rot, ttl=ttl: e.tensor_add(
                outp[ttl][:], outp[ttl][:], rot[:]), waits=[t_r1, t_mm], inc="ve")
            tok[("irfev", ttl)] = t_r2
        t_irf_all = tok[("irfev", TT - 1)]


    if phases >= 7:
        def w1_dram(e):
            return rw1_d[e] if e < NE else sw1_d

        def w2_dram(e):
            return rw2_d[e] if e < NE else sw2_d

        NU = 2 * NEXP
        h1cnt = 0
        ucnt = 0
        ffn_mm = {}
        gel_by_cnt = {}
        uev_by_cnt = {}
        umm_by = {}
        w1tok = {}
        w2tok = {}

        def emit_ffn_load(u):
            e, half = divmod(u, 2)
            s = u % 2
            war = [ffn_mm.get(u - 2), t_irf_all if u < 2 else None]
            w1tok[(u, 0)] = dma(
                "qg", ("sw10", "sw11")[s],
                rw1hb[s].ap().rearrange("p (dc h) -> p dc h", dc=DC)[:, :, 0:512],
                w1_dram(e).rearrange("(dc p) h -> p dc h", p=128)[:, :, half * 1024:half * 1024 + 512],
                waits=war)
            w1tok[(u, 1)] = dma(
                "qg", ("sw10", "sw11")[s],
                rw1hb[s].ap().rearrange("p (dc h) -> p dc h", dc=DC)[:, :, 512:1024],
                w1_dram(e).rearrange("(dc p) h -> p dc h", p=128)[:, :, half * 1024 + 512:half * 1024 + 1024])
            w2tok[u] = dma(
                "qg", ("sw20", "sw21")[s],
                rw2hb[s].ap().rearrange("p (hc d) -> p hc d", hc=DC),
                w2_dram(e).rearrange("(hc p) d -> p hc d", p=128)[:, half * 8:(half + 1) * 8, :],
                waits=war)

        emit_ffn_load(0)
        emit_ffn_load(1)
        for u in range(NU):
            e, half = divmod(u, 2)
            s = u % 2
            last_e8 = (e == NEXP - 1 and half == 1)
            t_mm = None
            for step in range(5):
                if step < 4:
                    tch4 = step
                    g = u * 4 + tch4
                    b = tch4 % 2
                    t0 = tch4 * 512
                    for hc8 in range(8):
                        hcg = half * 8 + hc8
                        pa = (ps[0], ps[1])[(h1cnt // 2) % 2]
                        c0 = (h1cnt % 2) * 512
                        t_hmm = None
                        for dc in range(DC):
                            w0 = []
                            if dc == 0:
                                w0 = [w1tok[(u, hc8 // 4)],
                                      gel_by_cnt.get(h1cnt - 4),
                                      t_irf_all if u == 0 and tch4 == 0 else None]
                            t_hmm = op("te", lambda e_, pa=pa, c0=c0, s=s, dc=dc, hc8=hc8, t0=t0:
                                       e_.matmul(pa[:, c0:c0 + 512],
                                                 rw1hb[s][:, dc * 1024 + hc8 * 128:dc * 1024 + (hc8 + 1) * 128],
                                                 xnT[:, dc * T + t0:dc * T + t0 + 512],
                                                 start=(dc == 0), stop=(dc == DC - 1)),
                                       waits=w0, inc="te" if dc == DC - 1 else None)
                        if e < NE:
                            bias_ap = (lambda hcg=hcg, e=e: rb1T[:, hcg * NE + e:hcg * NE + e + 1])
                        else:
                            bias_ap = (lambda hcg=hcg: sb1T[:, hcg:hcg + 1])
                        t_g = op("ac", lambda e_, pa=pa, c0=c0, b=b, hc8=hc8, bias_ap=bias_ap:
                                 e_.activation(hwb[b][:, hc8 * 512:(hc8 + 1) * 512],
                                               pa[:, c0:c0 + 512], AF.Gelu, bias=bias_ap()),
                                 waits=[t_hmm, t_bias_tr, tok[("rb1ev", HC - 1)],
                                        umm_by.get(g - 2) if hc8 == 0 else None], inc="ac")
                        gel_by_cnt[h1cnt] = t_g
                        h1cnt += 1
                    tok[("ffn_gel", g)] = t_g
                if step > 0:
                    tc = step - 1
                    g2 = u * 4 + tc
                    b2 = tc % 2
                    for tl in range(4):
                        ttl = tc * 4 + tl
                        pb = ps[2 + ucnt % 2]
                        t_mm = None
                        for hc8 in range(8):
                            for nd in range(2):
                                w0 = []
                                if hc8 == 0 and nd == 0:
                                    w0 = [w2tok[u], tok[("ffn_gel", g2)],
                                          uev_by_cnt.get(ucnt - 2)]
                                t_mm = op("te", lambda e_, pb=pb, b2=b2, s=s, hc8=hc8, nd=nd, tl=tl:
                                          e_.matmul(pb[:, nd * 512:(nd + 1) * 512],
                                                    hwb[b2][:, hc8 * 512 + tl * 128:hc8 * 512 + (tl + 1) * 128],
                                                    rw2hb[s][:, hc8 * 1024 + nd * 512:hc8 * 1024 + (nd + 1) * 512],
                                                    start=(hc8 == 0),
                                                    stop=(hc8 == 7 and not last_e8)),
                                          waits=w0,
                                          inc="te" if (hc8 == 7 and nd == 1 and not last_e8) else None)
                        if last_e8:
                            op("te", lambda e_, pb=pb, ttl=ttl: e_.matmul(
                                pb[:, 0:512], wT[:, ttl * 128:(ttl + 1) * 128],
                                rb2sb[:, 0:512], start=False, stop=False),
                                waits=[tok[("wT", ttl)], t_rb2], inc=None)
                            op("te", lambda e_, pb=pb, ttl=ttl: e_.matmul(
                                pb[:, 512:1024], wT[:, ttl * 128:(ttl + 1) * 128],
                                rb2sb[:, 512:1024], start=False, stop=False), inc=None)
                            op("te", lambda e_, pb=pb: e_.matmul(
                                pb[:, 0:512], onesb[:], sb2sb[:, 0:512],
                                start=False, stop=False), waits=[t_onesb, t_sb2], inc=None)
                            t_mm = op("te", lambda e_, pb=pb: e_.matmul(
                                pb[:, 512:1024], onesb[:], sb2sb[:, 512:1024],
                                start=False, stop=True), inc="te")
                        if e < NE:
                            t_ev = op("ve", lambda e_, pb=pb, ttl=ttl, e=e: e_.scalar_tensor_tensor(
                                outp[ttl][:], pb[:], wtok[:, ttl * NE + e:ttl * NE + e + 1],
                                outp[ttl][:], ALU.mult, ALU.add),
                                waits=[t_mm, tok[("w", ttl)] if u == 0 else None,
                                       tok[("irfev", ttl)] if u == 0 else None], inc="ve")
                        else:
                            t_ev = op("ve", lambda e_, pb=pb, ttl=ttl: e_.tensor_add(
                                outp[ttl][:], outp[ttl][:], pb[:]),
                                waits=[t_mm], inc="ve")
                        uev_by_cnt[ucnt] = t_ev
                        ucnt += 1
                        if last_e8:
                            sdma(y_d[ttl * 128:(ttl + 1) * 128, :], outp[ttl][:], waits=[t_ev])
                    umm_by[g2] = t_mm
            ffn_mm[u] = t_mm
            if u + 2 < NU:
                emit_ffn_load(u + 2)



    if debug:
        dbg_c = nc.dram_tensor("dbg_c", [2 * FC, 128, T], F32, kind="ExternalOutput").ap()
        dbg_w = nc.dram_tensor("dbg_w", [128, TT * NE], F32, kind="ExternalOutput").ap()
        dbg_stat = nc.dram_tensor("dbg_stat", [128, 3 * TT], F32, kind="ExternalOutput").ap()
        dbg_S = nc.dram_tensor("dbg_S", [128, 10], F32, kind="ExternalOutput").ap()
        if K_SUB == 1:
            sdma(dbg_w, rb1T[:], waits=[tok[("rb1ev", HC - 1)]])
        if phases >= 1 and K_SUB != 1:
            sdma(dbg_w, wtok[:], waits=[tok[("w", TT - 1)]])
            sdma(dbg_stat[:, 0:TT], statnm[:], waits=[tok[("pr", TT - 1)]])
            sdma(dbg_stat[:, TT:2 * TT], statrs[:], waits=[tok[("pr", TT - 1)]])
            sdma(dbg_stat[:, 2 * TT:3 * TT], statpr[:], waits=[tok[("pr", TT - 1)]])
        if phases >= 3 and phases < 6:
            for fc in range(FC):
                fw = FCW[fc]
                sdma(dbg_c[fc, 0:fw, :], cre[fc][0:fw, :], waits=[tok[("a1c", fc, 1, 0)]])
                sdma(dbg_c[FC + fc, 0:fw, :], cim[fc][0:fw, :], waits=[tok[("a1c", fc, 1, 1)]])
        if phases >= 4:
            sdma(dbg_S, Ssb[:], waits=[t_S])
        if phases >= 5:
            dbg_g = nc.dram_tensor("dbg_g", [2 * FC, 128, T], F32, kind="ExternalOutput").ap()
            for fc in range(FC):
                fw = FCW[fc]
                dma("qg", "s_dbg", dbg_g[fc, 0:fw, :], gre[fc][0:fw, :],
                    waits=[tok[("gdone", FC - 1, 1)]])
                dma("qg", "s_dbg", dbg_g[FC + fc, 0:fw, :], gim[fc][0:fw, :],
                    waits=[tok[("gdone", FC - 1, 1)]])
        if phases >= 6 and phases < 7:
            for ttl in range(TT):
                sdma(y_d[ttl * 128:(ttl + 1) * 128, :], outp[ttl][:],
                     waits=[tok[("irfev", ttl)]])
    op("sy", lambda e: e.nop(), waits=[("st", cnt["st"])])

    import contextlib
    with contextlib.ExitStack() as ctx:
        sems = {s: ctx.enter_context(nc.semaphore(s)) for s in sem_names}
        block = ctx.enter_context(nc.Block())

        def emit(stream_key):
            def body(eng):
                seen = {s: 0 for s in sem_names}
                for fn, waits, inc, by in streams[stream_key]:
                    for (s, v) in waits:
                        if v > seen[s]:
                            eng.wait_ge(sems[s], v)
                            seen[s] = v
                    ins = fn(eng)
                    if inc is not None:
                        ins.then_inc(sems[inc], by)
            return body

        block.tensor(emit("te"))
        block.vector(emit("ve"))
        block.scalar(emit("ac"))
        block.gpsimd(emit("gp"))
        block.sync(emit("sy"))

    return nc


_PROG_CACHE = {}


def _consts():
    e = np.arange(D, dtype=np.float64)
    f = np.arange(NF, dtype=np.float64)
    ang = 2.0 * np.pi * np.outer(e, f) / D
    fre = np.cos(ang)
    fim = -np.sin(ang)
    fim[:, 0] = 0.0
    fim[:, NF - 1] = 0.0
    dd = np.arange(D, dtype=np.float64)
    angi = 2.0 * np.pi * np.outer(f, dd) / D
    cf = np.full(NF, 2.0)
    cf[0] = 1.0
    cf[NF - 1] = 1.0
    ifre = (cf[:, None] / D) * np.cos(angi)
    ifim = -(cf[:, None] / D) * np.sin(angi)
    ifre_p = np.zeros((640, D))
    ifim_p = np.zeros((640, D))
    ifre_p[:NF] = ifre
    ifim_p[:NF] = ifim
    bf = ml_dtypes.bfloat16
    return (fre.astype(bf), fim.astype(bf), ifre_p.astype(bf), ifim_p.astype(bf))


def kernel(L0_attn_x, L0_attn_Wq, L0_attn_Wk, L0_attn_Wv,
           L0_ffn_router_w, L0_ffn_router_b, L0_ffn_sw1, L0_ffn_sb1,
           L0_ffn_sw2, L0_ffn_sb2, L0_ffn_rw1, L0_ffn_rb1,
           L0_ffn_rw2, L0_ffn_rb2):
    if "nc" not in _PROG_CACHE:
        _PROG_CACHE["nc"] = build_program()
    nc = _PROG_CACHE["nc"]

    fre, fim, ifre, ifim = _consts()
    ident = np.eye(128, dtype=np.float32)
    onesf = np.ones((1, 128), dtype=np.float32)
    onesb = np.ones((1, 128), dtype=ml_dtypes.bfloat16)

    x = np.ascontiguousarray(np.asarray(L0_attn_x, dtype=np.float32).reshape(NCORES * T, D))
    common = dict(
        Wq=np.asarray(L0_attn_Wq, np.float32), Wk=np.asarray(L0_attn_Wk, np.float32),
        Wv=np.asarray(L0_attn_Wv, np.float32),
        router_w=np.asarray(L0_ffn_router_w, np.float32),
        router_b=np.asarray(L0_ffn_router_b, np.float32),
        sw1=np.asarray(L0_ffn_sw1, np.float32), sb1=np.asarray(L0_ffn_sb1, np.float32),
        sw2=np.asarray(L0_ffn_sw2, np.float32), sb2=np.asarray(L0_ffn_sb2, np.float32),
        rw1=np.asarray(L0_ffn_rw1, np.float32), rb1=np.asarray(L0_ffn_rb1, np.float32),
        rw2=np.asarray(L0_ffn_rw2, np.float32), rb2=np.asarray(L0_ffn_rb2, np.float32),
        dft_re=fre, dft_im=fim, idft_re=ifre, idft_im=ifim,
        ident=ident, onesf=onesf, onesb=onesb,
    )
    in_maps = []
    for c in range(NCORES):
        mask = ((np.arange(NCORES) < c) & (np.arange(NCORES) // 4 == c // 4)).astype(np.float32)
        m = dict(common)
        m["x"] = x[c * T:(c + 1) * T]
        m["maskv"] = mask
        in_maps.append(m)
    res = run_bass_kernel_spmd(nc, in_maps, core_ids=list(range(NCORES)))
    out = np.concatenate([np.asarray(res.results[c]["y"], np.float32) for c in range(NCORES)], axis=0)
    return out.reshape(2, 8192, D)


# version 14
# speedup vs baseline: 1.2204x; 1.0045x over previous
import numpy as np
import ml_dtypes
import concourse.bass as bass
import concourse.mybir as mybir
from concourse.bass_utils import run_bass_kernel_spmd

F32 = mybir.dt.float32
BF16 = mybir.dt.bfloat16
AF = mybir.ActivationFunctionType
ALU = mybir.AluOpType
AX = mybir.AxisListType.X

NCORES = 8
D = 1024
HID = 2048
NE = 8
NEXP = NE + 1
T = 2048
NF = 513
DC = D // 128
HC = HID // 128
TT = T // 128
FC = 5
FCW = [128, 128, 128, 128, 1]
EPS_FFT = 1e-8
EPS_LN = 1e-5
NEG_BIG = -1e9

XNT_O = 36832
R_O = XNT_O + 32768
G_O = R_O + 81920
IFR_O = G_O + 40960
WB_O = IFR_O + 20480
END_O = WB_O + 16384
assert END_O <= 229344, END_O

OUTP_O = [R_O + i * 4096 for i in range(16)]
XSTG_O = [R_O + 16 * 4096, R_O + 17 * 4096]
XSQ_O = R_O + 18 * 4096
XNF_O = R_O + 19 * 4096
XNT32_O = R_O + 4 * 4096
RB1RAW_O = R_O + 5 * 4096
SB1RAW_O = R_O + 7 * 4096
GATH_O = G_O
RW2HI_O = [R_O + (16 + i) * 4096 for i in range(3)]

A1T_O = [G_O + i * 4096 for i in range(8)]
ZERO_O = G_O + 8 * 4096
FRE_O = G_O
FIM_O = G_O + 8224
ATS_O = G_O + 16448


def g_slot(fc):
    return G_O + fc * 8192


def hw_slot(hc):
    return G_O + hc * 2048


FQT_O = [IFR_O + i * 4096 for i in range(4)]
IFRE_O = IFR_O
IFIM_O = IFR_O + 10240

WSB_O = WB_O


def abuf_slot(i):
    return WB_O + i * 2048


A2SCR_O = [WB_O + 8192, WB_O + 12288]


def rw1_slot(i):
    return WB_O + i * 2048


def rw2_slot(hc):
    if hc < 10:
        return IFR_O + hc * 2048
    return RW2HI_O[(hc - 10) // 2] + ((hc - 10) % 2) * 2048


def build_program(phases=7, debug=False):
    import os as _os
    K_SUB = int(_os.environ.get("K_SUB", "0"))
    K_TTLIM = int(_os.environ.get("K_TTLIM", "16"))
    nc = bass.Bass("TRN2", target_bir_lowering=False, num_devices=NCORES)

    x_d = nc.dram_tensor("x", [T, D], F32, kind="ExternalInput").ap()
    wq_d = nc.dram_tensor("Wq", [D, D], F32, kind="ExternalInput").ap()
    wk_d = nc.dram_tensor("Wk", [D, D], F32, kind="ExternalInput").ap()
    wv_d = nc.dram_tensor("Wv", [D, D], F32, kind="ExternalInput").ap()
    rww_d = nc.dram_tensor("router_w", [D, NE], F32, kind="ExternalInput").ap()
    rwb_d = nc.dram_tensor("router_b", [NE], F32, kind="ExternalInput").ap()
    sw1_d = nc.dram_tensor("sw1", [D, HID], F32, kind="ExternalInput").ap()
    sb1_d = nc.dram_tensor("sb1", [HID], F32, kind="ExternalInput").ap()
    sw2_d = nc.dram_tensor("sw2", [HID, D], F32, kind="ExternalInput").ap()
    sb2_d = nc.dram_tensor("sb2", [D], F32, kind="ExternalInput").ap()
    rw1_d = nc.dram_tensor("rw1", [NE, D, HID], F32, kind="ExternalInput").ap()
    rb1_d = nc.dram_tensor("rb1", [NE, HID], F32, kind="ExternalInput").ap()
    rw2_d = nc.dram_tensor("rw2", [NE, HID, D], F32, kind="ExternalInput").ap()
    rb2_d = nc.dram_tensor("rb2", [NE, D], F32, kind="ExternalInput").ap()
    fre_d = nc.dram_tensor("dft_re", [D, NF], BF16, kind="ExternalInput").ap()
    fim_d = nc.dram_tensor("dft_im", [D, NF], BF16, kind="ExternalInput").ap()
    ifre_d = nc.dram_tensor("idft_re", [640, D], BF16, kind="ExternalInput").ap()
    ifim_d = nc.dram_tensor("idft_im", [640, D], BF16, kind="ExternalInput").ap()
    ident_d = nc.dram_tensor("ident", [128, 128], F32, kind="ExternalInput").ap()
    onesf_d = nc.dram_tensor("onesf", [1, 128], F32, kind="ExternalInput").ap()
    onesb_d = nc.dram_tensor("onesb", [1, 128], BF16, kind="ExternalInput").ap()
    maskv_d = nc.dram_tensor("maskv", [NCORES], F32, kind="ExternalInput").ap()
    y_d = nc.dram_tensor("y", [T, D], F32, kind="ExternalOutput").ap()

    a_dram = nc.dram_tensor("a_scr", [6, D, NF], BF16, kind="Internal")
    a_nyq = nc.dram_tensor("a_nyq", [6, D], BF16, kind="Internal")
    cc_in = nc.dram_tensor("cc_in", [1, 1280], F32, kind="Internal")
    cc_out = nc.dram_tensor("cc_out", [NCORES, 1280], F32, kind="Internal")

    def at(name, shape, dtype, off):
        import os as _os
        if _os.environ.get("KAUTO") == "1":
            return nc.alloc_sbuf_tensor(name, shape, dtype)
        return nc.alloc_sbuf_tensor_at(name, shape, dtype, offset=off)

    xnT = at("xnT_s", [128, DC * T], BF16, XNT_O)
    xstg = [at(f"xstg{i}", [128, 1024], F32, XSTG_O[i]) for i in range(2)]
    xsq = at("xsq", [128, 1024], F32, XSQ_O)
    xnf = at("xnf", [128, 1024], F32, XNF_O)
    xnT32 = at("xnT32", [128, DC * 128], F32, XNT32_O)
    rb1raw = at("rb1raw", [NE, HID], F32, RB1RAW_O)
    sb1raw = at("sb1raw", [1, HID], F32, SB1RAW_O)
    gath = at("gath", [NCORES, 1280], F32, GATH_O)
    outp = [at(f"outp{i}", [128, 1024], F32, OUTP_O[i]) for i in range(16)]
    cre = [at(f"cre{fc}", [128, T], F32, R_O + fc * 16384) for fc in range(FC)]
    cim = [at(f"cim{fc}", [128, T], F32, R_O + fc * 16384 + 8192) for fc in range(FC)]
    a1t = [at(f"a1t{i}", [128, 1024], F32, A1T_O[i]) for i in range(8)]
    zeros = at("zeros", [128, 1024], F32, ZERO_O)
    fre_sb = at("fre_sb", [128, 8 * NF], BF16, FRE_O)
    fim_sb = at("fim_sb", [128, 8 * NF], BF16, FIM_O)
    ats = [at("ats", [128, NF], BF16, ATS_O),
           at("ats2", [128, NF], BF16, ATS_O + 1056)]
    gre = [at(f"gre{fc}", [128, T], BF16, g_slot(fc)) for fc in range(FC)]
    gim = [at(f"gim{fc}", [128, T], BF16, g_slot(fc) + 4096) for fc in range(FC)]
    hw = [at(f"hw{hc}", [128, 1024], BF16, hw_slot(hc)) for hc in range(HC)]
    fqt = [at(f"fqt{i}", [128, 1024], F32, FQT_O[i]) for i in range(4)]
    ifre_sb = at("ifre_sb", [128, FC * 1024], BF16, IFRE_O)
    ifim_sb = at("ifim_sb", [128, FC * 1024], BF16, IFIM_O)
    wsb_g = [at("wsbg0", [128, 8 * 1024], BF16, IFR_O),
             at("wsbg1", [128, 8 * 1024], BF16, WSB_O)]
    abuf = [at(f"abuf{i}", [128, 8 * 128], BF16, abuf_slot(i)) for i in range(8)]
    a2scr = [at(f"a2scr{i}", [128, 1024], F32, A2SCR_O[i]) for i in range(2)]
    rw1hb = [at("rw1h0", [128, 8 * 1024], BF16, G_O),
             at("rw1h1", [128, 8 * 1024], BF16, IFR_O)]
    rw2hb = [at("rw2h0", [128, 8 * 1024], BF16, G_O + 16384),
             at("rw2h1", [128, 8 * 1024], BF16, IFR_O + 16384)]
    hwb = [at("hwb0", [128, 8 * 512], BF16, R_O + 16 * 4096),
           at("hwb1", [128, 8 * 512], BF16, R_O + 18 * 4096)]

    ident = nc.alloc_sbuf_tensor("ident_sb", [128, 128], F32)
    onesf = nc.alloc_sbuf_tensor("onesf_sb", [1, 128], F32)
    onesb = nc.alloc_sbuf_tensor("onesb_sb", [1, 128], BF16)
    statnm = nc.alloc_sbuf_tensor("statnm", [128, TT], F32)
    statrs = nc.alloc_sbuf_tensor("statrs", [128, TT], F32)
    statpr = nc.alloc_sbuf_tensor("statpr", [128, TT], F32)
    wtok = nc.alloc_sbuf_tensor("wtok_sb", [128, TT * NE], F32)
    wT = nc.alloc_sbuf_tensor("wT_sb", [NE, T], BF16)
    rb1T = nc.alloc_sbuf_tensor("rb1T_sb", [128, HC * NE], F32)
    sb1T = nc.alloc_sbuf_tensor("sb1T_sb", [128, HC], F32)
    rb2sb = nc.alloc_sbuf_tensor("rb2_sb", [NE, D], BF16)
    sb2sb = nc.alloc_sbuf_tensor("sb2_sb", [1, D], BF16)
    rwwsb = nc.alloc_sbuf_tensor("rww_sb", [128, DC * NE], F32)
    rwbsb = nc.alloc_sbuf_tensor("rwb_sb", [1, NE], F32)
    Ssb = nc.alloc_sbuf_tensor("S_sb", [128, 10], F32)
    Ttot = nc.alloc_sbuf_tensor("Ttot_sb", [128, 10], F32)
    Trow = nc.alloc_sbuf_tensor("Trow_sb", [10, 128], F32)
    maskv = nc.alloc_sbuf_tensor("maskv_sb", [NCORES, 1], F32)
    sme = nc.alloc_sbuf_tensor("sme", [128, NE], F32)
    smr = nc.alloc_sbuf_tensor("smr", [128, NE], F32)
    smp = nc.alloc_sbuf_tensor("smp", [128, NE], F32)
    smq = nc.alloc_sbuf_tensor("smq", [128, NE], F32)
    sms = nc.alloc_sbuf_tensor("sms", [128, 8], F32)
    cst = nc.alloc_sbuf_tensor("cst", [128, 2], F32)

    ps = [nc.alloc_psum_tensor(f"ps{i}", [128, 1024], F32) for i in range(4)]

    sem_names = (["te", "ve", "ac", "gp", "st", "cc", "s_set", "s_setg", "s_wab", "s_ast", "s_ast1", "s_if", "s_dbg", "sxa", "sxb", "sxc", "sxd", "sxd1", "sxd2", "sw10", "sw11", "sw20", "sw21", "s_rw2", "s_exq"] + [f"sab{i}" for i in range(8)])
    cnt = {s: 0 for s in sem_names}
    streams = {k: [] for k in ["te", "ve", "ac", "gp", "sy"]}

    def op(stream, fn, waits=(), inc=None, by=1):
        w = [t for t in waits if t is not None]
        streams[stream].append((fn, w, inc, by))
        if inc is None:
            return None
        cnt[inc] += by
        return (inc, cnt[inc])

    def dma(queue, sem, out, in_, waits=()):
        stream = {"qs": "sy", "qg": "gp", "qa": "ac"}[queue]
        return op(stream, lambda e, o=out, i=in_: e.dma_start(out=o, in_=i),
                  waits, inc=sem, by=16)

    def sdma(out, in_, waits=()):
        return op("sy", lambda e, o=out, i=in_: e.dma_start(out=o, in_=i),
                  waits, inc="st", by=16)

    tok = {}

    dma("qs", "s_set", ident[:], ident_d)
    dma("qs", "s_set", onesf[:], onesf_d)
    dma("qg", "s_setg", onesb[:], onesb_d)
    dma("qs", "s_set", rwwsb.ap().rearrange("p (dc e) -> p dc e", dc=DC),
        rww_d.rearrange("(dc p) e -> p dc e", p=128))
    dma("qs", "s_set", rwbsb[:], rwb_d[None, :])
    dma("qs", "s_set", maskv[:], maskv_d[:, None])
    dma("qs", "s_set", rb1raw[:], rb1_d)
    dma("qs", "s_set", sb1raw[:], sb1_d[None, :])
    dma("qg", "s_setg", rb2sb[:], rb2_d)
    dma("qg", "s_setg", sb2sb[:], sb2_d[None, :])
    dma("qg", "s_setg", fre_sb.ap().rearrange("p (dc f) -> p dc f", dc=8),
        fre_d.rearrange("(dc p) f -> p dc f", p=128))
    dma("qg", "s_setg", fim_sb.ap().rearrange("p (dc f) -> p dc f", dc=8),
        fim_d.rearrange("(dc p) f -> p dc f", p=128))
    t_set = ("s_set", cnt["s_set"])
    t_setg = ("s_setg", cnt["s_setg"])
    t_ident = t_ones = t_rww = t_rwb = t_maskv = t_rb1raw = t_sb1raw = t_set
    t_onesb = t_rb2 = t_sb2 = t_fre = t_fim = t_setg
    t_zero = op("gp", lambda e: e.memset(zeros[:], 0.0), inc="gp")
    op("gp", lambda e: e.memset(cst[:, 0:1], EPS_LN), inc=None)
    t_cst = op("gp", lambda e: e.memset(cst[:, 1:2], EPS_FFT), inc="gp")

    for hc in range(HC):
        tk = op("te", lambda e, hc=hc: e.transpose(
            ps[3][:, (hc % 2) * 512:(hc % 2) * 512 + 8],
            rb1raw[0:NE, hc * 128:(hc + 1) * 128], ident[0:NE, 0:NE]),
            waits=[t_rb1raw, t_ident, tok.get(("rb1ev", hc - 2))], inc="te")
        tok[("rb1ev", hc)] = op("ve", lambda e, hc=hc: e.tensor_copy(
            rb1T[:, hc * 8:(hc + 1) * 8], ps[3][:, (hc % 2) * 512:(hc % 2) * 512 + 8]),
            waits=[tk], inc="ve")
    for hc in range(HC):
        tk = op("te", lambda e, hc=hc: e.transpose(
            ps[2][:, (hc % 2) * 512:(hc % 2) * 512 + 1],
            sb1raw[0:1, hc * 128:(hc + 1) * 128], ident[0:1, 0:1]),
            waits=[t_sb1raw, t_ident, tok.get(("sb1ev", hc - 2))], inc="te")
        tok[("sb1ev", hc)] = op("ve", lambda e, hc=hc: e.tensor_copy(
            sb1T[:, hc:hc + 1], ps[2][:, (hc % 2) * 512:(hc % 2) * 512 + 1]),
            waits=[tk], inc="ve")
    t_bias_tr = tok[("sb1ev", HC - 1)]

    ab_ev = {}
    ab_store_mark = {}
    w_drams = [wk_d, wv_d, wq_d]
    dram_plane = [2, 3, 4, 5, 0, 1]
    t_wsb = {}
    if phases >= 2:
        t_wsb[0] = dma("qg", "s_wab", wsb_g[0].ap().rearrange("p (ec d) -> p ec d", ec=8),
                       w_drams[0].rearrange("(ec p) d -> p ec d", p=128))
        t_wsb[1] = dma("qg", "s_wab", wsb_g[1].ap().rearrange("p (ec d) -> p ec d", ec=8),
                       w_drams[1].rearrange("(ec p) d -> p ec d", p=128))

    def emit_ab_unit(unit):
        proj, rem = divmod(unit, 16)
        pl, dc = divmod(rem, 8)
        fsb = fre_sb if pl == 0 else fim_sb
        t_f = t_fre if pl == 0 else t_fim
        dpl = dram_plane[proj * 2 + pl]
        even = (unit % 2 == 0)
        wbuf = wsb_g[proj % 2]
        t_mm = None
        for ec in range(8):
            for k, (n0, n1) in enumerate([(0, 512), (512, 513)]):
                if even:
                    tgt = (lambda e, n0=n0, n1=n1: ps[3][:, n0:n1])
                else:
                    tgt = (lambda e, k=k: ps[2][:, 512:1024] if k == 0 else ps[3][:, 513:514])
                t_mm = op("te", lambda e, tgt=tgt, ec=ec, dc=dc, fsb=fsb, n0=n0, n1=n1, wbuf=wbuf:
                          e.matmul(tgt(e),
                                   wbuf[:, ec * 1024 + dc * 128:ec * 1024 + (dc + 1) * 128],
                                   fsb[:, ec * NF + n0:ec * NF + n1],
                                   start=(ec == 0), stop=(ec == 7)),
                          waits=[t_wsb[proj], t_f, ab_ev.get(unit - 1),
                                 t_bias_tr if unit < 2 else None]
                          if ec == 0 and k == 0 else [],
                          inc="te" if (ec == 7 and k == 1) else None)
        slot = unit % 2
        ssem = ("s_ast", "s_ast1")[slot]
        war = (ssem, 32 * (unit // 2)) if unit >= 2 else None
        if even:
            t_ev = op("ac", lambda e, slot=slot: e.copy(ats[slot][:, 0:513], ps[3][:, 0:513]),
                      waits=[t_mm, war], inc="ac")
        else:
            op("ac", lambda e, slot=slot: e.copy(ats[slot][:, 0:512], ps[2][:, 512:1024]),
               waits=[t_mm, war], inc=None)
            t_ev = op("ac", lambda e, slot=slot: e.copy(ats[slot][:, 512:513], ps[3][:, 513:514]),
                      inc="ac")
        dma("qa", ssem, a_dram.ap()[dpl, dc * 128:(dc + 1) * 128, :], ats[slot][:],
            waits=[t_ev])
        dma("qa", ssem, a_nyq.ap()[dpl, dc * 128:(dc + 1) * 128], ats[slot][:, 512:513])
        ab_ev[unit] = t_ev
        if unit == 15:
            t_wsb[2] = dma("qg", "s_wab", wsb_g[0].ap().rearrange("p (ec d) -> p ec d", ec=8),
                           w_drams[2].rearrange("(ec p) d -> p ec d", p=128),
                           waits=[t_mm])
        if unit == 31:
            ab_store_mark["kv"] = [("s_ast", cnt["s_ast"]), ("s_ast1", cnt["s_ast1"])]
        if unit == 47:
            ab_store_mark["q"] = [("s_ast", cnt["s_ast"]), ("s_ast1", cnt["s_ast1"])]

    AB_IL = int(_os.environ.get("AB_IL", "1"))

    def _ab_tick(ttl):
        if phases < 2 or K_SUB == 1 or not AB_IL:
            return
        if 3 <= ttl <= 14:
            for u_ in range((ttl - 3) * 4, (ttl - 3) * 4 + 4):
                emit_ab_unit(u_)

    for ttl in range(TT if K_SUB != 1 else 0):
        s = ttl % 2
        t_x = dma("qs", ("sxa", "sxb")[s], xstg[s][:], x_d[ttl * 128:(ttl + 1) * 128, :],
                  waits=[tok.get(("lnxdone", ttl - 2))])
        t_sq = op("ac", lambda e, s=s: e.activation(xsq[:], xstg[s][:], AF.Square),
                  waits=[t_x, tok.get(("lnssq", ttl - 1))], inc="ac")
        t_ssq = op("ve", lambda e: e.reduce_sum(sms[:, 0:1], xsq[:], axis=AX),
                   waits=[t_sq], inc="ve")
        tok[("lnssq", ttl)] = t_ssq
        t_xs = op("ve", lambda e, s=s: e.reduce_sum(sms[:, 1:2], xstg[s][:], axis=AX),
                  waits=[t_x, t_ssq], inc="ve")
        t_nm = op("ve", lambda e, ttl=ttl: e.tensor_scalar_mul(
            statnm[:, ttl:ttl + 1], sms[:, 1:2], -1.0 / D), waits=[t_xs], inc="ve")
        t_m2 = op("ve", lambda e, ttl=ttl: e.tensor_mul(
            sms[:, 2:3], statnm[:, ttl:ttl + 1], statnm[:, ttl:ttl + 1]),
            waits=[t_nm], inc="ve")
        t_va = op("ve", lambda e: e.tensor_scalar_mul(sms[:, 3:4], sms[:, 0:1], 1.0 / D),
                  waits=[t_m2], inc="ve")
        t_vb = op("ve", lambda e: e.tensor_sub(sms[:, 4:5], sms[:, 3:4], sms[:, 2:3]),
                  waits=[t_va], inc="ve")
        t_ln = op("ac", lambda e: e.activation(sms[:, 5:6], sms[:, 4:5], AF.Ln, bias=cst[:, 0:1]),
                  waits=[t_vb, t_cst], inc="ac")
        t_rs = op("ac", lambda e, ttl=ttl: e.activation(
            statrs[:, ttl:ttl + 1], sms[:, 5:6], AF.Exp, scale=-0.5),
            waits=[t_ln], inc="ac")
        t_pr = op("ve", lambda e, ttl=ttl: e.tensor_mul(
            statpr[:, ttl:ttl + 1], statnm[:, ttl:ttl + 1], statrs[:, ttl:ttl + 1]),
            waits=[t_rs], inc="ve")
        tok[("pr", ttl)] = t_pr
        t_xn = op("ve", lambda e, s=s, ttl=ttl: e.tensor_scalar(
            xnf[:], xstg[s][:], statnm[:, ttl:ttl + 1], statrs[:, ttl:ttl + 1],
            ALU.add, ALU.mult),
            waits=[t_pr, tok.get(("lntr", ttl - 1))], inc="ve")
        tok[("lnxdone", ttl)] = t_xn
        if K_SUB == 2:
            tok[("lnevac", ttl)] = t_xn
            tok[("lnxnT", ttl)] = t_xn
            tok[("smdone", ttl)] = t_xn
            tok[("wT", ttl)] = t_xn
            tok[("w", ttl)] = t_xn
            tok[("lnrb", ttl)] = None
            tok[("lntr", ttl)] = None
            _ab_tick(ttl)
            continue
        rot = ps[ttl % 2]
        t_tr = None
        for dc in range(DC):
            t_tr = op("te", lambda e, dc=dc, rot=rot: e.transpose(
                rot[:, dc * 128:(dc + 1) * 128], xnf[:, dc * 128:(dc + 1) * 128], ident[:]),
                waits=[t_xn, t_ident, tok.get(("lnevac", ttl - 2)),
                       tok.get(("lnxnT", ttl - 2))] if dc == 0 else [],
                inc="te" if dc == DC - 1 else None)
        tok[("lntr", ttl)] = t_tr
        last_ac = None
        for dc in range(DC):
            last_ac = op("ac", lambda e, dc=dc, rot=rot, ttl=ttl: e.copy(
                xnT[:, dc * T + ttl * 128:dc * T + (ttl + 1) * 128],
                rot[:, dc * 128:(dc + 1) * 128]), waits=[t_tr] if dc == 0 else [],
                inc="ac" if dc == DC - 1 else None)
        last_ev = None
        for dc in range(DC):
            last_ev = op("ve", lambda e, dc=dc, rot=rot: e.tensor_copy(
                xnT32[:, dc * 128:(dc + 1) * 128], rot[:, dc * 128:(dc + 1) * 128]),
                waits=[last_ac, tok.get(("lnrb", ttl - 1))] if dc == 0 else [],
                inc="ve" if dc == DC - 1 else None)
        tok[("lnevac", ttl)] = last_ev
        tok[("lnxnT", ttl)] = last_ac
        if K_SUB == 3:
            tok[("smdone", ttl)] = last_ev
            tok[("wT", ttl)] = last_ev
            tok[("w", ttl)] = last_ev
            tok[("lnrb", ttl)] = t_tr
            _ab_tick(ttl)
            continue
        rps = ps[2]
        rc = (ttl % 2) * 256
        for dc in range(DC):
            op("te", lambda e, dc=dc, rps=rps, rc=rc: e.matmul(
                rps[:, rc:rc + NE], xnT32[:, dc * 128:(dc + 1) * 128],
                rwwsb[:, dc * NE:(dc + 1) * NE], start=(dc == 0), stop=False),
                waits=[last_ev, t_rww, t_bias_tr, tok.get(("smdone", ttl - 1))]
                if dc == 0 else [], inc=None)
        t_rb = op("te", lambda e, rps=rps, rc=rc: e.matmul(
            rps[:, rc:rc + NE], onesf[:], rwbsb[:], start=False, stop=True),
            waits=[t_ones, t_rwb], inc="te")
        tok[("lnrb", ttl)] = t_rb
        if K_SUB == 4:
            t_cp = op("ve", lambda e, rps=rps, rc=rc, ttl=ttl: e.tensor_copy(
                wtok[:, ttl * NE:(ttl + 1) * NE], rps[:, rc:rc + NE]), waits=[t_rb], inc="ve")
            tok[("smdone", ttl)] = t_cp
            tok[("wT", ttl)] = t_cp
            tok[("w", ttl)] = t_cp
            _ab_tick(ttl)
            continue
        t_cpr = op("ve", lambda e, rps=rps, rc=rc: e.tensor_copy(
            smr[:], rps[:, rc:rc + NE]), waits=[t_rb], inc="ve")
        t_mx = op("ve", lambda e: e.reduce_max(
            sms[:, 6:7], smr[:], axis=AX, negate=True), waits=[t_cpr], inc="ve")
        t_ex = op("ac", lambda e: e.activation(
            sme[:], smr[:], AF.Exp, bias=sms[:, 6:7]), waits=[t_mx], inc="ac")
        t_sm = op("ve", lambda e: e.reduce_sum(sms[:, 7:8], sme[:], axis=AX),
                  waits=[t_ex], inc="ve")
        t_rc = op("ve", lambda e: e.reciprocal(sms[:, 0:1], sms[:, 7:8]),
                  waits=[t_sm], inc="ve")
        t_p = op("ve", lambda e: e.tensor_scalar_mul(smp[:], sme[:], sms[:, 0:1]),
                 waits=[t_rc], inc="ve")
        t_m1 = op("ve", lambda e: e.reduce_max(sms[:, 1:2], smp[:], axis=AX),
                  waits=[t_p], inc="ve")
        t_eq = op("ve", lambda e: e.tensor_single_scalar(
            smq[:], smp[:], sms[:, 1:2], ALU.is_equal), waits=[t_m1], inc="ve")
        t_mk = op("ve", lambda e: e.scalar_tensor_tensor(
            smq[:], smq[:], NEG_BIG, smp[:], ALU.mult, ALU.add), waits=[t_eq], inc="ve")
        t_m2b = op("ve", lambda e: e.reduce_max(sms[:, 2:3], smq[:], axis=AX),
                   waits=[t_mk], inc="ve")
        t_ge = op("ve", lambda e: e.tensor_single_scalar(
            smq[:], smp[:], sms[:, 2:3], ALU.is_ge), waits=[t_m2b], inc="ve")
        t_w = op("ve", lambda e, ttl=ttl: e.tensor_mul(
            wtok[:, ttl * NE:(ttl + 1) * NE], smp[:], smq[:]), waits=[t_ge], inc="ve")
        tok[("w", ttl)] = t_w
        t_wt = op("te", lambda e, rps=rps, rc=rc, ttl=ttl: e.transpose(
            rps[0:NE, rc + 8:rc + 136], wtok[:, ttl * NE:(ttl + 1) * NE], ident[:]),
            waits=[t_w], inc="te")
        t_wte = op("ve", lambda e, rps=rps, rc=rc, ttl=ttl: e.tensor_copy(
            wT[:, ttl * 128:(ttl + 1) * 128], rps[0:NE, rc + 8:rc + 136]), waits=[t_wt], inc="ve")
        tok[("smdone", ttl)] = t_wte
        tok[("wT", ttl)] = t_wte
        _ab_tick(ttl)

    t_ln_all = tok.get(("lnevac", TT - 1))
    if phases >= 2 and not AB_IL and K_SUB != 1:
        for u_ in range(48):
            emit_ab_unit(u_)
    t_ln_te = tok.get(("lnrb", TT - 1))

    if phases >= 2:
        t_ab_last_ev = [ab_ev[46], ab_ev[47]]

    if phases >= 3:
        def emit_a1_load(fc):
            fw = FCW[fc]
            for i, pl in enumerate([2, 3, 4, 5]):
                slot = i + 4 * (fc % 2)
                if fw == 1:
                    t_l = None
                    for dc in range(DC):
                        t_l = dma("qg", f"sab{slot}", abuf[slot].ap()[:, dc:dc + 1],
                                  a_nyq.ap()[pl, dc * 128:(dc + 1) * 128],
                                  waits=ab_store_mark["kv"] +
                                        [tok.get(("a1mmdone", fc - 2, i))] if dc == 0 else [])
                    tok[("a1w", fc, i)] = t_l
                else:
                    tok[("a1w", fc, i)] = dma(
                        "qg", f"sab{slot}", abuf[slot].ap()[:, 0:8 * fw].rearrange("p (dc f) -> p dc f", dc=DC),
                        a_dram.ap()[pl].rearrange("(dc p) f -> p dc f", p=128)[:, :, fc * 128:fc * 128 + fw],
                        waits=ab_store_mark["kv"] + [tok.get(("a1mmdone", fc - 2, i))])

        emit_a1_load(0)
        emit_a1_load(1)

        a1ev = {}
        a1u = 0
        for fc in range(FC):
            fw = FCW[fc]
            for tch in range(2):
                t0 = tch * 1024
                plane_ev = []
                for i in range(4):
                    slot = i + 4 * (fc % 2)
                    rot = ps[a1u % 4]
                    t_mm = None
                    for dc in range(DC):
                        for nq in range(2):
                            t_mm = op("te", lambda e, rot=rot, fw=fw, slot=slot, dc=dc, nq=nq, t0=t0:
                                      e.matmul(rot[0:fw, nq * 512:(nq + 1) * 512],
                                               abuf[slot][:, dc * fw:(dc + 1) * fw],
                                               xnT[:, dc * T + t0 + nq * 512:dc * T + t0 + (nq + 1) * 512],
                                               start=(dc == 0), stop=(dc == DC - 1)),
                                      waits=[tok[("a1w", fc, i)], a1ev.get(a1u - 4)] +
                                            ([tok[("lnxnT", TT - 1)], tok[("lnevac", TT - 1)]]
                                             if a1u < 2 else []) +
                                            ([tok[("smdone", TT - 1)]] + t_ab_last_ev
                                             if a1u in (2, 3) else [])
                                      if dc == 0 and nq == 0 else [],
                                      inc="te" if (dc == DC - 1 and nq == 1) else None)
                    if tch == 1:
                        tok[("a1mmdone", fc, i)] = t_mm
                        if i == 3 and fc + 2 < FC:
                            emit_a1_load(fc + 2)
                    prev_c = tok.get(("a1c", fc, tch - 1, 1)) if tch == 1 else \
                        tok.get(("a1c", fc - 1, 1, 1))
                    t_ev = op("ac", lambda e, i=i, rot=rot, fw=fw: e.copy(
                        a1t[i][0:fw, :], rot[0:fw, :]),
                        waits=[t_mm, prev_c if i == 0 else None], inc="ac")
                    plane_ev.append(t_ev)
                    a1ev[a1u] = t_ev
                    a1u += 1
                t_s1 = op("ac", lambda e, fw=fw: e.activation(a1t[4][0:fw, :], a1t[0][0:fw, :], AF.Square),
                          waits=[plane_ev[0]], inc="ac")
                t_s2 = op("ac", lambda e, fw=fw: e.activation(a1t[5][0:fw, :], a1t[1][0:fw, :], AF.Square),
                          waits=[plane_ev[1], t_s1], inc="ac")
                t_ss = op("ve", lambda e, fw=fw: e.tensor_add(a1t[4][0:fw, :], a1t[4][0:fw, :], a1t[5][0:fw, :]),
                          waits=[t_s2], inc="ve")
                t_l1 = op("ac", lambda e, fw=fw: e.activation(a1t[4][0:fw, :], a1t[4][0:fw, :], AF.Ln),
                          waits=[t_ss], inc="ac")
                t_e1 = op("ac", lambda e, fw=fw: e.activation(a1t[4][0:fw, :], a1t[4][0:fw, :], AF.Exp, scale=0.5),
                          waits=[t_l1], inc="ac")
                t_l2 = op("ac", lambda e, fw=fw: e.activation(a1t[4][0:fw, :], a1t[4][0:fw, :], AF.Ln, bias=cst[0:fw, 1:2]),
                          waits=[t_l1, t_e1], inc="ac")
                t_e2 = op("ac", lambda e, fw=fw: e.activation(a1t[4][0:fw, :], a1t[4][0:fw, :], AF.Exp, scale=-1.0),
                          waits=[t_l2], inc="ac")
                t_n1 = op("ve", lambda e, fw=fw: e.tensor_mul(a1t[0][0:fw, :], a1t[0][0:fw, :], a1t[4][0:fw, :]),
                          waits=[t_e2], inc="ve")
                t_n2 = op("ve", lambda e, fw=fw: e.tensor_mul(a1t[1][0:fw, :], a1t[1][0:fw, :], a1t[4][0:fw, :]),
                          waits=[t_n1], inc="ve")
                t_p1 = op("ve", lambda e, fw=fw: e.tensor_mul(a1t[5][0:fw, :], a1t[0][0:fw, :], a1t[2][0:fw, :]),
                          waits=[t_n2, plane_ev[2]], inc="ve")
                t_p2 = op("ve", lambda e, fw=fw: e.tensor_mul(a1t[6][0:fw, :], a1t[1][0:fw, :], a1t[3][0:fw, :]),
                          waits=[t_p1, plane_ev[3]], inc="ve")
                t_p3 = op("ve", lambda e, fw=fw: e.tensor_sub(a1t[5][0:fw, :], a1t[5][0:fw, :], a1t[6][0:fw, :]),
                          waits=[t_p2], inc="ve")
                t_c1 = op("ve", lambda e, fw=fw, fc=fc, t0=t0, tch=tch: e.tensor_tensor_scan(
                    cre[fc][0:fw, t0:t0 + 1024], zeros[0:fw, :], a1t[5][0:fw, :],
                    0.0 if tch == 0 else cre[fc][0:fw, t0 - 1:t0],
                    ALU.add, ALU.add),
                    waits=[t_p3, t_zero, tok.get(("a1c", fc, tch - 1, 0))], inc="ve")
                tok[("a1c", fc, tch, 0)] = t_c1
                t_q1 = op("ve", lambda e, fw=fw: e.tensor_mul(a1t[6][0:fw, :], a1t[0][0:fw, :], a1t[3][0:fw, :]),
                          waits=[t_c1], inc="ve")
                t_q2 = op("ve", lambda e, fw=fw: e.tensor_mul(a1t[7][0:fw, :], a1t[1][0:fw, :], a1t[2][0:fw, :]),
                          waits=[t_q1], inc="ve")
                t_q3 = op("ve", lambda e, fw=fw: e.tensor_add(a1t[6][0:fw, :], a1t[6][0:fw, :], a1t[7][0:fw, :]),
                          waits=[t_q2], inc="ve")
                t_c2 = op("ve", lambda e, fw=fw, fc=fc, t0=t0, tch=tch: e.tensor_tensor_scan(
                    cim[fc][0:fw, t0:t0 + 1024], zeros[0:fw, :], a1t[6][0:fw, :],
                    0.0 if tch == 0 else cim[fc][0:fw, t0 - 1:t0],
                    ALU.add, ALU.add),
                    waits=[t_q3, tok.get(("a1c", fc, tch - 1, 1))], inc="ve")
                tok[("a1c", fc, tch, 1)] = t_c2
        t_a1_last_ev = [a1ev[a1u - 1], a1ev[a1u - 2], a1ev[a1u - 3], a1ev[a1u - 4]]


    if phases >= 4:
        t_tz = op("ve", lambda e: e.memset(Ttot[:], 0.0), inc="ve")
        t_tt = t_tz
        for fc in range(FC):
            fw = FCW[fc]
            t_tt = op("ve", lambda e, fc=fc, fw=fw: e.tensor_copy(
                Ttot[0:fw, fc:fc + 1], cre[fc][0:fw, T - 1:T]),
                waits=[tok[("a1c", fc, 1, 0)], t_tt], inc="ve")
            t_tt = op("ve", lambda e, fc=fc, fw=fw: e.tensor_copy(
                Ttot[0:fw, 5 + fc:6 + fc], cim[fc][0:fw, T - 1:T]),
                waits=[tok[("a1c", fc, 1, 1)], t_tt], inc="ve")
        t_trw = op("ve", lambda e: e.tensor_copy(Trow[:], ps[3][0:10, 0:128]),
                   waits=[("sxd1", 1)], inc="ve")
        t_S = op("ve", lambda e: e.tensor_copy(Ssb[:], ps[3][:, 512:522]),
                 waits=[("sxd2", 1)], inc="ve")


    if phases >= 5:
        def emit_a2_load(fc):
            fw = FCW[fc]
            for i in range(2):
                slot = i + 2 * (fc % 2)
                w0 = ab_store_mark["q"] + [tok.get(("a2mmdone", fc - 2, i)),
                      tok.get(("a1mmdone", FC - 1, i + 2 * (fc % 2))) if fc < 2 else None]
                if fw == 1:
                    t_l = None
                    for dc in range(DC):
                        t_l = dma("qa", f"sab{slot}", abuf[slot].ap()[:, dc:dc + 1],
                                  a_nyq.ap()[i, dc * 128:(dc + 1) * 128],
                                  waits=w0 if dc == 0 else [])
                    tok[("a2w", fc, i)] = t_l
                else:
                    tok[("a2w", fc, i)] = dma(
                        "qa", f"sab{slot}", abuf[slot].ap()[:, 0:8 * fw].rearrange("p (dc f) -> p dc f", dc=DC),
                        a_dram.ap()[i].rearrange("(dc p) f -> p dc f", p=128)[:, :, fc * 128:fc * 128 + fw],
                        waits=w0)

        emit_a2_load(0)
        emit_a2_load(1)
        a2ev = {}
        a2u = 0
        for fc in range(FC):
            fw = FCW[fc]
            for tch in range(2):
                t0 = tch * 1024
                plane_ev = []
                for i in range(2):
                    slot = i + 2 * (fc % 2)
                    rot = ps[a2u % 3]
                    t_mm = None
                    for dc in range(DC):
                        for nq in range(2):
                            t_mm = op("te", lambda e, rot=rot, fw=fw, slot=slot, dc=dc, nq=nq, t0=t0:
                                      e.matmul(rot[0:fw, nq * 512:(nq + 1) * 512],
                                               abuf[slot][:, dc * fw:(dc + 1) * fw],
                                               xnT[:, dc * T + t0 + nq * 512:dc * T + t0 + (nq + 1) * 512],
                                               start=(dc == 0), stop=(dc == DC - 1)),
                                      waits=[tok[("a2w", fc, i)], a2ev.get(a2u - 3)] +
                                            (t_a1_last_ev if a2u < 4 else [])
                                      if dc == 0 and nq == 0 else [],
                                      inc="te" if (dc == DC - 1 and nq == 1) else None)
                    if tch == 1:
                        tok[("a2mmdone", fc, i)] = t_mm
                        if i == 1 and fc + 2 < FC:
                            emit_a2_load(fc + 2)
                    prev_g = tok.get(("gdone", fc, tch - 1)) if tch == 1 else \
                        tok.get(("gdone", fc - 1, 1))
                    t_ev = op("ac", lambda e, i=i, rot=rot, fw=fw: e.copy(
                        fqt[i][0:fw, :], rot[0:fw, :]),
                        waits=[t_mm, prev_g if i == 0 else None], inc="ac")
                    plane_ev.append(t_ev)
                    a2ev[a2u] = t_ev
                    a2u += 1
                t_s1 = op("ac", lambda e, fw=fw: e.activation(a2scr[0][0:fw, :], fqt[0][0:fw, :], AF.Square),
                          waits=[plane_ev[0], tok.get(("gdone", fc, tch - 1)),
                                 tok.get(("gdone", fc - 1, 1))], inc="ac")
                t_s2 = op("ac", lambda e, fw=fw: e.activation(a2scr[1][0:fw, :], fqt[1][0:fw, :], AF.Square),
                          waits=[plane_ev[1], t_s1], inc="ac")
                t_ss = op("ve", lambda e, fw=fw: e.tensor_add(a2scr[0][0:fw, :], a2scr[0][0:fw, :], a2scr[1][0:fw, :]),
                          waits=[t_s2], inc="ve")
                t_l1 = op("ac", lambda e, fw=fw: e.activation(a2scr[0][0:fw, :], a2scr[0][0:fw, :], AF.Ln),
                          waits=[t_ss], inc="ac")
                t_e1 = op("ac", lambda e, fw=fw: e.activation(a2scr[0][0:fw, :], a2scr[0][0:fw, :], AF.Exp, scale=0.5),
                          waits=[t_l1], inc="ac")
                t_l2 = op("ac", lambda e, fw=fw: e.activation(a2scr[0][0:fw, :], a2scr[0][0:fw, :], AF.Ln, bias=cst[0:fw, 1:2]),
                          waits=[t_e1], inc="ac")
                t_e2 = op("ac", lambda e, fw=fw: e.activation(a2scr[0][0:fw, :], a2scr[0][0:fw, :], AF.Exp, scale=-1.0),
                          waits=[t_l2], inc="ac")
                t_n1 = op("ve", lambda e, fw=fw: e.tensor_mul(fqt[2][0:fw, :], fqt[0][0:fw, :], a2scr[0][0:fw, :]),
                          waits=[t_e2], inc="ve")
                t_n2 = op("ve", lambda e, fw=fw: e.tensor_mul(fqt[3][0:fw, :], fqt[1][0:fw, :], a2scr[0][0:fw, :]),
                          waits=[t_n1], inc="ve")
                jre, jim = fc, 5 + fc
                t_a = op("ve", lambda e, fw=fw, fc=fc, t0=t0, jre=jre: e.tensor_scalar_add(
                    fqt[0][0:fw, :], cre[fc][0:fw, t0:t0 + 1024], Ssb[0:fw, jre:jre + 1]),
                    waits=[t_n2, t_S], inc="ve")
                t_b = op("ve", lambda e, fw=fw, fc=fc, t0=t0, jim=jim: e.tensor_scalar_add(
                    fqt[1][0:fw, :], cim[fc][0:fw, t0:t0 + 1024], Ssb[0:fw, jim:jim + 1]),
                    waits=[t_a], inc="ve")
                t_u1 = op("ve", lambda e, fw=fw: e.tensor_mul(a2scr[0][0:fw, :], fqt[0][0:fw, :], fqt[2][0:fw, :]),
                          waits=[t_b], inc="ve")
                t_u2 = op("ve", lambda e, fw=fw: e.tensor_mul(a2scr[1][0:fw, :], fqt[1][0:fw, :], fqt[3][0:fw, :]),
                          waits=[t_u1], inc="ve")
                t_u3 = op("ve", lambda e, fw=fw: e.tensor_add(a2scr[0][0:fw, :], a2scr[0][0:fw, :], a2scr[1][0:fw, :]),
                          waits=[t_u2], inc="ve")
                t_g1 = op("ve", lambda e, fw=fw, fc=fc, t0=t0: e.tensor_copy(
                    gre[fc][0:fw, t0:t0 + 1024], a2scr[0][0:fw, :]), waits=[t_u3], inc="ve")
                t_v1 = op("ve", lambda e, fw=fw: e.tensor_mul(a2scr[1][0:fw, :], fqt[1][0:fw, :], fqt[2][0:fw, :]),
                          waits=[t_g1], inc="ve")
                t_v2 = op("ve", lambda e, fw=fw: e.tensor_mul(a2scr[0][0:fw, :], fqt[0][0:fw, :], fqt[3][0:fw, :]),
                          waits=[t_v1], inc="ve")
                t_v3 = op("ve", lambda e, fw=fw: e.tensor_sub(a2scr[1][0:fw, :], a2scr[1][0:fw, :], a2scr[0][0:fw, :]),
                          waits=[t_v2], inc="ve")
                t_g2 = op("ve", lambda e, fw=fw, fc=fc, t0=t0: e.tensor_copy(
                    gim[fc][0:fw, t0:t0 + 1024], a2scr[1][0:fw, :]), waits=[t_v3], inc="ve")
                tok[("gdone", fc, tch)] = t_g2
                if fc == 0 and tch == 0:
                    op("te", lambda e: e.transpose(ps[3][0:10, 0:128], Ttot[:, 0:10], ident[:]),
                       waits=[t_tt, tok[("smdone", TT - 1)]], inc="sxd1")
                if fc == 0 and tch == 1:
                    t_cin = dma("qs", "s_exq",
                                cc_in.ap().rearrange("o (j f) -> (o j) f", j=10), Trow[:],
                                waits=[t_trw])
                    if _os.environ.get("K_NOCC") == "1":
                        t_cc = None
                        for _r in range(NCORES):
                            t_cc = op("gp", lambda e, _r=_r: e.dma_start(
                                out=cc_out.ap()[_r:_r + 1, :], in_=cc_in.ap()),
                                waits=[t_cin] if _r == 0 else [], inc="cc", by=16)
                    else:
                        t_cc = op("gp", lambda e: e.collective_compute(
                            "AllGather", ALU.bypass, replica_groups=[list(range(NCORES))],
                            ins=[cc_in.ap().opt()], outs=[cc_out.ap().opt()]),
                            waits=[t_cin], inc="cc", by=1)
                    t_gth = dma("qs", "s_exq", gath[:], cc_out.ap(),
                                waits=[t_cc, tok[("lnxdone", TT - 1)]])
                    for j in range(10):
                        op("te", lambda e, j=j: e.matmul(
                            ps[3][:, 512 + j:513 + j], gath[0:NCORES, j * 128:(j + 1) * 128],
                            maskv[:], start=True, stop=True),
                            waits=[t_gth, t_maskv, t_trw] if j == 0 else [],
                            inc="sxd2" if j == 9 else None)
        t_g_all = tok[("gdone", FC - 1, 1)]

        dma("qg", "s_if", ifre_sb.ap().rearrange("p (fc d) -> p fc d", fc=FC),
            ifre_d.rearrange("(fc p) d -> p fc d", p=128)[:, 0:FC, :],
            waits=[t_g_all])
        t_ifim = dma("qg", "s_if", ifim_sb.ap().rearrange("p (fc d) -> p fc d", fc=FC),
                     ifim_d.rearrange("(fc p) d -> p fc d", p=128)[:, 0:FC, :])
        t_ifre = t_ifim


    if phases >= 6:
        xstg4 = xstg + [xsq, xnf]
        for ttl in range(TT):
            s = ttl % 4
            t_x = dma("qs", ("sxa", "sxb", "sxc", "sxd")[s], xstg4[s][:],
                      x_d[ttl * 128:(ttl + 1) * 128, :],
                      waits=[t_g_all if ttl < 4 else None, tok.get(("irfr1", ttl - 4))])
            rot = ps[ttl % 4]
            t_mm = None
            first = True
            for nd in range(2):
                for fc in range(FC):
                    fw = FCW[fc]
                    for pl in range(2):
                        gsb = gre[fc] if pl == 0 else gim[fc]
                        isb = ifre_sb if pl == 0 else ifim_sb
                        t_mm = op("te", lambda e, rot=rot, nd=nd, fc=fc, fw=fw, gsb=gsb, isb=isb, ttl=ttl, pl=pl:
                                  e.matmul(rot[:, nd * 512:(nd + 1) * 512],
                                           gsb[0:fw, ttl * 128:(ttl + 1) * 128],
                                           isb[0:fw, fc * 1024 + nd * 512:fc * 1024 + (nd + 1) * 512],
                                           start=(fc == 0 and pl == 0), stop=(fc == FC - 1 and pl == 1)),
                                  waits=[t_g_all, t_ifim, tok.get(("irfev", ttl - 2)),
                                         a2ev.get(a2u - 1), a2ev.get(a2u - 2)]
                                  if first else [],
                                  inc="te" if (nd == 1 and fc == FC - 1 and pl == 1) else None)
                        first = False
            t_xn2 = op("ve", lambda e, s=s, ttl=ttl: e.tensor_scalar(
                a2scr[0][:], xstg4[s][:], statrs[:, ttl:ttl + 1], statpr[:, ttl:ttl + 1],
                ALU.mult, ALU.add),
                waits=[t_x, tok[("pr", ttl)], tok.get(("irfr1", ttl - 1))], inc="ve")
            t_r1 = op("ve", lambda e, s=s, ttl=ttl: e.tensor_add(
                outp[ttl][:], xstg4[s][:], a2scr[0][:]), waits=[t_xn2], inc="ve")
            tok[("irfr1", ttl)] = t_r1
            t_r2 = op("ve", lambda e, rot=rot, ttl=ttl: e.tensor_add(
                outp[ttl][:], outp[ttl][:], rot[:]), waits=[t_r1, t_mm], inc="ve")
            tok[("irfev", ttl)] = t_r2
        t_irf_all = tok[("irfev", TT - 1)]


    if phases >= 7:
        def w1_dram(e):
            return rw1_d[e] if e < NE else sw1_d

        def w2_dram(e):
            return rw2_d[e] if e < NE else sw2_d

        NU = 2 * NEXP
        h1cnt = 0
        ucnt = 0
        ffn_mm = {}
        gel_by_cnt = {}
        uev_by_cnt = {}
        umm_by = {}
        w1tok = {}
        w2tok = {}

        def emit_ffn_load(u):
            e, half = divmod(u, 2)
            s = u % 2
            war = [ffn_mm.get(u - 2), t_irf_all if u < 2 else None]
            w1tok[(u, 0)] = dma(
                "qg", ("sw10", "sw11")[s],
                rw1hb[s].ap().rearrange("p (dc h) -> p dc h", dc=DC)[:, :, 0:512],
                w1_dram(e).rearrange("(dc p) h -> p dc h", p=128)[:, :, half * 1024:half * 1024 + 512],
                waits=war)
            w1tok[(u, 1)] = dma(
                "qg", ("sw10", "sw11")[s],
                rw1hb[s].ap().rearrange("p (dc h) -> p dc h", dc=DC)[:, :, 512:1024],
                w1_dram(e).rearrange("(dc p) h -> p dc h", p=128)[:, :, half * 1024 + 512:half * 1024 + 1024])
            w2tok[u] = dma(
                "qg", ("sw20", "sw21")[s],
                rw2hb[s].ap().rearrange("p (hc d) -> p hc d", hc=DC),
                w2_dram(e).rearrange("(hc p) d -> p hc d", p=128)[:, half * 8:(half + 1) * 8, :],
                waits=war)

        emit_ffn_load(0)
        emit_ffn_load(1)
        for u in range(NU):
            e, half = divmod(u, 2)
            s = u % 2
            last_e8 = (e == NEXP - 1 and half == 1)
            t_mm = None
            for step in range(5):
                if step < 4:
                    tch4 = step
                    g = u * 4 + tch4
                    b = tch4 % 2
                    t0 = tch4 * 512
                    for hc8 in range(8):
                        hcg = half * 8 + hc8
                        pa = (ps[0], ps[1])[(h1cnt // 2) % 2]
                        c0 = (h1cnt % 2) * 512
                        t_hmm = None
                        for dc in range(DC):
                            w0 = []
                            if dc == 0:
                                w0 = [w1tok[(u, hc8 // 4)],
                                      gel_by_cnt.get(h1cnt - 4),
                                      t_irf_all if u == 0 and tch4 == 0 else None]
                            t_hmm = op("te", lambda e_, pa=pa, c0=c0, s=s, dc=dc, hc8=hc8, t0=t0:
                                       e_.matmul(pa[:, c0:c0 + 512],
                                                 rw1hb[s][:, dc * 1024 + hc8 * 128:dc * 1024 + (hc8 + 1) * 128],
                                                 xnT[:, dc * T + t0:dc * T + t0 + 512],
                                                 start=(dc == 0), stop=(dc == DC - 1)),
                                       waits=w0, inc="te" if dc == DC - 1 else None)
                        if e < NE:
                            bias_ap = (lambda hcg=hcg, e=e: rb1T[:, hcg * NE + e:hcg * NE + e + 1])
                        else:
                            bias_ap = (lambda hcg=hcg: sb1T[:, hcg:hcg + 1])
                        t_g = op("ac", lambda e_, pa=pa, c0=c0, b=b, hc8=hc8, bias_ap=bias_ap:
                                 e_.activation(hwb[b][:, hc8 * 512:(hc8 + 1) * 512],
                                               pa[:, c0:c0 + 512], AF.Gelu, bias=bias_ap()),
                                 waits=[t_hmm, t_bias_tr, tok[("rb1ev", HC - 1)],
                                        umm_by.get(g - 2) if hc8 == 0 else None], inc="ac")
                        gel_by_cnt[h1cnt] = t_g
                        h1cnt += 1
                    tok[("ffn_gel", g)] = t_g
                if step > 0:
                    tc = step - 1
                    g2 = u * 4 + tc
                    b2 = tc % 2
                    for tl in range(4):
                        ttl = tc * 4 + tl
                        pb = ps[2 + ucnt % 2]
                        t_mm = None
                        for hc8 in range(8):
                            for nd in range(2):
                                w0 = []
                                if hc8 == 0 and nd == 0:
                                    w0 = [w2tok[u], tok[("ffn_gel", g2)],
                                          uev_by_cnt.get(ucnt - 2)]
                                t_mm = op("te", lambda e_, pb=pb, b2=b2, s=s, hc8=hc8, nd=nd, tl=tl:
                                          e_.matmul(pb[:, nd * 512:(nd + 1) * 512],
                                                    hwb[b2][:, hc8 * 512 + tl * 128:hc8 * 512 + (tl + 1) * 128],
                                                    rw2hb[s][:, hc8 * 1024 + nd * 512:hc8 * 1024 + (nd + 1) * 512],
                                                    start=(hc8 == 0),
                                                    stop=(hc8 == 7 and not last_e8)),
                                          waits=w0,
                                          inc="te" if (hc8 == 7 and nd == 1 and not last_e8) else None)
                        if last_e8:
                            op("te", lambda e_, pb=pb, ttl=ttl: e_.matmul(
                                pb[:, 0:512], wT[:, ttl * 128:(ttl + 1) * 128],
                                rb2sb[:, 0:512], start=False, stop=False),
                                waits=[tok[("wT", ttl)], t_rb2], inc=None)
                            op("te", lambda e_, pb=pb, ttl=ttl: e_.matmul(
                                pb[:, 512:1024], wT[:, ttl * 128:(ttl + 1) * 128],
                                rb2sb[:, 512:1024], start=False, stop=False), inc=None)
                            op("te", lambda e_, pb=pb: e_.matmul(
                                pb[:, 0:512], onesb[:], sb2sb[:, 0:512],
                                start=False, stop=False), waits=[t_onesb, t_sb2], inc=None)
                            t_mm = op("te", lambda e_, pb=pb: e_.matmul(
                                pb[:, 512:1024], onesb[:], sb2sb[:, 512:1024],
                                start=False, stop=True), inc="te")
                        if e < NE:
                            t_ev = op("ve", lambda e_, pb=pb, ttl=ttl, e=e: e_.scalar_tensor_tensor(
                                outp[ttl][:], pb[:], wtok[:, ttl * NE + e:ttl * NE + e + 1],
                                outp[ttl][:], ALU.mult, ALU.add),
                                waits=[t_mm, tok[("w", ttl)] if u == 0 else None,
                                       tok[("irfev", ttl)] if u == 0 else None], inc="ve")
                        else:
                            t_ev = op("ve", lambda e_, pb=pb, ttl=ttl: e_.tensor_add(
                                outp[ttl][:], outp[ttl][:], pb[:]),
                                waits=[t_mm], inc="ve")
                        uev_by_cnt[ucnt] = t_ev
                        ucnt += 1
                        if last_e8:
                            sdma(y_d[ttl * 128:(ttl + 1) * 128, :], outp[ttl][:], waits=[t_ev])
                    umm_by[g2] = t_mm
            ffn_mm[u] = t_mm
            if u + 2 < NU:
                emit_ffn_load(u + 2)



    if debug:
        dbg_c = nc.dram_tensor("dbg_c", [2 * FC, 128, T], F32, kind="ExternalOutput").ap()
        dbg_w = nc.dram_tensor("dbg_w", [128, TT * NE], F32, kind="ExternalOutput").ap()
        dbg_stat = nc.dram_tensor("dbg_stat", [128, 3 * TT], F32, kind="ExternalOutput").ap()
        dbg_S = nc.dram_tensor("dbg_S", [128, 10], F32, kind="ExternalOutput").ap()
        if K_SUB == 1:
            sdma(dbg_w, rb1T[:], waits=[tok[("rb1ev", HC - 1)]])
        if phases >= 1 and K_SUB != 1:
            sdma(dbg_w, wtok[:], waits=[tok[("w", TT - 1)]])
            sdma(dbg_stat[:, 0:TT], statnm[:], waits=[tok[("pr", TT - 1)]])
            sdma(dbg_stat[:, TT:2 * TT], statrs[:], waits=[tok[("pr", TT - 1)]])
            sdma(dbg_stat[:, 2 * TT:3 * TT], statpr[:], waits=[tok[("pr", TT - 1)]])
        if phases >= 3 and phases < 6:
            for fc in range(FC):
                fw = FCW[fc]
                sdma(dbg_c[fc, 0:fw, :], cre[fc][0:fw, :], waits=[tok[("a1c", fc, 1, 0)]])
                sdma(dbg_c[FC + fc, 0:fw, :], cim[fc][0:fw, :], waits=[tok[("a1c", fc, 1, 1)]])
        if phases >= 4:
            sdma(dbg_S, Ssb[:], waits=[t_S])
        if phases >= 5:
            dbg_g = nc.dram_tensor("dbg_g", [2 * FC, 128, T], F32, kind="ExternalOutput").ap()
            for fc in range(FC):
                fw = FCW[fc]
                dma("qg", "s_dbg", dbg_g[fc, 0:fw, :], gre[fc][0:fw, :],
                    waits=[tok[("gdone", FC - 1, 1)]])
                dma("qg", "s_dbg", dbg_g[FC + fc, 0:fw, :], gim[fc][0:fw, :],
                    waits=[tok[("gdone", FC - 1, 1)]])
        if phases >= 6 and phases < 7:
            for ttl in range(TT):
                sdma(y_d[ttl * 128:(ttl + 1) * 128, :], outp[ttl][:],
                     waits=[tok[("irfev", ttl)]])
    op("sy", lambda e: e.nop(), waits=[("st", cnt["st"])])

    import contextlib
    with contextlib.ExitStack() as ctx:
        sems = {s: ctx.enter_context(nc.semaphore(s)) for s in sem_names}
        block = ctx.enter_context(nc.Block())

        def emit(stream_key):
            def body(eng):
                seen = {s: 0 for s in sem_names}
                for fn, waits, inc, by in streams[stream_key]:
                    for (s, v) in waits:
                        if v > seen[s]:
                            eng.wait_ge(sems[s], v)
                            seen[s] = v
                    ins = fn(eng)
                    if inc is not None:
                        ins.then_inc(sems[inc], by)
            return body

        block.tensor(emit("te"))
        block.vector(emit("ve"))
        block.scalar(emit("ac"))
        block.gpsimd(emit("gp"))
        block.sync(emit("sy"))

    return nc


_PROG_CACHE = {}


def _consts():
    e = np.arange(D, dtype=np.float64)
    f = np.arange(NF, dtype=np.float64)
    ang = 2.0 * np.pi * np.outer(e, f) / D
    fre = np.cos(ang)
    fim = -np.sin(ang)
    fim[:, 0] = 0.0
    fim[:, NF - 1] = 0.0
    dd = np.arange(D, dtype=np.float64)
    angi = 2.0 * np.pi * np.outer(f, dd) / D
    cf = np.full(NF, 2.0)
    cf[0] = 1.0
    cf[NF - 1] = 1.0
    ifre = (cf[:, None] / D) * np.cos(angi)
    ifim = -(cf[:, None] / D) * np.sin(angi)
    ifre_p = np.zeros((640, D))
    ifim_p = np.zeros((640, D))
    ifre_p[:NF] = ifre
    ifim_p[:NF] = ifim
    bf = ml_dtypes.bfloat16
    return (fre.astype(bf), fim.astype(bf), ifre_p.astype(bf), ifim_p.astype(bf))


def kernel(L0_attn_x, L0_attn_Wq, L0_attn_Wk, L0_attn_Wv,
           L0_ffn_router_w, L0_ffn_router_b, L0_ffn_sw1, L0_ffn_sb1,
           L0_ffn_sw2, L0_ffn_sb2, L0_ffn_rw1, L0_ffn_rb1,
           L0_ffn_rw2, L0_ffn_rb2):
    if "nc" not in _PROG_CACHE:
        _PROG_CACHE["nc"] = build_program()
    nc = _PROG_CACHE["nc"]

    fre, fim, ifre, ifim = _consts()
    ident = np.eye(128, dtype=np.float32)
    onesf = np.ones((1, 128), dtype=np.float32)
    onesb = np.ones((1, 128), dtype=ml_dtypes.bfloat16)

    x = np.ascontiguousarray(np.asarray(L0_attn_x, dtype=np.float32).reshape(NCORES * T, D))
    common = dict(
        Wq=np.asarray(L0_attn_Wq, np.float32), Wk=np.asarray(L0_attn_Wk, np.float32),
        Wv=np.asarray(L0_attn_Wv, np.float32),
        router_w=np.asarray(L0_ffn_router_w, np.float32),
        router_b=np.asarray(L0_ffn_router_b, np.float32),
        sw1=np.asarray(L0_ffn_sw1, np.float32), sb1=np.asarray(L0_ffn_sb1, np.float32),
        sw2=np.asarray(L0_ffn_sw2, np.float32), sb2=np.asarray(L0_ffn_sb2, np.float32),
        rw1=np.asarray(L0_ffn_rw1, np.float32), rb1=np.asarray(L0_ffn_rb1, np.float32),
        rw2=np.asarray(L0_ffn_rw2, np.float32), rb2=np.asarray(L0_ffn_rb2, np.float32),
        dft_re=fre, dft_im=fim, idft_re=ifre, idft_im=ifim,
        ident=ident, onesf=onesf, onesb=onesb,
    )
    in_maps = []
    for c in range(NCORES):
        mask = ((np.arange(NCORES) < c) & (np.arange(NCORES) // 4 == c // 4)).astype(np.float32)
        m = dict(common)
        m["x"] = x[c * T:(c + 1) * T]
        m["maskv"] = mask
        in_maps.append(m)
    res = run_bass_kernel_spmd(nc, in_maps, core_ids=list(range(NCORES)))
    out = np.concatenate([np.asarray(res.results[c]["y"], np.float32) for c in range(NCORES)], axis=0)
    return out.reshape(2, 8192, D)
